# version 04
# baseline (speedup 1.0000x reference)
import contextlib
import numpy as np
from concourse.bass_utils import run_bass_kernel_spmd

import concourse.bass as bass
import concourse.mybir as mybir
from concourse.alu_op_type import AluOpType

F32 = mybir.dt.float32
BF16 = mybir.dt.bfloat16
I32 = mybir.dt.int32
AF = mybir.ActivationFunctionType
AX = mybir.AxisListType
OP = AluOpType

T = 2048
NT = 16
D = 1024
NCH = 8
S = 4096
H = 8
HD = 64
EPS = 1e-5

ENGS = ("te", "act", "dve", "pool", "sync")


class Sem:
    __slots__ = ("h", "n", "step", "name")

    def __init__(self, h, step, name):
        self.h, self.n, self.step, self.name = h, 0, step, name


class Rec:

    def __init__(self, nc, stack):
        self.nc = nc
        self.stack = stack
        self.streams = {k: [] for k in ENGS}
        self.sems = {}
        self.own = {}

    def sem(self, name, step=1):
        h = self.stack.enter_context(self.nc.semaphore(name))
        s = Sem(h, step, name)
        self.sems[name] = s
        return s

    def dsem(self, name):
        if name not in self.sems:
            self.sem(name, 16)
        return self.sems[name]

    def emit(self, eng, fn, inc=None):
        if inc is not None:
            inc.n += inc.step
            tgt = inc

            def wrapped(e, fn=fn, tgt=tgt):
                ins = fn(e)
                ins.then_inc(tgt.h, tgt.step)
                return ins

            self.streams[eng].append(wrapped)
            return inc.n
        self.streams[eng].append(fn)
        return None

    def wait(self, eng, sem, val=None):
        v = sem.n if val is None else val
        if v <= 0:
            return
        self.streams[eng].append(lambda e, s=sem, v=v: e.wait_ge(s.h, v))

    def barrier(self, engines=ENGS):
        for eng in engines:
            for own in self.own.get(eng, ()):
                self.wait(eng, own)
            s = self.sems["bar_" + eng]
            self.emit(eng, lambda e: e.nop(), inc=s)
        for eng in engines:
            for other in engines:
                if other != eng:
                    self.wait(eng, self.sems["bar_" + other])



def host_in_maps(inp):
    x = np.asarray(inp["L0_attn_inputs"], np.float32)

    wq = np.asarray(inp["L0_attn_W_q"], np.float32).transpose(1, 0, 2).reshape(D, 1024)
    wk = np.asarray(inp["L0_attn_W_k"], np.float32).transpose(1, 0, 2).reshape(D, 1024)
    wv0 = np.ascontiguousarray(
        np.asarray(inp["L0_attn_W_v"], np.float32).transpose(1, 0, 2).reshape(D, 1024))
    w_qk0 = np.ascontiguousarray(np.concatenate([wq, wk], axis=1))
    b_qk0 = np.ascontiguousarray(
        np.concatenate([np.asarray(inp["L0_attn_b_q"], np.float32).reshape(-1),
                        np.asarray(inp["L0_attn_b_k"], np.float32).reshape(-1)])
        .reshape(16, 128).T)
    bv0 = np.asarray(inp["L0_attn_b_v"], np.float32).reshape(1, 1024)
    woT = np.asarray(inp["L0_attn_W_o"], np.float32).T.reshape(16, 64, 1024)
    bo0 = np.asarray(inp["L0_attn_b_o"], np.float32).reshape(1, 1024)

    router_T = np.ascontiguousarray(np.asarray(inp["L0_ffn_router_w"], np.float32).T)
    cbias = np.asarray(inp["L0_ffn_correction_bias"], np.float32).reshape(1, 16)
    gate_w = np.asarray(inp["L0_ffn_gate_w"], np.float32)
    up_w = np.asarray(inp["L0_ffn_up_w"], np.float32)
    down_w = np.asarray(inp["L0_ffn_down_w"], np.float32)
    sgwT = np.ascontiguousarray(np.asarray(inp["L0_ffn_shared_gate_w"], np.float32).T)
    suwT = np.ascontiguousarray(np.asarray(inp["L0_ffn_shared_up_w"], np.float32).T)
    sdwT = np.ascontiguousarray(np.asarray(inp["L0_ffn_shared_down_w"], np.float32).T)

    wqkv1 = np.asarray(inp["L1_attn_w_qkv"], np.float32)
    w_qk1 = np.ascontiguousarray(wqkv1[:, 0:2048])
    wv1 = np.ascontiguousarray(wqkv1[:, 2048:3072])
    wpT = np.asarray(inp["L1_attn_w_proj"], np.float32).reshape(16, 64, 1024)

    w1T = np.ascontiguousarray(np.asarray(inp["L1_ffn_W1"], np.float32).T)
    b1c = np.ascontiguousarray(np.asarray(inp["L1_ffn_b1"], np.float32).reshape(32, 128).T)
    w2T = np.ascontiguousarray(np.asarray(inp["L1_ffn_W2"], np.float32).T)
    b2r = np.asarray(inp["L1_ffn_b2"], np.float32).reshape(1, 1024)

    ones_f = np.ones((1, 128), np.float32)
    import ml_dtypes
    ones_b = np.ones((1, 128), np.float32)
    triu = np.triu(np.ones((128, 128), np.float32))
    eye = np.eye(64, dtype=np.float32)
    i7113 = np.ascontiguousarray(np.concatenate([7 * eye, 15 * eye, 13 * eye], axis=1))

    wo_g = np.ascontiguousarray(woT.reshape(1024, 1024))
    wp_g = np.ascontiguousarray(wpT.reshape(1024, 1024))
    maps = []
    for c in range(8):
        p, par = c // 2, c % 2
        wo_pc, wp_pc = wo_g, wp_g
        cid = np.zeros((1, 64), np.int32)
        cid[0, 0] = par
        cid[0, 1] = 1 - par
        for i in range(4):
            cid[0, 2 + i] = par * 4 + i
        for i in range(8):
            cid[0, 6 + i] = par * 8 + i
        for j in range(16):
            gh = (par * 8 + j) if j < 8 else ((1 - par) * 8 + (j - 8))
            cid[0, 14 + j] = gh * 2 + par
        for h in range(8):
            for s2 in range(2):
                cid[0, 32 + 2 * h + s2] = (par * 8 + h) * 2 + s2
        for i in range(4):
            cid[0, 48 + i] = 8 + par * 4 + i
        maps.append({
            "x": np.ascontiguousarray(x[p, par * T:(par + 1) * T, :]),
            "cid": cid,
            "w_qk0": w_qk0, "b_qk0": b_qk0, "wv0": wv0, "bv0": bv0,
            "wo": wo_pc, "bo0": bo0,
            "router_T": router_T, "cbias": cbias,
            "gate_w": gate_w, "up_w": up_w, "down_w": down_w,
            "sgwT": sgwT, "suwT": suwT, "sdwT": sdwT,
            "w_qk1": w_qk1, "wv1": wv1, "wp": wp_pc,
            "w1T": w1T, "b1c": b1c, "w2T": w2T, "b2r": b2r,
            "ones_f": ones_f, "ones_b": ones_b, "triu": triu, "i7113": i7113,
        })
    return maps


def host_gather(results):
    out = np.zeros((4, S, D), np.float32)
    for c in range(8):
        p, par = c // 2, c % 2
        out[p, par * T:(par + 1) * T, :] = results[c]["y"]
    return out



class MK:
    def __init__(self, parts=("moe", "att0", "moe", "ffn1", "att1", "ffn1")):
        self.parts = parts
        self.nc = bass.Bass("TRN2", target_bir_lowering=False)
        self.stack = contextlib.ExitStack()
        self.R = Rec(self.nc, self.stack)
        self.regs = {}
        self._declare()

    def _declare(self):
        nc = self.nc
        st = self.stack
        ap = lambda t: t.ap()

        self.x_in = ap(nc.dram_tensor("x", [T, D], F32, kind="ExternalInput"))
        self.cid = ap(nc.dram_tensor("cid", [1, 64], I32, kind="ExternalInput"))
        self.y = ap(nc.dram_tensor("y", [T, D], F32, kind="ExternalOutput"))
        for name, shape in [
            ("w_qk0", [D, 2048]), ("b_qk0", [128, 16]), ("wv0", [D, 1024]),
            ("bv0", [1, 1024]), ("wo", [D, D]), ("bo0", [1, D]),
            ("router_T", [D, 16]), ("cbias", [1, 16]),
            ("gate_w", [16, D, 256]), ("up_w", [16, D, 256]), ("down_w", [16, 256, D]),
            ("sgwT", [D, 2048]), ("suwT", [D, 2048]), ("sdwT", [2048, D]),
            ("w_qk1", [D, 2048]), ("wv1", [D, 1024]), ("wp", [D, D]),
            ("w1T", [D, 4096]), ("b1c", [128, 32]), ("w2T", [4096, D]), ("b2r", [1, D]),
            ("ones_f", [1, 128]), ("ones_b", [1, 128]), ("triu", [128, 128]),
            ("i7113", [64, 192]),
        ]:
            setattr(self, name, ap(nc.dram_tensor(name, shape, F32, kind="ExternalInput")))

        self.sh_qk = ap(nc.dram_tensor("sh_qk", [2, 2, 4, 2, 128, T], BF16, addr_space="Shared"))
        self.sh_v = ap(nc.dram_tensor("sh_v", [2, 2, 8, NT, 128, 64], BF16, addr_space="Shared"))
        self.sh_ao = ap(nc.dram_tensor("sh_ao", [2, 8, 2, 64, T], BF16, addr_space="Shared"))
        self.w1t_c = ap(nc.dram_tensor("w1t_c", [D, 4096], BF16))
        self.w2t_c = ap(nc.dram_tensor("w2t_c", [4096, D], BF16))
        self.gate_c = ap(nc.dram_tensor("gate_c", [16, NCH, 128, 256], BF16))
        self.up_c = ap(nc.dram_tensor("up_c", [16, NCH, 128, 256], BF16))
        self.down_c = ap(nc.dram_tensor("down_c", [16, 2, 128, D], BF16))
        self.sgw_c = ap(nc.dram_tensor("sgw_c", [NCH, 128, 2048], BF16))
        self.suw_c = ap(nc.dram_tensor("suw_c", [NCH, 128, 2048], BF16))
        self.sdw_c = ap(nc.dram_tensor("sdw_c", [16, 128, D], BF16))
        self.cin = ap(nc.dram_tensor("cin", [1, 16], F32))
        self.cout = ap(nc.dram_tensor("cout", [1, 16], F32))
        self.qk_loc = ap(nc.dram_tensor("qk_loc", [2, 4, 2, 128, T], BF16))
        self.v_loc = ap(nc.dram_tensor("v_loc", [2, 8, NT, 128, 64], BF16))
        self.ao_loc = ap(nc.dram_tensor("ao_loc", [8, 2, 64, T], BF16))
        self.qk_rd = ap(nc.dram_tensor("qk_rd", [2, 4, 2, 128, T], BF16))
        self.v_rd = ap(nc.dram_tensor("v_rd", [2, 8, NT, 128, 64], BF16))

        sb = lambda n, s, d: st.enter_context(nc.sbuf_tensor(n, s, d))
        self.x_res = sb("x_res", [128, NT, D], F32)
        self.xT = sb("xT", [128, NCH, T], BF16)
        self.xln = sb("xln", [128, D], BF16)
        self.hbuf = sb("hbuf", [128, 32, 512], BF16)
        flat = self.hbuf[:, :, :].rearrange("p a b -> p (a b)")
        self.qT = flat[0:64, 0:4096].rearrange("p (a b) -> p a b", b=128)
        self.kT = flat[0:64, 4096:8192].rearrange("p (a b) -> p a b", b=128)
        self.vhead = flat[:, 8192:10272].rearrange("p (a b) -> p a b", b=65)
        self.ptil = flat[:, 10272:12320].rearrange("p (a b) -> p a b", b=1024)
        self.e3t = flat[:, 10272:12320].rearrange("p (a b) -> p a b", b=64)
        self.e1t = flat[0:64, 12320:13344].rearrange("p (a b) -> p a b", b=512)
        self.stg2 = flat[0:64, 13344:15392].rearrange("p (a b) -> p a b", b=1024)
        self.stg = sb("stg", [128, 2, 512], BF16)
        self.wa = sb("wa", [128, 2, 8, 256], BF16)
        self.wb = sb("wb", [128, 2, 8, 256], BF16)
        self.wc = sb("wc", [128, 2, 2, 1024], BF16)
        self.wsh = sb("wsh", [128, 2, 2048], BF16)
        self.wsh2 = sb("wsh2", [128, 2, 2048], BF16)
        self.combine = sb("combine", [128, NT, 32], F32)
        self.cT = sb("cT", [32, T], F32)
        self.csc = sb("csc", [1, 64], I32)
        self.onesf = sb("onesf", [1, 128], F32)
        self.onesb = sb("onesb", [1, 128], BF16)
        self.trium = sb("trium", [128, 128], BF16)
        self.bqk = sb("bqk", [128, 16], F32)
        self.b1t = sb("b1t", [128, 32], F32)
        self.brow = sb("brow", [1, 2, 1024], BF16)
        self.brow2 = sb("brow2", [1, 1024], BF16)
        self.stat = sb("stat", [128, 24], F32)
        self.sml = sb("sml", [128, 32], F32)
        self.smlb = sb("smlb", [128, 8], BF16)
        w2f = self.wsh2[:, :, :].rearrange("p a b -> p (a b)").bitcast(F32)
        self.nrow = w2f[0:1, :].rearrange("p (a b) -> p a b", b=1024)
        self.nrec = w2f[32:33, :].rearrange("p (a b) -> p a b", b=1024)
        waf = self.wa[:, :, :, :].rearrange("p a b c -> p (a b c)").bitcast(F32)
        wbf = self.wb[:, :, :, :].rearrange("p a b c -> p (a b c)")
        self.nyf = waf[0:64, 0:768].rearrange("p (a b) -> p a b", b=64)
        self.qlm = waf[0:64, 768:896].rearrange("p (a b) -> p a b", b=64)
        self.i7cv = waf[0:64, 896:1088].rearrange("p (a b) -> p a b", b=64)
        self.qlmb = wbf[0:64, 0:128].rearrange("p (a b) -> p a b", b=64)
        self.nyb = wbf[0:64, 128:258].rearrange("p (a b) -> p a b", b=65)
        self.rowf = w2f[64:65, 0:1024]
        self.sml2 = sb("sml2", [1, 32], F32)

        pt = lambda n, s: st.enter_context(nc.psum_tensor(n, s, F32))
        self.psA = pt("psA", [128, 1024])
        self.psB = pt("psB", [128, 1024])
        self.psC = pt("psC", [128, 1024])
        self.psD = pt("psD", [128, 512])
        self.psE = pt("psE", [128, 512])

        R = self.R
        self.s_te = R.sem("s_te")
        self.s_act = R.sem("s_act")
        self.s_dve = R.sem("s_dve")
        self.s_pool = R.sem("s_pool")
        self.s_gd = R.sem("s_gd", 16)
        self.s_sd = R.sem("s_sd", 16)
        self.s_cc = R.sem("s_cc")
        for e in ENGS:
            R.sem("bar_" + e)
        R.own = {"te": [self.s_te], "act": [self.s_act], "dve": [self.s_dve],
                 "pool": [self.s_pool, self.s_gd, self.s_cc],
                 "sync": [self.s_sd]}

    def pair_barrier(self):
        R = self.R
        R.emit("pool", lambda e: e.collective_compute(
            "AllReduce", OP.add, replica_groups=[[0, 1], [2, 3], [4, 5], [6, 7]],
            ins=[self.cin], outs=[self.cout]), inc=self.s_cc)
        R.wait("pool", self.s_cc)
        R.barrier()

    def mm(self, out, lhsT, rhs, start, stop, inc=None):
        return self.R.emit("te", lambda e, o=out, a=lhsT, b=rhs, s0=start, s1=stop:
                           e.matmul(o, a, b, start=s0, stop=s1), inc=inc)

    def prologue(self):
        R, nc = self.R, self.nc
        P = mybir.EngineType.Pool

        for t in range(NT):
            R.emit("sync", lambda e, t=t: e.dma_start(
                out=self.x_res[:, t, :], in_=self.x_in[t * 128:(t + 1) * 128, :]),
                inc=self.s_sd)
        R.emit("pool", lambda e: e.dma_start(out=self.csc[:, :], in_=self.cid[:, :]),
               inc=R.dsem("d_cid"))
        R.emit("sync", lambda e: e.dma_start(out=self.onesf[:, :], in_=self.ones_f[:, :]),
               inc=self.s_sd)
        R.emit("sync", lambda e: e.dma_start(out=self.bqk[:, :], in_=self.b_qk0[:, :]),
               inc=self.s_sd)
        R.emit("sync", lambda e: e.dma_start(out=self.b1t[:, :], in_=self.b1c[:, :]),
               inc=self.s_sd)
        R.emit("pool", lambda e: e.dma_start(out=self.onesb[:, :], in_=self.ones_b[:, :]),
               inc=self.s_gd)
        R.emit("pool", lambda e: e.dma_start(out=self.trium[:, :], in_=self.triu[:, :]),
               inc=self.s_gd)
        for j, (srcrow, dstap, sc) in enumerate([
                (self.bv0, self.brow[:, 0, :], 1.0),
                (self.bo0, self.brow[:, 1, :], 1.0),
                (self.b2r, self.brow2[:, :], 0.5)]):
            if j > 0:
                R.wait("sync", self.s_act)
            R.emit("sync", lambda e, s=srcrow: e.dma_start(
                out=self.rowf[:, :], in_=s[:, :]), inc=R.dsem("d_row"))
            R.wait("act", R.dsem("d_row"))
            R.emit("act", lambda e, d=dstap, sc=sc: e.activation(
                self.rowf2[:, :] if False else d, self.rowf[:, :], AF.Copy, scale=sc),
                inc=self.s_act)
        R.emit("pool", lambda e: e.memset(self.sml2[0:1, 16:32], 0.0), inc=self.s_pool)
        R.wait("pool", self.s_pool)
        R.emit("pool", lambda e: e.dma_start(out=self.cin[:, :], in_=self.sml2[0:1, 16:32]),
               inc=self.s_gd)
        R.wait("pool", R.dsem("d_cid"))

        def loadregs(e):
            r = {}
            vl = lambda j, mx: nc.values_load(
                self.csc[0:1, j:j + 1], engines=[P], min_val=0, max_val=mx,
                skip_runtime_bounds_check=True)
            r["myslot"] = vl(0, 1)
            r["ptslot"] = vl(1, 1)

            self.regs.update(r)
        R.streams["pool"].append(loadregs)

        if "moe" in self.parts:
            self._moe_cache()
        if "ffn1" in self.parts:
            self._ternarize(self.w1T, self.w1t_c, 4096, -1.0, cb=0)
            self._ternarize(self.w2T, self.w2t_c, 1024, -0.5, rows=4096, cb=8)
        R.wait("sync", self.s_sd)
        R.wait("pool", self.s_gd)
        R.barrier()

    def _moe_cache(self):
        R = self.R
        sa, sb2 = R.dsem("d_mcA"), R.dsem("d_mcB")
        for srcp, dstp in [
            (self.sgwT.rearrange("(c p) h -> p c h", p=128),
             self.sgw_c.rearrange("c p h -> p c h")),
            (self.suwT.rearrange("(c p) h -> p c h", p=128),
             self.suw_c.rearrange("c p h -> p c h")),
            (self.sdwT.rearrange("(k p) d -> p k d", p=128),
             self.sdw_c.rearrange("k p d -> p k d")),
        ]:
            R.emit("pool", lambda e, s=srcp, d=dstp: e.dma_start(
                out=d, in_=s, max_dma_last_dim=8192), inc=sa)
        for ex in range(16):
            for srcp, dstp in [
                (self.gate_w[ex].rearrange("(c p) h -> p c h", p=128),
                 self.gate_c[ex].rearrange("c p h -> p c h")),
                (self.up_w[ex].rearrange("(c p) h -> p c h", p=128),
                 self.up_c[ex].rearrange("c p h -> p c h")),
                (self.down_w[ex].rearrange("(k p) d -> p k d", p=128),
                 self.down_c[ex].rearrange("k p d -> p k d")),
            ]:
                R.emit("pool", lambda e, s=srcp, d=dstp: e.dma_start(
                    out=d, in_=s, max_dma_last_dim=8192), inc=sb2)
        self._mc_done = True

    def _ternarize(self, srcT, dstT, ncols, final_scale, rows=D, cb=0):
        R = self.R
        view = srcT.rearrange("(k p) d -> k p d", p=128)
        dview = dstT.rearrange("(k p) d -> k p d", p=128)
        wf = self.hbuf[:, :, :].rearrange("p a b -> p (a b)")[:, 12288:16384] \
            .bitcast(F32)
        af = self.wsh2[:, :, :].rearrange("p a b -> p (a b)").bitcast(F32)
        dep = getattr(self, "_tz_prev", None)
        if dep is not None:
            R.wait("sync", self.s_dve, dep)
        R.wait("dve", self.s_act)
        if "d_row" in R.sems:
            R.wait("dve", R.sems["d_row"])
        R.emit("dve", lambda e: e.memset(self.sml[:, cb + 0:cb + 2], 0.0), inc=self.s_dve)
        R.wait("dve", self.s_dve)
        tw = min(1024, ncols)
        kk = rows // 128
        cc = ncols // tw
        t_red = [0, 0]
        for k in range(kk):
            for c in range(cc):
                i = k * cc + c
                b = i % 2
                sin = R.dsem("d_tz%d" % b)
                src_ap = view[k, :, c * tw:(c + 1) * tw]
                dst_sb = wf[:, b * 1024:b * 1024 + tw]
                R.wait("sync", self.s_dve, t_red[b])
                R.emit("sync", lambda e, o=dst_sb, s=src_ap: e.dma_start(
                    out=o, in_=s), inc=sin)
                R.wait("dve", sin)
                R.wait("dve", self.s_dve)
                R.emit("dve", lambda e, s=dst_sb: e.tensor_reduce(
                    self.sml[:, cb + 1:cb + 2], s, axis=AX.X, op=OP.add,
                    apply_absolute_value=True), inc=self.s_dve)
                R.wait("dve", self.s_dve)
                t_red[b] = R.emit("dve", lambda e: e.tensor_add(
                    self.sml[:, cb + 0:cb + 1], self.sml[:, cb + 0:cb + 1], self.sml[:, cb + 1:cb + 2]),
                    inc=self.s_dve)
        R.wait("pool", self.s_dve, max(t_red))
        R.wait("pool", self.s_pool)
        R.emit("pool", lambda e: e.tensor_reduce(
            self.sml2[0:1, cb + 0:cb + 1], self.sml[:, cb + 0:cb + 1], axis=AX.C, op=OP.add),
            inc=self.s_pool)
        R.wait("dve", self.s_pool)
        R.emit("dve", lambda e: e.tensor_scalar(
            self.sml2[0:1, cb + 1:cb + 2], self.sml2[0:1, cb + 0:cb + 1], 1.0 / (rows * ncols), None,
            op0=OP.mult), inc=self.s_dve)
        R.wait("dve", self.s_dve)
        R.emit("dve", lambda e: e.tensor_scalar(
            self.sml2[0:1, cb + 2:cb + 3], self.sml2[0:1, cb + 1:cb + 2], 1e-8, None, op0=OP.max),
            inc=self.s_dve)
        R.wait("dve", self.s_dve)
        R.emit("dve", lambda e: e.tensor_scalar(
            self.sml2[0:1, cb + 3:cb + 4], self.sml2[0:1, cb + 2:cb + 3], 0.5, None, op0=OP.mult),
            inc=self.s_dve)
        R.wait("dve", self.s_dve)
        t_thr = R.emit("dve", lambda e: e.tensor_scalar(
            self.sml2[0:1, cb + 4:cb + 5], self.sml2[0:1, cb + 3:cb + 4], -1.0, None, op0=OP.mult),
            inc=self.s_dve)
        R.wait("te", self.s_dve, t_thr)
        R.wait("te", self.s_sd)
        self.mm(self.psD[:, cb:cb + 1], self.onesf[:, :], self.sml2[0:1, cb + 3:cb + 4], True, True)
        t_b = self.mm(self.psD[:, cb + 1:cb + 2], self.onesf[:, :], self.sml2[0:1, cb + 4:cb + 5],
                      True, True, inc=self.s_te)
        R.wait("dve", self.s_te, t_b)
        t_thp = R.emit("dve", lambda e: e.tensor_copy(self.sml[:, cb + 2:cb + 4], self.psD[:, cb:cb + 2]),
                       inc=self.s_dve)
        t_cmp = [t_thp, t_thp]
        for k in range(kk):
            for c in range(cc):
                i = k * cc + c
                b = i % 2
                sin = R.dsem("d_tz%d" % b)
                sout = R.dsem("d_tzo%d" % b)
                src_ap = view[k, :, c * tw:(c + 1) * tw]
                w_sb = wf[:, b * 1024:b * 1024 + tw]
                a_sb = af[:, b * 1024:b * 1024 + tw]
                R.wait("sync", self.s_dve, t_cmp[b])
                R.emit("sync", lambda e, o=w_sb, s=src_ap: e.dma_start(
                    out=o, in_=s), inc=sin)
                R.wait("dve", sin)
                R.wait("dve", self.s_dve, t_thp)
                R.emit("dve", lambda e, w=w_sb, a=a_sb: e.tensor_scalar(
                    a, w, self.sml[:, cb + 2:cb + 3], None, op0=OP.is_ge), inc=self.s_dve)
                R.wait("dve", self.s_dve)
                R.emit("dve", lambda e, w=w_sb, a=a_sb: e.scalar_tensor_tensor(
                    w, w, self.sml[:, cb + 3:cb + 4], a, op0=OP.is_le, op1=OP.subtract),
                    inc=self.s_dve)
                R.wait("dve", self.s_dve)
                for q in range(tw // 512):
                    R.wait("dve", sout)
                    t_cmp[b] = R.emit("dve", lambda e, w=w_sb, q=q, b=b: e.tensor_scalar(
                        self.stg[:, b, :], w[:, q * 512:(q + 1) * 512],
                        final_scale, None, op0=OP.mult), inc=self.s_dve)
                    R.wait("sync", self.s_dve, t_cmp[b])
                    t_last = R.emit("sync", lambda e, d=dview[k, :, :], c=c, q=q, b=b:
                                    e.dma_start(out=d[:, c * tw + q * 512:
                                                      c * tw + (q + 1) * 512],
                                                in_=self.stg[:, b, :]), inc=sout)


        self._tz_prev = max(t_cmp)
        self._tz_dve_tok = t_thp
    def emit_ln(self):
        R = self.R
        t_act = {}
        t_chain = {}
        for t in range(NT):
            s = t % 2
            b = s * 12
            c0 = 12 * s
            if t >= 2:
                R.wait("dve", self.s_act, t_act[t - 2])
            R.emit("dve", lambda e, t=t, b=b: e.bn_stats(
                self.stat[:, b:b + 6], self.x_res[:, t, 0:512]), inc=self.s_dve)
            R.emit("dve", lambda e, t=t, b=b: e.bn_stats(
                self.stat[:, b + 6:b + 12], self.x_res[:, t, 512:1024]), inc=self.s_dve)
            R.wait("dve", self.s_dve)
            R.emit("dve", lambda e, b=b, c0=c0: e.bn_aggr(
                self.sml[:, c0:c0 + 2], self.stat[:, b:b + 12]), inc=self.s_dve)
            R.wait("dve", self.s_dve)
            R.emit("dve", lambda e, c0=c0: e.tensor_scalar(
                self.sml[:, c0 + 2:c0 + 3], self.sml[:, c0 + 1:c0 + 2], EPS, None,
                op0=OP.add), inc=self.s_dve)
            tok = self.s_dve.n
            R.wait("act", self.s_dve, tok)
            R.emit("act", lambda e, c0=c0: e.activation(
                self.sml[:, c0 + 3:c0 + 4], self.sml[:, c0 + 2:c0 + 3], AF.Sqrt),
                inc=self.s_act)
            R.wait("dve", self.s_act)
            R.emit("dve", lambda e, c0=c0: e.reciprocal(
                self.sml[:, c0 + 4:c0 + 5], self.sml[:, c0 + 3:c0 + 4]), inc=self.s_dve)
            R.wait("dve", self.s_dve)
            t_chain[t] = R.emit("dve", lambda e, c0=c0: e.scalar_tensor_tensor(
                self.sml[:, c0 + 5:c0 + 6], self.sml[:, c0:c0 + 1], -1.0,
                self.sml[:, c0 + 4:c0 + 5], op0=OP.mult, op1=OP.mult), inc=self.s_dve)
            R.wait("act", self.s_dve, t_chain[t])
            if t >= 1:
                R.wait("act", R.dsem("d_tr%d" % ((t - 1) % 2)))
            t_act[t] = R.emit("act", lambda e, t=t, c0=c0: e.activation(
                self.xln[:, :], self.x_res[:, t, :], AF.Identity,
                bias=self.sml[:, c0 + 5:c0 + 6], scale=self.sml[:, c0 + 4:c0 + 5]),
                inc=self.s_act)
            R.wait("sync", self.s_act, t_act[t])
            for c in range(NCH):
                R.emit("sync", lambda e, t=t, c=c: e.dma_start(
                    out=self.xT[:, c, t * 128:(t + 1) * 128],
                    in_=self.xln[:, c * 128:(c + 1) * 128], transpose=True),
                    inc=R.dsem("d_tr%d" % (t % 2)))

    def ln_drain(self):
        R = self.R
        R.wait("sync", R.dsem("d_tr0"))
        R.wait("sync", R.dsem("d_tr1"))

    def emit_ffn1(self):
        R = self.R
        for eng in ("sync", "pool"):
            for nm in ("d_tzo0", "d_tzo1"):
                if nm in R.sems:
                    R.wait(eng, R.sems[nm])
        banks8 = [self.psA[:, 0:512], self.psA[:, 512:1024],
                  self.psB[:, 0:512], self.psB[:, 512:1024],
                  self.psC[:, 0:512], self.psC[:, 512:1024],
                  self.psD[:, :], self.psE[:, :]]
        hps = [self.psD[:, :], self.psE[:, :]]
        w1v = self.w1t_c.rearrange("(c p) h -> p c h", p=128)
        t_down_last = 0
        t_evac = [0] * 8
        t_w2mm = [0] * 4
        t_w1mm = [0] * 4
        wcq = self.wc[:, :, :, :].rearrange("p a b c -> p (a b c)")
        for st in range(4):
            t_hevac = [0] * 32
            for m in range(32):
                b = m % 4
                lw = wcq[:, b * 1024:(b + 1) * 1024].rearrange("p (c k) -> p c k", k=128)
                R.wait("sync", self.s_te, t_w1mm[b] if (m >= 4 or st > 0) else 0)
                R.emit("sync", lambda e, m=m, o=lw: e.dma_start(
                    out=o, in_=w1v[:, :, m * 128:(m + 1) * 128]),
                    inc=R.dsem("d_w1%d" % b))
                R.wait("te", R.dsem("d_w1%d" % b))
                if m >= 2:
                    R.wait("te", self.s_act, t_hevac[m - 2])
                elif st > 0:
                    R.wait("te", self.s_dve, t_evac[6 + m % 2])
                for c in range(NCH):
                    tok = self.mm(hps[m % 2], lw[:, c, :],
                                  self.xT[:, c, st * 512:(st + 1) * 512],
                                  c == 0, c == NCH - 1,
                                  inc=self.s_te if c == NCH - 1 else None)
                t_w1mm[b] = tok
                R.wait("act", self.s_te, tok)
                if st > 0 and m == 0:
                    R.wait("act", self.s_te, t_down_last)
                t_hevac[m] = R.emit("act", lambda e, m=m, b=b: e.activation(
                    self.hbuf[:, m, :], hps[m % 2], AF.Gelu,
                    bias=self.b1t[:, m:m + 1]), inc=self.s_act)
            for i in range(8):
                R.wait("te", self.s_dve, t_evac[i])
            R.wait("te", self.s_act, t_hevac[31])
            for tt in range(4):
                for n in range(2):
                    self.mm(banks8[tt * 2 + n], self.onesb[:, :],
                            self.brow2[:, n * 512:(n + 1) * 512], True, False)
            for k in range(32):
                b = k % 4
                wbuf = self.wsh if b < 2 else self.wsh2
                R.wait("pool", self.s_te, t_w2mm[b])
                R.emit("pool", lambda e, k=k, b=b, wbuf=wbuf: e.dma_start(
                    out=wbuf[:, b % 2, 0:1024],
                    in_=self.w2t_c[k * 128:(k + 1) * 128, :]),
                    inc=R.dsem("d_w2%d" % b))
                R.wait("te", R.dsem("d_w2%d" % b))
                R.wait("te", self.s_act, t_hevac[k])
                for tt in range(4):
                    for n in range(2):
                        tok = self.mm(banks8[tt * 2 + n],
                                      self.hbuf[:, k, tt * 128:(tt + 1) * 128],
                                      wbuf[:, b % 2, n * 512:(n + 1) * 512],
                                      False, k == 31,
                                      inc=self.s_te if (tt == 3 and n == 1) else None)
                t_w2mm[b] = tok
            t_down_last = tok
            R.wait("dve", self.s_te, t_down_last)
            for tt in range(4):
                for n in range(2):
                    t_evac[tt * 2 + n] = R.emit(
                        "dve", lambda e, st=st, tt=tt, n=n: e.tensor_add(
                            self.x_res[:, st * 4 + tt, n * 512:(n + 1) * 512],
                            self.x_res[:, st * 4 + tt, n * 512:(n + 1) * 512],
                            banks8[tt * 2 + n]), inc=self.s_dve)

    def epilogue(self):
        R = self.R
        R.barrier()
        for t in range(NT):
            R.emit("sync", lambda e, t=t: e.dma_start(
                out=self.y[t * 128:(t + 1) * 128, :], in_=self.x_res[:, t, :]),
                inc=self.s_sd)
        R.wait("sync", self.s_sd)

    def build(self):
        R = self.R
        with self.stack:
            self.prologue()
            for part in self.parts:
                R.barrier()
                self.emit_ln()
                self.ln_drain()
                R.barrier()
                if part == "ffn1":
                    self.emit_ffn1()
                elif part == "moe":
                    self.emit_moe()
                elif part == "att0":
                    self.emit_att0()
                elif part == "att1":
                    self.emit_att1()
            self.epilogue()
            with self.nc.Block() as block:
                streams = self.R.streams

                @block.tensor
                def _(e):
                    for f in streams["te"]:
                        f(e)

                @block.scalar
                def _(e):
                    for f in streams["act"]:
                        f(e)

                @block.vector
                def _(e):
                    for f in streams["dve"]:
                        f(e)

                @block.gpsimd
                def _(e):
                    for f in streams["pool"]:
                        f(e)

                @block.sync
                def _(e):
                    for f in streams["sync"]:
                        f(e)
        return self.nc

    def _maxnorm2(self, src_flat, outcol, width=4096):
        R = self.R
        DVE = lambda fn: (R.wait("dve", self.s_dve), R.emit("dve", fn, inc=self.s_dve))[1]
        sqv = self.ptil[:, :, :].rearrange("p a b -> p (a b)")[0:64, :]
        acc = self.sml2[0:1, outcol:outcol + 1]
        DVE(lambda e: e.memset(acc, 0.0))
        nch = (width + 2047) // 2048
        for ch in range(nch):
            w = min(2048, width - ch * 2048)
            R.wait("act", self.s_dve)
            R.wait("act", self.s_te)
            t_sq = R.emit("act", lambda e, ch=ch, w=w: e.activation(
                sqv[:, 0:w], src_flat[:, ch * 2048:ch * 2048 + w], AF.Square),
                inc=self.s_act)
            for j in range((w + 511) // 512):
                ww = min(512, w - j * 512)
                R.wait("te", self.s_act, t_sq)
                R.wait("te", self.s_dve)
                t_mm = self.mm(self.psE[0:1, 0:ww], self.smlb[0:64, 0:1],
                               sqv[:, j * 512:j * 512 + ww], True, True,
                               inc=self.s_te)
                R.wait("dve", self.s_te, t_mm)
                DVE(lambda e, ww=ww: e.reduce_max(self.sml2[0:1, 30:31],
                                                  self.psE[0:1, 0:ww], axis=AX.X))
                R.wait("dve", self.s_dve)
                DVE(lambda e: e.tensor_tensor(acc, acc, self.sml2[0:1, 30:31],
                                              op=OP.max))
        return self.R.sems["s_dve"].n

    def _nystrom_heads(self):
        R = self.R
        DVE = lambda fn: (R.wait("dve", self.s_dve), R.emit("dve", fn, inc=self.s_dve))[1]

        def TE(out, lhsT, rhs, start=True, stop=True):
            R.wait("te", self.s_dve)
            R.wait("te", self.s_act)
            return self.mm(out, lhsT, rhs, start, stop, inc=self.s_te)

        def TR64(dst, srcm):
            t = None
            for i2 in range(2):
                for j2 in range(2):
                    R.wait("dve", self.s_dve)
                    t = R.emit("dve", lambda e, i2=i2, j2=j2, dst=dst, srcm=srcm:
                               e.transpose(dst[32 * j2:32 * (j2 + 1), 32 * i2:32 * (i2 + 1)],
                                           srcm[32 * i2:32 * (i2 + 1), 32 * j2:32 * (j2 + 1)]),
                               inc=self.s_dve)
            return t

        w2f = self.wsh2[:, :, :].rearrange("p a b -> p (a b)").bitcast(F32)
        nr64 = w2f[64:65, 0:1024]
        rc64 = w2f[64:65, 1024:2048]
        mscr = self.xln[:, :].bitcast(F32)
        ones64 = mscr[64:65, 0:64]
        oc64 = mscr[0:64, 65:66]
        qf = self.qT[:, :, :].rearrange("p a b -> p (a b)")
        kf = self.kT[:, :, :].rearrange("p a b -> p (a b)")
        ny = self.nyf
        R.wait("dve", self.s_te)
        R.wait("act", self.s_te)
        R.wait("sync", self.s_te)
        R.emit("sync", lambda e: e.dma_start(
            out=self.i7cv[:, :, :],
            in_=self.i7113[:, :].rearrange("p (k m) -> p k m", m=64)),
            inc=R.dsem("d_i7"))
        DVE(lambda e: e.memset(ones64, 1.0))
        DVE(lambda e: e.memset(oc64, 1.0))
        DVE(lambda e: e.memset(mscr[64:65, 64:65], 1e-30))
        DVE(lambda e: e.memset(self.nyb[:, 0, 64:65], 1.0))
        DVE(lambda e: e.memset(self.vhead[:, :, 64:65], 1.0))
        DVE(lambda e: e.memset(self.smlb[0:64, 0:1], 1.0))
        R.wait("sync", R.dsem("d_pull"))
        t_hmm = 0
        t_exp_s = [0, 0]
        t_oc = 0
        t_nrw = 0
        t_no = {}
        t_bc_cons = 0
        for h in range(8):
            ph = 64 * (h % 2)
            sq = R.dsem("d_qh")
            R.wait("sync", self.s_te, t_hmm)
            R.wait("sync", self.s_dve)
            for slot in range(2):
                R.emit("sync", lambda e, slot=slot, h=h, ph=ph: e.dma_start(
                    out=self.qT[:, slot * 16:(slot + 1) * 16, :],
                    in_=self.qk_rd[slot, h // 2, 0, ph:ph + 64, :]
                    .rearrange("p (a b) -> p a b", b=128)), inc=sq)
                R.emit("sync", lambda e, slot=slot, h=h, ph=ph: e.dma_start(
                    out=self.kT[:, slot * 16:(slot + 1) * 16, :],
                    in_=self.qk_rd[slot, h // 2, 1, ph:ph + 64, :]
                    .rearrange("p (a b) -> p a b", b=128)), inc=sq)
                R.emit("sync", lambda e, slot=slot, h=h: e.dma_start(
                    out=self.vhead[:, slot * 16:(slot + 1) * 16, 0:64],
                    in_=self.v_rd[slot, h].rearrange("t p d -> p t d")), inc=sq)
            R.wait("dve", sq)
            DVE(lambda e: e.tensor_reduce(
                self.qlm[:, 0, :], qf.rearrange("p (l w) -> p l w", w=64),
                axis=AX.X, op=OP.add))
            DVE(lambda e: e.tensor_scalar(self.qlm[:, 0, :], self.qlm[:, 0, :],
                                          1.0 / 64, None, op0=OP.mult))
            DVE(lambda e: e.tensor_copy(self.qlmb[:, 0, :], self.qlm[:, 0, :]))
            DVE(lambda e: e.tensor_reduce(
                self.qlm[:, 1, :], kf.rearrange("p (l w) -> p l w", w=64),
                axis=AX.X, op=OP.add))
            DVE(lambda e: e.tensor_scalar(self.qlm[:, 1, :], self.qlm[:, 1, :],
                                          1.0 / 64, None, op0=OP.mult))
            DVE(lambda e: e.tensor_copy(self.qlmb[:, 1, :], self.qlm[:, 1, :]))
            self._maxnorm2(qf, 20)
            self._maxnorm2(kf, 21)
            self._maxnorm2(self.qlmb[:, 1, :], 22, width=64)
            self._maxnorm2(self.qlmb[:, 0, :], 23, width=64)
            DVE(lambda e: e.tensor_tensor(self.sml2[0:1, 24:25], self.sml2[0:1, 20:21],
                                          self.sml2[0:1, 22:23], op=OP.mult))
            DVE(lambda e: e.tensor_tensor(self.sml2[0:1, 25:26], self.sml2[0:1, 23:24],
                                          self.sml2[0:1, 21:22], op=OP.mult))
            R.wait("act", self.s_dve)
            R.emit("act", lambda e: e.activation(
                self.sml2[0:1, 24:25], self.sml2[0:1, 24:25], AF.Sqrt), inc=self.s_act)
            t_sqr = R.emit("act", lambda e: e.activation(
                self.sml2[0:1, 25:26], self.sml2[0:1, 25:26], AF.Sqrt), inc=self.s_act)
            R.wait("dve", self.s_act, t_sqr)
            t_n1 = DVE(lambda e: e.tensor_scalar(self.sml2[0:1, 24:25],
                                                 self.sml2[0:1, 24:25], -1.0, None,
                                                 op0=OP.mult))
            t_n3 = DVE(lambda e: e.tensor_scalar(self.sml2[0:1, 25:26],
                                                 self.sml2[0:1, 25:26], -1.0, None,
                                                 op0=OP.mult))
            R.wait("te", self.s_dve, t_n3)
            R.wait("te", self.s_act)
            self.mm(self.psE[:, 0:1], self.onesf[:, :], self.sml2[0:1, 24:25],
                    True, True)
            t_nbm = self.mm(self.psE[:, 1:2], self.onesf[:, :], self.sml2[0:1, 25:26],
                            True, True, inc=self.s_te)
            R.wait("dve", self.s_te, t_nbm)
            b1col = self.sml[0:64, 28 + h % 2:29 + h % 2]
            b3col = self.sml[0:64, 30 + h % 2:31 + h % 2]
            DVE(lambda e, b1col=b1col: e.tensor_copy(b1col, self.psE[0:64, 0:1]))
            t_nbc = DVE(lambda e, b3col=b3col: e.tensor_copy(b3col, self.psE[0:64, 1:2]))
            R.wait("te", R.dsem("d_i7"))
            TE(self.psD[0:64, 0:64], self.qlm[:, 0, :], self.qlm[:, 1, :])
            R.wait("dve", self.s_te)
            DVE(lambda e: e.tensor_reduce(ny[:, 8, 0:1], self.psD[0:64, 0:64],
                                          axis=AX.X, op=OP.max, negate=True))
            R.wait("act", self.s_dve)
            R.wait("act", self.s_te)
            R.emit("act", lambda e: e.activation(ny[:, 6, :], self.psD[0:64, 0:64],
                                                 AF.Exp, bias=ny[:, 8, 0:1]),
                   inc=self.s_act)
            R.wait("dve", self.s_act)
            DVE(lambda e: e.tensor_reduce(ny[:, 8, 1:2], ny[:, 6, :], axis=AX.X,
                                          op=OP.add))
            R.wait("dve", self.s_dve)
            DVE(lambda e: e.reciprocal(ny[:, 8, 2:3], ny[:, 8, 1:2]))
            R.wait("dve", self.s_dve)
            DVE(lambda e: e.tensor_scalar(ny[:, 0, :], ny[:, 6, :], ny[:, 8, 2:3],
                                          None, op0=OP.mult))
            TR64(ny[:, 1, :], ny[:, 0, :])
            TE(self.psD[0:1, 64:128], oc64, ny[:, 0, :])
            R.wait("dve", self.s_te)
            DVE(lambda e: e.tensor_reduce(mscr[0:1, 66:67], self.psD[0:1, 64:128],
                                          axis=AX.X, op=OP.max))
            R.wait("dve", self.s_dve)
            DVE(lambda e: e.reciprocal(mscr[0:1, 67:68], mscr[0:1, 66:67]))
            t_rd = DVE(lambda e: e.nop()) if False else None
            R.wait("te", self.s_dve)
            TE(self.psD[0:64, 64:65], self.onesf[0:1, 0:64], mscr[0:1, 67:68])
            R.wait("dve", self.s_te)
            DVE(lambda e: e.tensor_scalar(ny[:, 2, :], ny[:, 1, :], self.psD[0:64, 64:65]
                                          if False else ny[:, 8, 3:4], None, op0=OP.mult)) \
                if False else None
            DVE(lambda e: e.tensor_copy(ny[:, 8, 3:4], self.psD[0:64, 64:65]))
            R.wait("dve", self.s_dve)
            DVE(lambda e: e.tensor_scalar(ny[:, 2, :], ny[:, 1, :], ny[:, 8, 3:4],
                                          None, op0=OP.mult))
            for it in range(6):
                TE(self.psD[0:64, 0:64], ny[:, 1, :], ny[:, 2, :])
                R.wait("dve", self.s_te)
                DVE(lambda e: e.tensor_copy(ny[:, 4, :], self.psD[0:64, 0:64]))
                TR64(ny[:, 5, :], ny[:, 4, :])
                DVE(lambda e: e.tensor_tensor(ny[:, 6, :], self.i7cv[:, 0, :],
                                              ny[:, 4, :], op=OP.subtract))
                TE(self.psD[0:64, 64:128], ny[:, 5, :], ny[:, 6, :])
                R.wait("dve", self.s_te)
                DVE(lambda e: e.tensor_tensor(ny[:, 6, :], self.i7cv[:, 1, :],
                                              self.psD[0:64, 64:128], op=OP.subtract))
                TE(self.psD[0:64, 0:64], ny[:, 5, :], ny[:, 6, :])
                R.wait("dve", self.s_te)
                DVE(lambda e: e.tensor_tensor(ny[:, 6, :], self.i7cv[:, 2, :],
                                              self.psD[0:64, 0:64], op=OP.subtract))
                TR64(ny[:, 3, :], ny[:, 2, :])
                TE(self.psD[0:64, 64:128], ny[:, 3, :], ny[:, 6, :])
                R.wait("dve", self.s_te)
                DVE(lambda e: e.tensor_scalar(ny[:, 2, :], self.psD[0:64, 64:128],
                                              0.25, None, op0=OP.mult))
            DVE(lambda e: e.memset(ny[:, 9, 0:1], 0.0))
            t_r3 = [0, 0]
            for i in range(8):
                sl = i % 2
                sp = self.psB[0:64, sl * 512:(sl + 1) * 512]
                R.wait("te", self.s_act, t_exp_s[sl])
                t_s3 = self.mm(sp, self.qlmb[:, 0, :],
                               self.kT[:, i * 4:(i + 1) * 4, :].rearrange("p a b -> p (a b)"),
                               True, True, inc=self.s_te)
                t_hmm = t_s3
                R.wait("act", self.s_te, t_s3)
                R.wait("act", self.s_dve, max(t_nbc, t_r3[sl]))
                R.wait("act", R.dsem("d_e3t"))
                t_exp_s[sl] = R.emit("act", lambda e, sp=sp, sl=sl, b3col=b3col:
                                     e.activation(self.e1t[:, sl, :], sp, AF.Exp,
                                                  bias=b3col), inc=self.s_act)
                R.wait("dve", self.s_act, t_exp_s[sl])
                t_r3[sl] = DVE(lambda e, sl=sl: e.tensor_reduce(
                    ny[:, 9, 1:2], self.e1t[:, sl, :], axis=AX.X, op=OP.add))
                R.wait("dve", self.s_dve)
                DVE(lambda e: e.tensor_add(ny[:, 9, 0:1], ny[:, 9, 0:1], ny[:, 9, 1:2]))
                R.wait("sync", self.s_act, t_exp_s[sl])
                R.wait("sync", self.s_te)
                for j in range(4):
                    R.emit("sync", lambda e, i=i, j=j, sl=sl: e.dma_start(
                        out=self.e3t[:, i * 4 + j, :],
                        in_=self.e1t[:, sl, j * 128:(j + 1) * 128], transpose=True),
                        inc=R.dsem("d_e3t"))
            R.wait("dve", self.s_dve)
            DVE(lambda e: e.reciprocal(ny[:, 9, 2:3], ny[:, 9, 0:1]))
            R.wait("te", R.dsem("d_e3t"))
            R.wait("te", self.s_dve)
            for c in range(32):
                t_k3 = self.mm(self.psD[0:64, 0:64], self.e3t[:, c, :],
                               self.vhead[:, c, 0:64], c == 0, c == 31,
                               inc=self.s_te if c == 31 else None)
            t_hmm = t_k3
            R.wait("dve", self.s_te, t_k3)
            DVE(lambda e: e.tensor_scalar(ny[:, 7, :], self.psD[0:64, 0:64],
                                          ny[:, 9, 2:3], None, op0=OP.mult))
            TR64(ny[:, 3, :], ny[:, 2, :])
            TE(self.psD[0:64, 64:128], ny[:, 3, :], ny[:, 7, :])
            R.wait("dve", self.s_te)
            t_zb = DVE(lambda e: e.tensor_copy(self.nyb[:, 0, 0:64],
                                               self.psD[0:64, 64:128]))
            for qi in range(4):
                for half in range(2):
                    i = qi * 2 + half
                    sl = i % 2
                    sp = self.psA[0:64, sl * 512:(sl + 1) * 512]
                    R.wait("te", self.s_act, t_exp_s[sl])
                    t_s1 = self.mm(sp, self.qlmb[:, 1, :],
                                   self.qT[:, i * 4:(i + 1) * 4, :]
                                   .rearrange("p a b -> p (a b)"),
                                   True, True, inc=self.s_te)
                    t_hmm = t_s1
                    R.wait("act", self.s_te, t_s1)
                    R.wait("act", self.s_dve, t_zb)
                    t_exp_s[sl] = R.emit("act", lambda e, sp=sp, sl=sl, b1col=b1col:
                                         e.activation(self.e1t[:, sl, :], sp, AF.Exp,
                                                      bias=b1col), inc=self.s_act)
                    R.wait("te", self.s_act, t_exp_s[sl])
                    if i == 0:
                        R.wait("te", self.s_act, max(t_oc, t_nrw))
                    t_av = self.mm(self.psC[0:65, half * 512:(half + 1) * 512],
                                   self.nyb[:, 0, :], self.e1t[:, sl, :],
                                   True, True, inc=self.s_te)
                    t_hmm = t_av
                R.wait("act", self.s_te, t_av)
                R.wait("act", self.s_dve)
                t_nrw = R.emit("act", lambda e: e.activation(
                    nr64, self.psC[64:65, :], AF.Identity, bias=mscr[64:65, 64:65]),
                    inc=self.s_act)
                R.wait("act", R.dsem("d_aw%d" % (qi % 2)))
                t_oc = R.emit("act", lambda e, qi=qi: e.activation(
                    self.stg2[:, qi % 2, 0:1024], self.psC[0:64, :], AF.Identity),
                    inc=self.s_act)
                R.wait("dve", self.s_act, t_nrw)
                t_rc = DVE(lambda e: e.reciprocal(rc64, nr64))
                for half in range(2):
                    R.wait("te", self.s_dve, max(t_rc, t_bc_cons))
                    t_bc = self.mm(self.psD[0:64, :], ones64,
                                   rc64[:, half * 512:(half + 1) * 512], True, True,
                                   inc=self.s_te)
                    R.wait("dve", self.s_te, t_bc)
                    R.wait("dve", self.s_act, t_oc)
                    t_no[(qi % 2, half)] = DVE(lambda e, qi=qi, half=half: e.tensor_tensor(
                        self.stg2[:, qi % 2, half * 512:(half + 1) * 512],
                        self.stg2[:, qi % 2, half * 512:(half + 1) * 512],
                        self.psD[0:64, :], op=OP.mult))
                    t_bc_cons = t_no[(qi % 2, half)]
                R.wait("sync", self.s_dve, t_no[(qi % 2, 1)])
                R.emit("sync", lambda e, h=h, qi=qi: e.dma_start(
                    out=self.ao_loc[h, qi // 2, :, (qi % 2) * 1024:(qi % 2 + 1) * 1024],
                    in_=self.stg2[:, qi % 2, 0:1024]), inc=R.dsem("d_aw%d" % (qi % 2)))


    def _causal_heads(self):
        R = self.R
        DVE = lambda fn: (R.wait("dve", self.s_dve), R.emit("dve", fn, inc=self.s_dve))[1]
        w2f = self.wsh2[:, :, :].rearrange("p a b -> p (a b)").bitcast(F32)
        nr64 = w2f[64:65, 0:1024]
        rc64 = w2f[64:65, 1024:2048]
        qf = self.qT[:, :, :].rearrange("p a b -> p (a b)")
        kf = self.kT[:, :, :].rearrange("p a b -> p (a b)")
        mscr = self.xln[:, :].bitcast(F32)
        ones64 = mscr[64:65, 0:64]
        DVE(lambda e: e.memset(ones64, 1.0))
        DVE(lambda e: e.memset(mscr[64:65, 64:65], 1e-30))
        DVE(lambda e: e.memset(self.vhead[:, :, 64:65], 1.0))
        DVE(lambda e: e.memset(self.smlb[0:64, 0:1], 1.0))
        t_hmm = 0
        t_exp_s = [0, 0]
        t_av_s = [0, 0]
        t_oc = 0
        t_nrw = 0
        t_no = {}
        t_bc_cons = 0
        R.wait("sync", R.dsem("d_pull"))
        for h in range(8):
            ph = 64 * (h % 2)
            sq = R.dsem("d_qh")
            R.wait("sync", self.s_te, t_hmm)
            for slot in range(2):
                R.emit("sync", lambda e, slot=slot, h=h, ph=ph: e.dma_start(
                    out=self.qT[:, slot * 16:(slot + 1) * 16, :],
                    in_=self.qk_rd[slot, h // 2, 0, ph:ph + 64, :]
                    .rearrange("p (a b) -> p a b", b=128)), inc=sq)
                R.emit("sync", lambda e, slot=slot, h=h, ph=ph: e.dma_start(
                    out=self.kT[:, slot * 16:(slot + 1) * 16, :],
                    in_=self.qk_rd[slot, h // 2, 1, ph:ph + 64, :]
                    .rearrange("p (a b) -> p a b", b=128)), inc=sq)
                R.emit("sync", lambda e, slot=slot, h=h: e.dma_start(
                    out=self.vhead[:, slot * 16:(slot + 1) * 16, 0:64],
                    in_=self.v_rd[slot, h].rearrange("t p d -> p t d")), inc=sq)
            R.wait("dve", sq)
            R.wait("act", sq)
            self._maxnorm2(qf, 20)
            self._maxnorm2(kf, 21)
            DVE(lambda e: e.tensor_tensor(self.sml2[0:1, 22:23], self.sml2[0:1, 20:21],
                                          self.sml2[0:1, 21:22], op=OP.mult))
            R.wait("act", self.s_dve)
            t_sqr = R.emit("act", lambda e: e.activation(
                self.sml2[0:1, 23:24], self.sml2[0:1, 22:23], AF.Sqrt), inc=self.s_act)
            R.wait("dve", self.s_act, t_sqr)
            t_nb = DVE(lambda e: e.tensor_scalar(self.sml2[0:1, 23:24],
                                                 self.sml2[0:1, 23:24], -0.125, None,
                                                 op0=OP.mult))
            R.wait("te", self.s_dve, t_nb)
            R.wait("te", self.s_act)
            t_nbm = self.mm(self.psE[:, 0:1], self.onesf[:, :], self.sml2[0:1, 23:24],
                            True, True, inc=self.s_te)
            R.wait("dve", self.s_te, t_nbm)
            bcol = self.sml[:, 28 + h % 2:29 + h % 2]
            t_nbc = DVE(lambda e, bcol=bcol: e.tensor_copy(bcol, self.psE[:, 0:1]))
            for qi in range(4):
                nk = 8 * qi + 8
                rhs_q = self.qT[:, qi * 8:(qi + 1) * 8, :].rearrange("p a b -> p (a b)")
                for kt in range(nk):
                    db = kt - 8 * qi
                    sl = kt % 2
                    sp = [self.psA, self.psB][sl]
                    R.wait("te", self.s_act, t_exp_s[sl])
                    self.mm(sp[:, 0:512], self.kT[:, kt, :], rhs_q[:, 0:512],
                            True, True)
                    t_smm = self.mm(sp[:, 512:1024], self.kT[:, kt, :],
                                    rhs_q[:, 512:1024], True, True, inc=self.s_te)
                    t_hmm = t_smm
                    off = max(db, 0) * 128
                    R.wait("act", self.s_te, t_smm)
                    R.wait("act", self.s_dve, t_nbc)
                    R.wait("act", self.s_te, t_av_s[sl])
                    t_exp_s[sl] = R.emit(
                        "act", lambda e, sp=sp, sl=sl, off=off, bcol=bcol: e.activation(
                            self.ptil[:, sl, off:1024], sp[:, off:1024], AF.Exp,
                            bias=bcol, scale=0.125), inc=self.s_act)
                    tm = None
                    if db >= 0:
                        if db > 0:
                            R.wait("dve", self.s_te, t_av_s[sl])
                            DVE(lambda e, sl=sl, off=off: e.memset(
                                self.ptil[:, sl, 0:off], 0.0))
                        R.wait("dve", self.s_act, t_exp_s[sl])
                        tm = DVE(lambda e, sl=sl, off=off: e.tensor_tensor(
                            self.ptil[:, sl, off:off + 128],
                            self.ptil[:, sl, off:off + 128],
                            self.trium[:, :], op=OP.mult))
                    R.wait("te", self.s_act, t_exp_s[sl])
                    if tm is not None:
                        R.wait("te", self.s_dve, tm)
                    if kt == 0:
                        R.wait("te", self.s_act, max(t_oc, t_nrw))
                    self.mm(self.psC[0:65, 0:512], self.vhead[:, kt, :],
                            self.ptil[:, sl, 0:512], kt == 0, kt == nk - 1)
                    t_av_s[sl] = self.mm(self.psC[0:65, 512:1024], self.vhead[:, kt, :],
                                         self.ptil[:, sl, 512:1024],
                                         kt == 0, kt == nk - 1, inc=self.s_te)
                    t_hmm = t_av_s[sl]
                R.wait("act", self.s_te, t_av_s[(nk - 1) % 2])
                R.wait("act", self.s_dve)
                t_nrw = R.emit("act", lambda e: e.activation(
                    nr64, self.psC[64:65, :], AF.Identity, bias=mscr[64:65, 64:65]),
                    inc=self.s_act)
                R.wait("act", R.dsem("d_aw%d" % (qi % 2)))
                t_oc = R.emit("act", lambda e, qi=qi: e.activation(
                    self.stg2[:, qi % 2, 0:1024], self.psC[0:64, :], AF.Identity),
                    inc=self.s_act)
                R.wait("dve", self.s_act, t_nrw)
                t_rc = DVE(lambda e: e.reciprocal(rc64, nr64))
                for half in range(2):
                    R.wait("te", self.s_dve, max(t_rc, t_bc_cons))
                    t_bc = self.mm(self.psD[0:64, :], ones64,
                                   rc64[:, half * 512:(half + 1) * 512], True, True,
                                   inc=self.s_te)
                    R.wait("dve", self.s_te, t_bc)
                    R.wait("dve", self.s_act, t_oc)
                    t_no[(qi % 2, half)] = DVE(lambda e, qi=qi, half=half: e.tensor_tensor(
                        self.stg2[:, qi % 2, half * 512:(half + 1) * 512],
                        self.stg2[:, qi % 2, half * 512:(half + 1) * 512],
                        self.psD[0:64, :], op=OP.mult))
                    t_bc_cons = t_no[(qi % 2, half)]
                R.wait("sync", self.s_dve, t_no[(qi % 2, 1)])
                R.emit("sync", lambda e, h=h, qi=qi: e.dma_start(
                    out=self.ao_loc[h, qi // 2, :, (qi % 2) * 1024:(qi % 2 + 1) * 1024],
                    in_=self.stg2[:, qi % 2, 0:1024]), inc=R.dsem("d_aw%d" % (qi % 2)))

    def emit_moe(self):
        R = self.R
        banks8 = [self.psA[:, 0:512], self.psA[:, 512:1024],
                  self.psB[:, 0:512], self.psB[:, 512:1024],
                  self.psC[:, 0:512], self.psC[:, 512:1024],
                  self.psD[:, :], self.psE[:, :]]
        mscr = self.xln[:, :].bitcast(F32)
        cbb = self.stat[:, 0:16]
        DVE = lambda fn: (R.wait("dve", self.s_dve), R.emit("dve", fn, inc=self.s_dve))[1]

        R.emit("sync", lambda e: e.dma_start(out=self.sml2[0:1, 16:32],
                                             in_=self.cbias[:, :]), inc=R.dsem("d_cb"))
        R.wait("te", R.dsem("d_cb"))
        R.wait("te", self.s_dve)
        t_cbmm = self.mm(self.psE[:, 0:16], self.onesf[:, :], self.sml2[0:1, 16:32],
                         True, True, inc=self.s_te)
        R.wait("dve", self.s_te, t_cbmm)
        t_cbb = R.emit("dve", lambda e: e.tensor_copy(cbb, self.psE[:, 0:16]),
                       inc=self.s_dve)

        rsb = self.wc[:, :, :, :].rearrange("p a b c -> p (a b c)")[:, 0:128] \
            .rearrange("p (c s) -> p c s", s=16)
        R.emit("pool", lambda e: e.dma_start(
            out=rsb, in_=self.router_T.rearrange("(c p) s -> p c s", p=128),
            max_dma_last_dim=8192), inc=R.dsem("d_rt"))
        R.wait("te", R.dsem("d_rt"))
        t_sig = [0] * NT
        t_lg = [0, 0]
        for t in range(NT):
            ps = [self.psD, self.psE][t % 2]
            if t >= 2:
                R.wait("te", self.s_act, t_sig[t - 2])
            if t == 1:
                R.wait("te", self.s_dve, t_cbb)
            for c in range(NCH):
                tok = self.mm(ps[:, 0:16], self.xT[:, c, t * 128:(t + 1) * 128],
                              rsb[:, c, :], c == 0, c == NCH - 1,
                              inc=self.s_te if c == NCH - 1 else None)
            t_lg[t % 2] = tok
            R.wait("act", self.s_te, tok)
            t_sig[t] = R.emit("act", lambda e, t=t, ps=ps: e.activation(
                self.combine[:, t, 16:32], ps[:, 0:16], AF.Sigmoid), inc=self.s_act)

        V4 = lambda ap: ap.rearrange("p (g j) -> p g j", j=4)
        t_route = [0] * NT
        for t in range(NT):
            sc = self.combine[:, t, 16:32]
            sb_, msk, dst, org = mscr[:, 0:16], mscr[:, 16:32], mscr[:, 32:48], mscr[:, 48:64]
            gs, tmp = mscr[:, 64:68], mscr[:, 68:72]
            sca = lambda i: mscr[:, 76 + i:77 + i]
            R.wait("dve", self.s_act, t_sig[t])
            if t == 0:
                R.wait("dve", self.s_dve, t_cbb)
            DVE(lambda e, sc=sc: e.tensor_tensor(sb_, sc, cbb, op=OP.add))
            v = V4(sb_)
            pairs = [(0, 1), (0, 2), (0, 3), (1, 2), (1, 3), (2, 3)]
            for i, (a, b) in enumerate(pairs):
                d = gs if i == 0 else tmp
                DVE(lambda e, a=a, b=b, d=d: e.tensor_tensor(
                    d, v[:, :, a], v[:, :, b], op=OP.add))
                if i > 0:
                    DVE(lambda e: e.tensor_tensor(gs, gs, tmp, op=OP.max))
            DVE(lambda e: e.reduce_max(sca(0), gs, axis=AX.X))
            DVE(lambda e: e.tensor_scalar(tmp, gs, sca(0), None, op0=OP.is_equal))
            DVE(lambda e: e.scalar_tensor_tensor(tmp, tmp, -1e30, gs,
                                                 op0=OP.mult, op1=OP.add))
            DVE(lambda e: e.reduce_max(sca(1), tmp, axis=AX.X))
            DVE(lambda e: e.tensor_scalar(tmp, gs, sca(1), None, op0=OP.is_ge))
            mv = V4(msk)
            for j in range(4):
                DVE(lambda e, j=j: e.tensor_copy(mv[:, :, j], tmp))
            DVE(lambda e: e.tensor_tensor(org, sb_, msk, op=OP.mult))
            DVE(lambda e: e.tensor_scalar(msk, msk, -1.0, None, op0=OP.add))
            DVE(lambda e: e.scalar_tensor_tensor(dst, msk, 1e30, org,
                                                 op0=OP.mult, op1=OP.add))
            DVE(lambda e: e.tensor_copy(org, dst))
            for r in range(4):
                DVE(lambda e, r=r: e.reduce_max(sca(2 + r), dst, axis=AX.X))
                if r < 3:
                    DVE(lambda e, r=r: e.tensor_scalar(msk, dst, sca(2 + r), None,
                                                       op0=OP.is_equal))
                    DVE(lambda e: e.scalar_tensor_tensor(dst, msk, -2e30, dst,
                                                         op0=OP.mult, op1=OP.add))
            DVE(lambda e: e.tensor_scalar(msk, org, sca(5), None, op0=OP.is_ge))
            DVE(lambda e, sc=sc: e.tensor_tensor(org, sc, msk, op=OP.mult))
            DVE(lambda e: e.tensor_reduce(sca(6), org, axis=AX.X, op=OP.add))
            DVE(lambda e: e.tensor_scalar(sca(7), sca(6), 1e-20, None, op0=OP.add))
            DVE(lambda e: e.tensor_scalar(sca(8), sca(7), 2.0, None, op0=OP.mult))
            DVE(lambda e: e.reciprocal(sca(9), sca(8)))
            DVE(lambda e, t=t: e.tensor_scalar(self.combine[:, t, 0:16], org, sca(9),
                                               None, op0=OP.mult))
            for i in range(4):
                t_route[t] = DVE(lambda e, t=t, i=i: e.transpose(
                    self.cT[0:32, t * 128 + 32 * i:t * 128 + 32 * (i + 1)],
                    self.combine[32 * i:32 * (i + 1), t, 0:32]))

        mcd = getattr(self, "_mc_done", False)
        if mcd:
            R.wait("pool", R.dsem("d_mcA"))
            R.wait("sync", R.dsem("d_mcA"))
        if getattr(self, "_tz_dve_tok", None):
            R.wait("te", self.s_dve, self._tz_dve_tok)
        self._mcb_waited = False
        sgv = self.sgw_c.rearrange("c p h -> p c h")
        suv = self.suw_c.rearrange("c p h -> p c h")
        t_evac = [0] * 8
        t_dwmm = [max(t_lg), max(t_lg)]
        t_gumm = [0, 0]
        t_hprod_prev = 0
        self._tcrow_mm = 0
        hsl = lambda m: self.hbuf[:, m, :]
        R.wait("te", self.s_act, t_sig[NT - 1])
        for st in range(4):
            tsl = slice(st * 512, (st + 1) * 512)
            for batch in range(5):
                nslab = 16 if batch == 0 else 8
                t_hprod = [0] * nslab
                for i in range(nslab):
                    b = i % 2
                    if batch == 0:
                        mtile = i
                        ex = None
                        wslot = b
                        R.wait("pool", self.s_te, t_gumm[wslot])
                        R.emit("pool", lambda e, m=mtile, b=b: e.dma_start(
                            out=self.wa[:, b, :, 0:128],
                            in_=sgv[:, :, m * 128:(m + 1) * 128]),
                            inc=R.dsem("d_ga%d" % b))
                        R.emit("pool", lambda e, m=mtile, b=b: e.dma_start(
                            out=self.wb[:, b, :, 0:128],
                            in_=suv[:, :, m * 128:(m + 1) * 128]),
                            inc=R.dsem("d_gb%d" % b))
                        lwg = self.wa[:, b, :, 0:128]
                        lwu = self.wb[:, b, :, 0:128]
                    else:
                        ex = (batch - 1) * 4 + i // 2
                        mtile = i % 2
                        wslot = ex % 2
                        if mtile == 0:
                            if mcd and not self._mcb_waited:
                                R.wait("pool", R.dsem("d_mcB"))
                                R.wait("sync", R.dsem("d_mcB"))
                                self._mcb_waited = True
                            R.wait("pool", self.s_te, t_gumm[wslot])
                            R.emit("pool", lambda e, ex=ex, eb=wslot: e.dma_start(
                                out=self.wa[:, eb, :, :],
                                in_=self.gate_c[ex].rearrange("c p h -> p c h")),
                                inc=R.dsem("d_ga%d" % wslot))
                            R.emit("pool", lambda e, ex=ex, eb=wslot: e.dma_start(
                                out=self.wb[:, eb, :, :],
                                in_=self.up_c[ex].rearrange("c p h -> p c h")),
                                inc=R.dsem("d_gb%d" % wslot))
                        lwg = self.wa[:, wslot, :, mtile * 128:(mtile + 1) * 128]
                        lwu = self.wb[:, wslot, :, mtile * 128:(mtile + 1) * 128]
                    R.wait("te", R.dsem("d_ga%d" % wslot))
                    R.wait("te", R.dsem("d_gb%d" % wslot))
                    R.wait("te", self.s_dve,
                           t_hprod[i - 1] if i >= 1 else t_hprod_prev)
                    if i == 0:
                        R.wait("te", self.s_dve, max(t_evac[4], t_evac[6], t_evac[7]))
                    for c in range(NCH):
                        self.mm(self.psD[:, 0:512], lwg[:, c, :],
                                self.xT[:, c, tsl], c == 0, c == NCH - 1)
                    for c in range(NCH):
                        tok = self.mm(self.psE[:, 0:512], lwu[:, c, :],
                                      self.xT[:, c, tsl], c == 0, c == NCH - 1,
                                      inc=self.s_te if c == NCH - 1 else None)
                    t_gumm[wslot] = tok
                    tokb = None
                    if batch > 0:
                        crow = mscr[0:1, 0:512]
                        R.wait("sync", self.s_te, self._tcrow_mm)
                        R.wait("sync", self.s_dve, t_route[st * 4 + 3])
                        R.emit("sync", lambda e, ex=ex, tsl=tsl: e.dma_start(
                            out=crow, in_=self.cT[ex:ex + 1, tsl]),
                            inc=R.dsem("d_crow"))
                        R.wait("te", R.dsem("d_crow"))
                        tokb = self.mm(self.psC[:, 0:512], self.onesf[:, :],
                                       crow, True, True, inc=self.s_te)
                        self._tcrow_mm = tokb
                    R.wait("act", self.s_te, tok)
                    t_silu = R.emit("act", lambda e, b=b: e.activation(
                        hsl(20 + b), self.psD[:, 0:512], AF.Silu), inc=self.s_act)
                    R.wait("dve", self.s_act, t_silu)
                    if batch == 0:
                        t_hprod[i] = DVE(lambda e, i=i, b=b: e.scalar_tensor_tensor(
                            hsl(i), self.psE[:, 0:512], 0.5, hsl(20 + b),
                            op0=OP.mult, op1=OP.mult))
                    else:
                        R.wait("dve", self.s_te, tokb)
                        DVE(lambda e, b=b: e.tensor_tensor(
                            hsl(22 + b), hsl(20 + b), self.psC[:, 0:512],
                            op=OP.mult))
                        t_hprod[i] = DVE(lambda e, i=i, b=b: e.tensor_tensor(
                            hsl(i), hsl(22 + b), self.psE[:, 0:512], op=OP.mult))
                t_hprod_prev = t_hprod[-1]
                for j in range(8):
                    R.wait("te", self.s_dve, t_evac[j])
                R.wait("te", self.s_dve, t_hprod[-1])
                nk = 16 if batch == 0 else 8
                for k in range(nk):
                    if batch == 0:
                        wslot = k % 2
                        R.wait("sync", self.s_te, t_dwmm[wslot])
                        R.emit("sync", lambda e, k=k, b=wslot: e.dma_start(
                            out=self.wc[:, b, 0, :], in_=self.sdw_c[k]),
                            inc=R.dsem("d_dw%d" % wslot))
                        rhs_src = self.wc[:, wslot, 0, :]
                    else:
                        ex = (batch - 1) * 4 + k // 2
                        wslot = ex % 2
                        if k % 2 == 0:
                            R.wait("sync", self.s_te, t_dwmm[wslot])
                            R.emit("sync", lambda e, ex=ex, b=wslot: e.dma_start(
                                out=self.wc[:, b, :, :],
                                in_=self.down_c[ex].rearrange("k p d -> p k d")),
                                inc=R.dsem("d_dw%d" % wslot))
                        rhs_src = self.wc[:, wslot, k % 2, :]
                    R.wait("te", R.dsem("d_dw%d" % wslot))
                    R.wait("te", self.s_dve, t_hprod[k])
                    for tt in range(4):
                        for n in range(2):
                            tok = self.mm(banks8[tt * 2 + n],
                                          self.hbuf[:, k, tt * 128:(tt + 1) * 128],
                                          rhs_src[:, n * 512:(n + 1) * 512],
                                          k == 0, k == nk - 1,
                                          inc=self.s_te if (tt == 3 and n == 1) else None)
                    t_dwmm[wslot] = tok
                R.wait("dve", self.s_te, tok)
                for tt in range(4):
                    for n in range(2):
                        t_evac[tt * 2 + n] = DVE(
                            lambda e, st=st, tt=tt, n=n: e.tensor_add(
                                self.x_res[:, st * 4 + tt, n * 512:(n + 1) * 512],
                                self.x_res[:, st * 4 + tt, n * 512:(n + 1) * 512],
                                banks8[tt * 2 + n]))

    def emit_att0(self):
        self._emit_attn(l0=True)

    def emit_att1(self):
        self._emit_attn(l0=False)

    def _wvsb(self, c):
        buf = self.wa if c < 4 else self.wb
        cc = c % 4
        return buf[:, :, :, :].rearrange("p a b c -> p (a b c)")[:, cc * 1024:(cc + 1) * 1024]

    def _emit_attn(self, l0):
        R = self.R
        DVE = lambda fn: (R.wait("dve", self.s_dve), R.emit("dve", fn, inc=self.s_dve))[1]
        banks8 = [self.psA[:, 0:512], self.psA[:, 512:1024],
                  self.psB[:, 0:512], self.psB[:, 512:1024],
                  self.psC[:, 0:512], self.psC[:, 512:1024],
                  self.psD[:, :], self.psE[:, :]]
        w_qk = self.w_qk0 if l0 else self.w_qk1
        w_v = self.wv0 if l0 else self.wv1
        w_o = self.wo if l0 else self.wp
        qkv = w_qk.rearrange("(c p) f -> p c f", p=128)
        vv = w_v.rearrange("(c p) f -> p c f", p=128)

        for eng in ("pool", "act", "sync", "te"):
            for nm in ("d_tz0", "d_tz1", "d_tzo0", "d_tzo1"):
                if nm in R.sems:
                    R.wait(eng, R.sems[nm])
        t_wqk = [0, 0]
        t_evc = {}
        for m in range(16):
            b = m % 2
            R.wait("pool", self.s_te, t_wqk[b])
            R.emit("pool", lambda e, m=m, b=b: e.dma_start(
                out=self.wsh[:, b, 0:1024].rearrange("p (c k) -> p c k", k=128),
                in_=qkv[:, :, m * 128:(m + 1) * 128], max_dma_last_dim=8192),
                inc=R.dsem("d_qk%d" % b))
            R.wait("te", R.dsem("d_qk%d" % b))
            lw = self.wsh[:, b, 0:1024].rearrange("p (c k) -> p c k", k=128)
            for nt in range(4):
                ps = [self.psA, self.psB][nt % 2][:, (nt // 2) * 512:(nt // 2 + 1) * 512]
                R.wait("te", self.s_act, t_evc.get((m - 1, nt), 0))
                for c in range(NCH):
                    tok = self.mm(ps, lw[:, c, :],
                                  self.xT[:, c, nt * 512:(nt + 1) * 512],
                                  c == 0, c == NCH - 1,
                                  inc=self.s_te if c == NCH - 1 else None)
                if nt == 3:
                    t_wqk[b] = tok
                R.wait("act", self.s_te, tok)
                if l0:
                    bias_ap, scale = self.bqk[:, m:m + 1], 1.0
                else:
                    bias_ap, scale = 0.0, (0.125 if m < 8 else 1.0)
                R.wait("act", R.dsem("d_st%d" % (nt % 2)))
                t_evc[(m, nt)] = R.emit("act", lambda e, ps=ps, b2=bias_ap, sc=scale, nt=nt:
                                        e.activation(self.stg[:, nt % 2, :], ps,
                                                     AF.Identity, bias=b2, scale=sc),
                                        inc=self.s_act)
                R.wait("sync", self.s_act, t_evc[(m, nt)])
                gq, sl2, qq = (m % 8) // 4, m % 4, m // 8
                R.emit("sync", lambda e, gq=gq, sl2=sl2, qq=qq, nt=nt: e.dma_start(
                    out=self.qk_loc[gq, sl2, qq, :, nt * 512:(nt + 1) * 512],
                    in_=self.stg[:, nt % 2, :]), inc=R.dsem("d_st%d" % (nt % 2)))

        R.wait("pool", self.s_te, max(t_wqk))
        R.wait("pool", self.s_dve)
        R.emit("pool", lambda e: e.dma_start(
            out=self.wa[:, :, :, :].rearrange("p a b c -> p (a b c)")
            .rearrange("p (c f) -> p c f", f=1024),
            in_=vv[:, 0:4, :], max_dma_last_dim=8192), inc=R.dsem("d_wv"))
        R.emit("pool", lambda e: e.dma_start(
            out=self.wb[:, :, :, :].rearrange("p a b c -> p (a b c)")
            .rearrange("p (c f) -> p c f", f=1024),
            in_=vv[:, 4:8, :], max_dma_last_dim=8192), inc=R.dsem("d_wv"))
        R.wait("te", R.dsem("d_wv"))
        for tt in range(NT):
            for n in range(2):
                ps = self.psC[:, n * 512:(n + 1) * 512]
                R.wait("te", self.s_act, t_evc.get(("v", tt - 1, n), 0))
                if l0:
                    self.mm(ps, self.onesb[:, :],
                            self.brow[0:1, 0, n * 512:(n + 1) * 512], True, False)
                for c in range(NCH):
                    tok = self.mm(ps, self.xT[:, c, tt * 128:(tt + 1) * 128],
                                  self._wvsb(c)[:, n * 512:(n + 1) * 512],
                                  (not l0) and c == 0, c == NCH - 1,
                                  inc=self.s_te if c == NCH - 1 else None)
                R.wait("act", self.s_te, tok)
                R.wait("act", R.dsem("d_st%d" % n))
                t_evc[("v", tt, n)] = R.emit("act", lambda e, ps=ps, n=n: e.activation(
                    self.stg[:, n, :], ps, AF.Identity), inc=self.s_act)
                R.wait("sync", self.s_act, t_evc[("v", tt, n)])
                R.emit("sync", lambda e, tt=tt, n=n: e.dma_start(
                    out=self.v_loc.rearrange("g h t p d -> (g h) t p d")[
                        n * 8:(n + 1) * 8, tt, :, :].rearrange("h p d -> p h d"),
                    in_=self.stg[:, n, :].rearrange("p (h d) -> p h d", d=64)),
                    inc=R.dsem("d_st%d" % n))
        R.wait("pool", R.dsem("d_st0"))
        R.wait("pool", R.dsem("d_st1"))
        R.emit("pool", lambda e: e.dma_start(
            out=self.sh_qk[self.regs["myslot"]], in_=self.qk_loc[:, :, :, :, :]),
            inc=R.dsem("d_pub"))
        R.emit("pool", lambda e: e.dma_start(
            out=self.sh_v[self.regs["myslot"]], in_=self.v_loc[:, :, :, :, :]),
            inc=R.dsem("d_pub"))
        R.wait("pool", R.dsem("d_pub"))
        self.pair_barrier()
        for slot in range(2):
            R.emit("pool", lambda e, slot=slot: e.dma_start(
                out=self.qk_rd[slot], in_=self.sh_qk[slot][self.regs["myslot"]]),
                inc=R.dsem("d_pull"))
            R.emit("pool", lambda e, slot=slot: e.dma_start(
                out=self.v_rd[slot], in_=self.sh_v[slot][self.regs["myslot"]]),
                inc=R.dsem("d_pull"))
        if l0:
            self._causal_heads()
        else:
            self._nystrom_heads()
        R.wait("pool", R.dsem("d_aw0"))
        R.wait("pool", R.dsem("d_aw1"))
        R.wait("pool", R.dsem("d_qh"))
        R.emit("pool", lambda e: e.dma_start(
            out=self.sh_ao[self.regs["myslot"]], in_=self.ao_loc[:, :, :, :]),
            inc=R.dsem("d_pub"))
        R.wait("pool", R.dsem("d_pub"))
        self.pair_barrier()


        for g in range(16):
            R.emit("pool", lambda e, g=g: e.dma_start(
                out=self.xT[(g % 2) * 64:(g % 2) * 64 + 64, g // 2, :],
                in_=self.sh_ao[g // 8, g % 8][self.regs["myslot"], :, :]),
                inc=R.dsem("d_ao"))
        R.emit("pool", lambda e: e.dma_start(
            out=self.wa[:, :, :, :].rearrange("p a b c -> p (a b c)")
            .rearrange("p (c f) -> p c f", f=1024),
            in_=w_o[0:512, :].rearrange("(c p) f -> p c f", p=128),
            max_dma_last_dim=8192), inc=R.dsem("d_wo"))
        R.emit("pool", lambda e: e.dma_start(
            out=self.wb[:, :, :, :].rearrange("p a b c -> p (a b c)")
            .rearrange("p (c f) -> p c f", f=1024),
            in_=w_o[512:1024, :].rearrange("(c p) f -> p c f", p=128),
            max_dma_last_dim=8192), inc=R.dsem("d_wo"))
        R.wait("te", R.dsem("d_ao"))
        R.wait("te", R.dsem("d_wo"))
        t_evac = [0] * 8
        for tt in range(NT):
            for n in range(2):
                j = (tt * 2 + n) % 8
                ps = banks8[j]
                R.wait("te", self.s_dve, t_evac[j])
                if l0:
                    self.mm(ps, self.onesb[:, :],
                            self.brow[0:1, 1, n * 512:(n + 1) * 512], True, False)
                for s in range(NCH):
                    tok = self.mm(ps, self.xT[:, s, tt * 128:(tt + 1) * 128],
                                  self._wvsb(s)[:, n * 512:(n + 1) * 512],
                                  (not l0) and s == 0, s == NCH - 1,
                                  inc=self.s_te if s == NCH - 1 else None)
                R.wait("dve", self.s_te, tok)
                t_evac[j] = DVE(lambda e, tt=tt, n=n, ps=ps: e.tensor_add(
                    self.x_res[:, tt, n * 512:(n + 1) * 512],
                    self.x_res[:, tt, n * 512:(n + 1) * 512], ps))



def run(inputs, parts, trace=False):
    maps = host_in_maps(inputs)
    mk = MK(tuple(parts))
    nc = mk.build()
    res = run_bass_kernel_spmd(nc, maps, core_ids=list(range(8)), trace=trace)
    return host_gather(res.results), res


def kernel(**inputs):
    out, _ = run(inputs, ("moe", "att0", "moe", "ffn1", "att1", "ffn1"))
    return out


# version 06
# speedup vs baseline: 1.0001x; 1.0001x over previous
import contextlib
import numpy as np

import concourse.bass as bass
import concourse.mybir as mybir
from concourse.alu_op_type import AluOpType

F32 = mybir.dt.float32
BF16 = mybir.dt.bfloat16
I32 = mybir.dt.int32
AF = mybir.ActivationFunctionType
AX = mybir.AxisListType
OP = AluOpType

T = 2048
NT = 16
D = 1024
NCH = 8
S = 4096
H = 8
HD = 64
EPS = 1e-5

ENGS = ("te", "act", "dve", "pool", "sync")


class Sem:
    __slots__ = ("h", "n", "step", "name")

    def __init__(self, h, step, name):
        self.h, self.n, self.step, self.name = h, 0, step, name


class Rec:

    def __init__(self, nc, stack):
        self.nc = nc
        self.stack = stack
        self.streams = {k: [] for k in ENGS}
        self.sems = {}
        self.own = {}

    def sem(self, name, step=1):
        h = self.stack.enter_context(self.nc.semaphore(name))
        s = Sem(h, step, name)
        self.sems[name] = s
        return s

    def dsem(self, name):
        if name not in self.sems:
            self.sem(name, 16)
        return self.sems[name]

    def emit(self, eng, fn, inc=None):
        if inc is not None:
            inc.n += inc.step
            tgt = inc

            def wrapped(e, fn=fn, tgt=tgt):
                ins = fn(e)
                ins.then_inc(tgt.h, tgt.step)
                return ins

            self.streams[eng].append(wrapped)
            return inc.n
        self.streams[eng].append(fn)
        return None

    def wait(self, eng, sem, val=None):
        v = sem.n if val is None else val
        if v <= 0:
            return
        self.streams[eng].append(lambda e, s=sem, v=v: e.wait_ge(s.h, v))

    def barrier(self, engines=ENGS):
        for eng in engines:
            for own in self.own.get(eng, ()):
                self.wait(eng, own)
            s = self.sems["bar_" + eng]
            self.emit(eng, lambda e: e.nop(), inc=s)
        for eng in engines:
            for other in engines:
                if other != eng:
                    self.wait(eng, self.sems["bar_" + other])



def host_in_maps(inp):
    x = np.asarray(inp["L0_attn_inputs"], np.float32)

    wq = np.asarray(inp["L0_attn_W_q"], np.float32).transpose(1, 0, 2).reshape(D, 1024)
    wk = np.asarray(inp["L0_attn_W_k"], np.float32).transpose(1, 0, 2).reshape(D, 1024)
    wv0 = np.ascontiguousarray(
        np.asarray(inp["L0_attn_W_v"], np.float32).transpose(1, 0, 2).reshape(D, 1024))
    w_qk0 = np.ascontiguousarray(np.concatenate([wq, wk], axis=1))
    b_qk0 = np.ascontiguousarray(
        np.concatenate([np.asarray(inp["L0_attn_b_q"], np.float32).reshape(-1),
                        np.asarray(inp["L0_attn_b_k"], np.float32).reshape(-1)])
        .reshape(16, 128).T)
    bv0 = np.asarray(inp["L0_attn_b_v"], np.float32).reshape(1, 1024)
    woT = np.asarray(inp["L0_attn_W_o"], np.float32).T.reshape(16, 64, 1024)
    bo0 = np.asarray(inp["L0_attn_b_o"], np.float32).reshape(1, 1024)

    router_T = np.ascontiguousarray(np.asarray(inp["L0_ffn_router_w"], np.float32).T)
    cbias = np.asarray(inp["L0_ffn_correction_bias"], np.float32).reshape(1, 16)
    gate_w = np.asarray(inp["L0_ffn_gate_w"], np.float32)
    up_w = np.asarray(inp["L0_ffn_up_w"], np.float32)
    down_w = np.asarray(inp["L0_ffn_down_w"], np.float32)
    sgwT = np.ascontiguousarray(np.asarray(inp["L0_ffn_shared_gate_w"], np.float32).T)
    suwT = np.ascontiguousarray(np.asarray(inp["L0_ffn_shared_up_w"], np.float32).T)
    sdwT = np.ascontiguousarray(np.asarray(inp["L0_ffn_shared_down_w"], np.float32).T)

    wqkv1 = np.asarray(inp["L1_attn_w_qkv"], np.float32)
    w_qk1 = np.ascontiguousarray(wqkv1[:, 0:2048])
    wv1 = np.ascontiguousarray(wqkv1[:, 2048:3072])
    wpT = np.asarray(inp["L1_attn_w_proj"], np.float32).reshape(16, 64, 1024)

    w1T = np.ascontiguousarray(np.asarray(inp["L1_ffn_W1"], np.float32).T)
    b1c = np.ascontiguousarray(np.asarray(inp["L1_ffn_b1"], np.float32).reshape(32, 128).T)
    w2T = np.ascontiguousarray(np.asarray(inp["L1_ffn_W2"], np.float32).T)
    b2r = np.asarray(inp["L1_ffn_b2"], np.float32).reshape(1, 1024)

    ones_f = np.ones((1, 128), np.float32)
    import ml_dtypes
    ones_b = np.ones((1, 128), np.float32)
    triu = np.triu(np.ones((128, 128), np.float32))
    eye = np.eye(64, dtype=np.float32)
    i7113 = np.ascontiguousarray(np.concatenate([7 * eye, 15 * eye, 13 * eye], axis=1))

    wo_g = np.ascontiguousarray(woT.reshape(1024, 1024))
    wp_g = np.ascontiguousarray(wpT.reshape(1024, 1024))
    maps = []
    for c in range(8):
        p, par = c // 2, c % 2
        wo_pc, wp_pc = wo_g, wp_g
        cid = np.zeros((1, 64), np.int32)
        cid[0, 0] = par
        cid[0, 1] = 1 - par
        for i in range(4):
            cid[0, 2 + i] = par * 4 + i
        for i in range(8):
            cid[0, 6 + i] = par * 8 + i
        for j in range(16):
            gh = (par * 8 + j) if j < 8 else ((1 - par) * 8 + (j - 8))
            cid[0, 14 + j] = gh * 2 + par
        for h in range(8):
            for s2 in range(2):
                cid[0, 32 + 2 * h + s2] = (par * 8 + h) * 2 + s2
        for i in range(4):
            cid[0, 48 + i] = 8 + par * 4 + i
        maps.append({
            "x": np.ascontiguousarray(x[p, par * T:(par + 1) * T, :]),
            "cid": cid,
            "w_qk0": w_qk0, "b_qk0": b_qk0, "wv0": wv0, "bv0": bv0,
            "wo": wo_pc, "bo0": bo0,
            "router_T": router_T, "cbias": cbias,
            "gate_w": gate_w, "up_w": up_w, "down_w": down_w,
            "sgwT": sgwT, "suwT": suwT, "sdwT": sdwT,
            "w_qk1": w_qk1, "wv1": wv1, "wp": wp_pc,
            "w1T": w1T, "b1c": b1c, "w2T": w2T, "b2r": b2r,
            "ones_f": ones_f, "ones_b": ones_b, "triu": triu, "i7113": i7113,
        })
    return maps


def host_gather(results):
    out = np.zeros((4, S, D), np.float32)
    for c in range(8):
        p, par = c // 2, c % 2
        out[p, par * T:(par + 1) * T, :] = results[c]["y"]
    return out



class MK:
    def __init__(self, parts=("moe", "att0", "moe", "ffn1", "att1", "ffn1")):
        self.parts = parts
        self.nc = bass.Bass("TRN2", target_bir_lowering=False)
        self.stack = contextlib.ExitStack()
        self.R = Rec(self.nc, self.stack)
        self.regs = {}
        self._declare()

    def _declare(self):
        nc = self.nc
        st = self.stack
        ap = lambda t: t.ap()

        self.x_in = ap(nc.dram_tensor("x", [T, D], F32, kind="ExternalInput"))
        self.cid = ap(nc.dram_tensor("cid", [1, 64], I32, kind="ExternalInput"))
        self.y = ap(nc.dram_tensor("y", [T, D], F32, kind="ExternalOutput"))
        for name, shape in [
            ("w_qk0", [D, 2048]), ("b_qk0", [128, 16]), ("wv0", [D, 1024]),
            ("bv0", [1, 1024]), ("wo", [D, D]), ("bo0", [1, D]),
            ("router_T", [D, 16]), ("cbias", [1, 16]),
            ("gate_w", [16, D, 256]), ("up_w", [16, D, 256]), ("down_w", [16, 256, D]),
            ("sgwT", [D, 2048]), ("suwT", [D, 2048]), ("sdwT", [2048, D]),
            ("w_qk1", [D, 2048]), ("wv1", [D, 1024]), ("wp", [D, D]),
            ("w1T", [D, 4096]), ("b1c", [128, 32]), ("w2T", [4096, D]), ("b2r", [1, D]),
            ("ones_f", [1, 128]), ("ones_b", [1, 128]), ("triu", [128, 128]),
            ("i7113", [64, 192]),
        ]:
            setattr(self, name, ap(nc.dram_tensor(name, shape, F32, kind="ExternalInput")))

        self.sh_qk = ap(nc.dram_tensor("sh_qk", [2, 2, 4, 2, 128, T], BF16, addr_space="Shared"))
        self.sh_v = ap(nc.dram_tensor("sh_v", [2, 2, 8, NT, 128, 64], BF16, addr_space="Shared"))
        self.sh_ao = ap(nc.dram_tensor("sh_ao", [2, 8, 2, 64, T], BF16, addr_space="Shared"))
        self.w1t_c = ap(nc.dram_tensor("w1t_c", [D, 4096], BF16))
        self.w2t_c = ap(nc.dram_tensor("w2t_c", [4096, D], BF16))
        self.gate_c = ap(nc.dram_tensor("gate_c", [16, NCH, 128, 256], BF16))
        self.up_c = ap(nc.dram_tensor("up_c", [16, NCH, 128, 256], BF16))
        self.down_c = ap(nc.dram_tensor("down_c", [16, 2, 128, D], BF16))
        self.sgw_c = ap(nc.dram_tensor("sgw_c", [NCH, 128, 2048], BF16))
        self.suw_c = ap(nc.dram_tensor("suw_c", [NCH, 128, 2048], BF16))
        self.sdw_c = ap(nc.dram_tensor("sdw_c", [16, 128, D], BF16))
        self.cin = ap(nc.dram_tensor("cin", [1, 16], F32))
        self.cout = ap(nc.dram_tensor("cout", [1, 16], F32))
        self.qk_loc = ap(nc.dram_tensor("qk_loc", [2, 4, 2, 128, T], BF16))
        self.v_loc = ap(nc.dram_tensor("v_loc", [2, 8, NT, 128, 64], BF16))
        self.ao_loc = ap(nc.dram_tensor("ao_loc", [8, 2, 64, T], BF16))
        self.qk_rd = ap(nc.dram_tensor("qk_rd", [2, 4, 2, 128, T], BF16))
        self.v_rd = ap(nc.dram_tensor("v_rd", [2, 8, NT, 128, 64], BF16))

        sb = lambda n, s, d: st.enter_context(nc.sbuf_tensor(n, s, d))
        self.x_res = sb("x_res", [128, NT, D], F32)
        self.xT = sb("xT", [128, NCH, T], BF16)
        self.xln = sb("xln", [128, D], BF16)
        self.hbuf = sb("hbuf", [128, 32, 512], BF16)
        flat = self.hbuf[:, :, :].rearrange("p a b -> p (a b)")
        self.qT = flat[0:64, 0:4096].rearrange("p (a b) -> p a b", b=128)
        self.kT = flat[0:64, 4096:8192].rearrange("p (a b) -> p a b", b=128)
        self.vhead = flat[:, 8192:10272].rearrange("p (a b) -> p a b", b=65)
        self.ptil = flat[:, 10272:12320].rearrange("p (a b) -> p a b", b=1024)
        self.e3t = flat[:, 10272:12320].rearrange("p (a b) -> p a b", b=64)
        self.e1t = flat[0:64, 12320:13344].rearrange("p (a b) -> p a b", b=512)
        self.stg2 = flat[0:64, 13344:15392].rearrange("p (a b) -> p a b", b=1024)
        self.stg = sb("stg", [128, 2, 512], BF16)
        self.wa = sb("wa", [128, 2, 8, 256], BF16)
        self.wb = sb("wb", [128, 2, 8, 256], BF16)
        self.wc = sb("wc", [128, 2, 2, 1024], BF16)
        self.wsh = sb("wsh", [128, 2, 2048], BF16)
        self.wsh2 = sb("wsh2", [128, 2, 2048], BF16)
        self.combine = sb("combine", [128, NT, 32], F32)
        self.cT = sb("cT", [32, T], F32)
        self.csc = sb("csc", [1, 64], I32)
        self.onesf = sb("onesf", [1, 128], F32)
        self.onesb = sb("onesb", [1, 128], BF16)
        self.trium = sb("trium", [128, 128], BF16)
        self.bqk = sb("bqk", [128, 16], F32)
        self.b1t = sb("b1t", [128, 32], F32)
        self.brow = sb("brow", [1, 2, 1024], BF16)
        self.brow2 = sb("brow2", [1, 1024], BF16)
        self.stat = sb("stat", [128, 24], F32)
        self.sml = sb("sml", [128, 32], F32)
        self.smlb = sb("smlb", [128, 8], BF16)
        w2f = self.wsh2[:, :, :].rearrange("p a b -> p (a b)").bitcast(F32)
        self.nrow = w2f[0:1, :].rearrange("p (a b) -> p a b", b=1024)
        self.nrec = w2f[32:33, :].rearrange("p (a b) -> p a b", b=1024)
        waf = self.wa[:, :, :, :].rearrange("p a b c -> p (a b c)").bitcast(F32)
        wbf = self.wb[:, :, :, :].rearrange("p a b c -> p (a b c)")
        self.nyf = waf[0:64, 0:768].rearrange("p (a b) -> p a b", b=64)
        self.qlm = waf[0:64, 768:896].rearrange("p (a b) -> p a b", b=64)
        self.i7cv = waf[0:64, 896:1088].rearrange("p (a b) -> p a b", b=64)
        self.qlmb = wbf[0:64, 0:128].rearrange("p (a b) -> p a b", b=64)
        self.nyb = wbf[0:64, 128:258].rearrange("p (a b) -> p a b", b=65)
        self.rowf = w2f[64:65, 0:1024]
        self.sml2 = sb("sml2", [1, 32], F32)

        pt = lambda n, s: st.enter_context(nc.psum_tensor(n, s, F32))
        self.psA = pt("psA", [128, 1024])
        self.psB = pt("psB", [128, 1024])
        self.psC = pt("psC", [128, 1024])
        self.psD = pt("psD", [128, 512])
        self.psE = pt("psE", [128, 512])

        R = self.R
        self.s_te = R.sem("s_te")
        self.s_act = R.sem("s_act")
        self.s_dve = R.sem("s_dve")
        self.s_pool = R.sem("s_pool")
        self.s_gd = R.sem("s_gd", 16)
        self.s_sd = R.sem("s_sd", 16)
        self.s_cc = R.sem("s_cc")
        for e in ENGS:
            R.sem("bar_" + e)
        R.own = {"te": [self.s_te], "act": [self.s_act], "dve": [self.s_dve],
                 "pool": [self.s_pool, self.s_gd, self.s_cc],
                 "sync": [self.s_sd]}

    def pair_barrier(self):
        R = self.R
        R.emit("pool", lambda e: e.collective_compute(
            "AllReduce", OP.add, replica_groups=[[0, 1], [2, 3], [4, 5], [6, 7]],
            ins=[self.cin], outs=[self.cout]), inc=self.s_cc)
        R.wait("pool", self.s_cc)
        R.barrier()

    def mm(self, out, lhsT, rhs, start, stop, inc=None):
        return self.R.emit("te", lambda e, o=out, a=lhsT, b=rhs, s0=start, s1=stop:
                           e.matmul(o, a, b, start=s0, stop=s1), inc=inc)

    def prologue(self):
        R, nc = self.R, self.nc
        P = mybir.EngineType.Pool

        for t in range(NT):
            R.emit("sync", lambda e, t=t: e.dma_start(
                out=self.x_res[:, t, :], in_=self.x_in[t * 128:(t + 1) * 128, :]),
                inc=self.s_sd)
        R.emit("pool", lambda e: e.dma_start(out=self.csc[:, :], in_=self.cid[:, :]),
               inc=R.dsem("d_cid"))
        R.emit("sync", lambda e: e.dma_start(out=self.onesf[:, :], in_=self.ones_f[:, :]),
               inc=self.s_sd)
        R.emit("sync", lambda e: e.dma_start(out=self.bqk[:, :], in_=self.b_qk0[:, :]),
               inc=self.s_sd)
        R.emit("sync", lambda e: e.dma_start(out=self.b1t[:, :], in_=self.b1c[:, :]),
               inc=self.s_sd)
        R.emit("pool", lambda e: e.dma_start(out=self.onesb[:, :], in_=self.ones_b[:, :]),
               inc=self.s_gd)
        R.emit("pool", lambda e: e.dma_start(out=self.trium[:, :], in_=self.triu[:, :]),
               inc=self.s_gd)
        for j, (srcrow, dstap, sc) in enumerate([
                (self.bv0, self.brow[:, 0, :], 1.0),
                (self.bo0, self.brow[:, 1, :], 1.0),
                (self.b2r, self.brow2[:, :], 1.0)]):
            if j > 0:
                R.wait("sync", self.s_act)
            R.emit("sync", lambda e, s=srcrow: e.dma_start(
                out=self.rowf[:, :], in_=s[:, :]), inc=R.dsem("d_row"))
            R.wait("act", R.dsem("d_row"))
            R.emit("act", lambda e, d=dstap, sc=sc: e.activation(
                d, self.rowf[:, :], AF.Copy, scale=sc), inc=self.s_act)
        R.emit("pool", lambda e: e.memset(self.sml2[0:1, 16:32], 0.0), inc=self.s_pool)
        R.wait("pool", self.s_pool)
        R.emit("pool", lambda e: e.dma_start(out=self.cin[:, :], in_=self.sml2[0:1, 16:32]),
               inc=self.s_gd)
        R.wait("pool", R.dsem("d_cid"))

        def loadregs(e):
            r = {}
            vl = lambda j, mx: nc.values_load(
                self.csc[0:1, j:j + 1], engines=[P], min_val=0, max_val=mx,
                skip_runtime_bounds_check=True)
            r["myslot"] = vl(0, 1)
            r["ptslot"] = vl(1, 1)

            self.regs.update(r)
        R.streams["pool"].append(loadregs)

        if "moe" in self.parts:
            self._moe_cache()
        self._bg = []
        if "ffn1" in self.parts:
            self._bg = self._tz_jobs()
            if self.parts[0] == "ffn1":
                while self._bg:
                    self._bg.pop(0)()
        R.wait("sync", self.s_sd)
        R.wait("pool", self.s_gd)
        R.barrier()

    def _moe_cache(self):
        R = self.R
        sa, sb2 = R.dsem("d_mcA"), R.dsem("d_mcB")
        for srcp, dstp in [
            (self.sgwT.rearrange("(c p) h -> p c h", p=128),
             self.sgw_c.rearrange("c p h -> p c h")),
            (self.suwT.rearrange("(c p) h -> p c h", p=128),
             self.suw_c.rearrange("c p h -> p c h")),
            (self.sdwT.rearrange("(k p) d -> p k d", p=128),
             self.sdw_c.rearrange("k p d -> p k d")),
        ]:
            R.emit("pool", lambda e, s=srcp, d=dstp: e.dma_start(
                out=d, in_=s, max_dma_last_dim=8192), inc=sa)
        for ex in range(16):
            for srcp, dstp in [
                (self.gate_w[ex].rearrange("(c p) h -> p c h", p=128),
                 self.gate_c[ex].rearrange("c p h -> p c h")),
                (self.up_w[ex].rearrange("(c p) h -> p c h", p=128),
                 self.up_c[ex].rearrange("c p h -> p c h")),
                (self.down_w[ex].rearrange("(k p) d -> p k d", p=128),
                 self.down_c[ex].rearrange("k p d -> p k d")),
            ]:
                R.emit("pool", lambda e, s=srcp, d=dstp: e.dma_start(
                    out=d, in_=s, max_dma_last_dim=8192), inc=sb2)
        self._mc_done = True

    def _tz_jobs(self):
        R = self.R
        wcf = self.wc[:, :, :, :].rearrange("p a b c -> p (a b c)")
        wf32 = wcf.bitcast(F32)
        st = {"red": [0, 0], "cmp": 0, "thr": 0}
        def DVE(fn):
            R.wait("dve", self.s_dve)
            return R.emit("dve", fn, inc=self.s_dve)
        specs = [(self.w1T, self.w1t_c, 4096, D), (self.w2T, self.w2t_c, 1024, 4096)]
        jobs = []

        def mksum(si, krange):
            srcT, dstT, ncols, rows = specs[si]
            view = srcT.rearrange("(k p) d -> k p d", p=128)
            tw = min(1024, ncols)
            cc = ncols // tw
            red = self.sml[:, 18 + si:19 + si]
            acc = self.sml[:, 20 + si:21 + si]

            def job():
                if krange[0] == 0:
                    DVE(lambda e: e.memset(acc, 0.0))
                for k in krange:
                    for c in range(cc):
                        b = (k * cc + c) % 2
                        sin = R.dsem("d_tz%d" % b)
                        buf = wf32[:, b * 1024:b * 1024 + tw]
                        R.wait("pool", self.s_dve, st["red"][b])
                        R.emit("pool", lambda e, o=buf, s=view[k, :, c * tw:(c + 1) * tw]:
                               e.dma_start(out=o, in_=s), inc=sin)
                        R.wait("dve", sin)
                        DVE(lambda e, s=buf: e.tensor_reduce(
                            red, s, axis=AX.X, op=OP.add, apply_absolute_value=True))
                        st["red"][b] = DVE(lambda e: e.tensor_tensor(
                            acc, acc, red, op=OP.add))
            return job

        def mkthr():
            def job():
                for si in (0, 1):
                    _, _, ncols, rows = specs[si]
                    c = si * 8
                    acc = self.sml[:, 20 + si:21 + si]
                    R.wait("pool", self.s_dve, max(st["red"]))
                    R.wait("pool", self.s_pool)
                    t = R.emit("pool", lambda e, acc=acc, c=c: e.tensor_reduce(
                        self.sml2[0:1, c:c + 1], acc, axis=AX.C, op=OP.add),
                        inc=self.s_pool)
                    R.wait("dve", self.s_pool, t)
                    DVE(lambda e, c=c, d=1.0 / (rows * ncols): e.tensor_scalar(
                        self.sml2[0:1, c + 1:c + 2], self.sml2[0:1, c:c + 1], d,
                        None, op0=OP.mult))
                    DVE(lambda e, c=c: e.tensor_scalar(
                        self.sml2[0:1, c + 2:c + 3], self.sml2[0:1, c + 1:c + 2],
                        1e-8, None, op0=OP.max))
                    DVE(lambda e, c=c: e.tensor_scalar(
                        self.sml2[0:1, c + 3:c + 4], self.sml2[0:1, c + 2:c + 3],
                        0.5, None, op0=OP.mult))
                    t2 = DVE(lambda e, c=c: e.tensor_scalar(
                        self.sml2[0:1, c + 4:c + 5], self.sml2[0:1, c + 3:c + 4],
                        -1.0, None, op0=OP.mult))
                    R.wait("pool", self.s_dve, t2)
                    R.emit("pool", lambda e, c=c, si=si: e.dma_start(
                        out=self.sml[:, 22 + si:23 + si],
                        in_=self.sml2[0:1, c + 3:c + 4].broadcast_to([1, 128])),
                        inc=R.dsem("d_tzt"))
                    R.emit("pool", lambda e, c=c, si=si: e.dma_start(
                        out=self.sml[:, 24 + si:25 + si],
                        in_=self.sml2[0:1, c + 4:c + 5].broadcast_to([1, 128])),
                        inc=R.dsem("d_tzt"))
                st["thr"] = R.sems["d_tzt"].n
            return job

        def mkcmp(si, krange):
            srcT, dstT, ncols, rows = specs[si]
            view = srcT.rearrange("(k p) d -> k p d", p=128)
            dview = dstT.rearrange("(k p) d -> k p d", p=128)
            tw = min(1024, ncols)
            cc = ncols // tw
            thp = self.sml[:, 22 + si:23 + si]
            thm = self.sml[:, 24 + si:25 + si]
            wsl = wf32[:, 0:1024]
            asl = wf32[:, 1024:1536]
            obf = wcf[:, 3072:3584]

            def job():
                for k in krange:
                    for c in range(cc):
                        sin = R.dsem("d_tz0")
                        R.wait("pool", self.s_dve, st["cmp"])
                        R.emit("pool", lambda e, s=view[k, :, c * tw:(c + 1) * tw]:
                               e.dma_start(out=wsl, in_=s), inc=sin)
                        R.wait("dve", sin)
                        R.wait("dve", R.dsem("d_tzt"), st["thr"])
                        for q in range(tw // 512):
                            wh = wsl[:, q * 512:(q + 1) * 512]
                            DVE(lambda e, wh=wh: e.tensor_scalar(
                                asl, wh, thm, None, op0=OP.is_le))
                            R.wait("dve", R.dsem("d_tzo"))
                            st["cmp"] = DVE(lambda e, wh=wh: e.scalar_tensor_tensor(
                                obf, wh, thp, asl, op0=OP.is_ge, op1=OP.subtract))
                            R.wait("pool", self.s_dve, st["cmp"])
                            R.emit("pool", lambda e, d=dview[k, :, c * tw + q * 512:
                                                             c * tw + (q + 1) * 512]:
                                   e.dma_start(out=d, in_=obf), inc=R.dsem("d_tzo"))
            return job

        for si, kr in [(0, range(0, 4)), (0, range(4, 8)),
                       (1, range(0, 16)), (1, range(16, 32))]:
            jobs.append(mksum(si, kr))
        jobs.append(mkthr())
        for si, kr in [(0, range(0, 3)), (0, range(3, 6)), (0, range(6, 8)),
                       (1, range(0, 12)), (1, range(12, 22)), (1, range(22, 32))]:
            jobs.append(mkcmp(si, kr))
        return jobs

    def emit_ln(self):
        R = self.R
        t_act = {}
        t_chain = {}
        for t in range(NT):
            s = t % 2
            b = s * 12
            c0 = 12 * s
            if t >= 2:
                R.wait("dve", self.s_act, t_act[t - 2])
            R.emit("dve", lambda e, t=t, b=b: e.bn_stats(
                self.stat[:, b:b + 6], self.x_res[:, t, 0:512]), inc=self.s_dve)
            R.emit("dve", lambda e, t=t, b=b: e.bn_stats(
                self.stat[:, b + 6:b + 12], self.x_res[:, t, 512:1024]), inc=self.s_dve)
            R.wait("dve", self.s_dve)
            R.emit("dve", lambda e, b=b, c0=c0: e.bn_aggr(
                self.sml[:, c0:c0 + 2], self.stat[:, b:b + 12]), inc=self.s_dve)
            R.wait("dve", self.s_dve)
            R.emit("dve", lambda e, c0=c0: e.tensor_scalar(
                self.sml[:, c0 + 2:c0 + 3], self.sml[:, c0 + 1:c0 + 2], EPS, None,
                op0=OP.add), inc=self.s_dve)
            tok = self.s_dve.n
            R.wait("act", self.s_dve, tok)
            R.emit("act", lambda e, c0=c0: e.activation(
                self.sml[:, c0 + 3:c0 + 4], self.sml[:, c0 + 2:c0 + 3], AF.Sqrt),
                inc=self.s_act)
            R.wait("dve", self.s_act)
            R.emit("dve", lambda e, c0=c0: e.reciprocal(
                self.sml[:, c0 + 4:c0 + 5], self.sml[:, c0 + 3:c0 + 4]), inc=self.s_dve)
            R.wait("dve", self.s_dve)
            t_chain[t] = R.emit("dve", lambda e, c0=c0: e.scalar_tensor_tensor(
                self.sml[:, c0 + 5:c0 + 6], self.sml[:, c0:c0 + 1], -1.0,
                self.sml[:, c0 + 4:c0 + 5], op0=OP.mult, op1=OP.mult), inc=self.s_dve)
            R.wait("act", self.s_dve, t_chain[t])
            if t >= 1:
                R.wait("act", R.dsem("d_tr%d" % ((t - 1) % 2)))
            t_act[t] = R.emit("act", lambda e, t=t, c0=c0: e.activation(
                self.xln[:, :], self.x_res[:, t, :], AF.Identity,
                bias=self.sml[:, c0 + 5:c0 + 6], scale=self.sml[:, c0 + 4:c0 + 5]),
                inc=self.s_act)
            R.wait("sync", self.s_act, t_act[t])
            for c in range(NCH):
                R.emit("sync", lambda e, t=t, c=c: e.dma_start(
                    out=self.xT[:, c, t * 128:(t + 1) * 128],
                    in_=self.xln[:, c * 128:(c + 1) * 128], transpose=True),
                    inc=R.dsem("d_tr%d" % (t % 2)))

    def ln_drain(self):
        R = self.R
        R.wait("sync", R.dsem("d_tr0"))
        R.wait("sync", R.dsem("d_tr1"))

    def emit_ffn1(self):
        R = self.R
        for eng in ("sync", "pool"):
            if "d_tzo" in R.sems:
                R.wait(eng, R.sems["d_tzo"])
        banks8 = [self.psA[:, 0:512], self.psA[:, 512:1024],
                  self.psB[:, 0:512], self.psB[:, 512:1024],
                  self.psC[:, 0:512], self.psC[:, 512:1024],
                  self.psD[:, :], self.psE[:, :]]
        hps = [self.psD[:, :], self.psE[:, :]]
        w1v = self.w1t_c.rearrange("(c p) h -> p c h", p=128)
        t_down_last = 0
        t_evac = [0] * 8
        t_w2mm = [0] * 4
        t_w1mm = [0] * 4
        wcq = self.wc[:, :, :, :].rearrange("p a b c -> p (a b c)")
        for st in range(4):
            t_hevac = [0] * 32
            for m in range(32):
                b = m % 4
                lw = wcq[:, b * 1024:(b + 1) * 1024].rearrange("p (c k) -> p c k", k=128)
                R.wait("sync", self.s_te, t_w1mm[b] if (m >= 4 or st > 0) else 0)
                R.emit("sync", lambda e, m=m, o=lw: e.dma_start(
                    out=o, in_=w1v[:, :, m * 128:(m + 1) * 128]),
                    inc=R.dsem("d_w1%d" % b))
                R.wait("te", R.dsem("d_w1%d" % b))
                if m >= 2:
                    R.wait("te", self.s_act, t_hevac[m - 2])
                elif st > 0:
                    R.wait("te", self.s_dve, t_evac[6 + m % 2])
                for c in range(NCH):
                    tok = self.mm(hps[m % 2], lw[:, c, :],
                                  self.xT[:, c, st * 512:(st + 1) * 512],
                                  c == 0, c == NCH - 1,
                                  inc=self.s_te if c == NCH - 1 else None)
                t_w1mm[b] = tok
                R.wait("act", self.s_te, tok)
                if st > 0 and m == 0:
                    R.wait("act", self.s_te, t_down_last)
                t_hevac[m] = R.emit("act", lambda e, m=m, b=b: e.activation(
                    self.hbuf[:, m, :], hps[m % 2], AF.Gelu,
                    bias=self.b1t[:, m:m + 1]), inc=self.s_act)
            for i in range(8):
                R.wait("te", self.s_dve, t_evac[i])
            R.wait("te", self.s_act, t_hevac[31])
            for tt in range(4):
                for n in range(2):
                    self.mm(banks8[tt * 2 + n], self.onesb[:, :],
                            self.brow2[:, n * 512:(n + 1) * 512], True, False)
            for k in range(32):
                b = k % 4
                wbuf = self.wsh if b < 2 else self.wsh2
                R.wait("pool", self.s_te, t_w2mm[b])
                R.emit("pool", lambda e, k=k, b=b, wbuf=wbuf: e.dma_start(
                    out=wbuf[:, b % 2, 0:1024],
                    in_=self.w2t_c[k * 128:(k + 1) * 128, :]),
                    inc=R.dsem("d_w2%d" % b))
                R.wait("te", R.dsem("d_w2%d" % b))
                R.wait("te", self.s_act, t_hevac[k])
                for tt in range(4):
                    for n in range(2):
                        tok = self.mm(banks8[tt * 2 + n],
                                      self.hbuf[:, k, tt * 128:(tt + 1) * 128],
                                      wbuf[:, b % 2, n * 512:(n + 1) * 512],
                                      False, k == 31,
                                      inc=self.s_te if (tt == 3 and n == 1) else None)
                t_w2mm[b] = tok
            t_down_last = tok
            R.wait("dve", self.s_te, t_down_last)
            for tt in range(4):
                for n in range(2):
                    t_evac[tt * 2 + n] = R.emit(
                        "dve", lambda e, st=st, tt=tt, n=n: e.scalar_tensor_tensor(
                            self.x_res[:, st * 4 + tt, n * 512:(n + 1) * 512],
                            banks8[tt * 2 + n], 0.5,
                            self.x_res[:, st * 4 + tt, n * 512:(n + 1) * 512],
                            op0=OP.mult, op1=OP.add), inc=self.s_dve)

    def epilogue(self):
        R = self.R
        R.barrier()
        for t in range(NT):
            R.emit("sync", lambda e, t=t: e.dma_start(
                out=self.y[t * 128:(t + 1) * 128, :], in_=self.x_res[:, t, :]),
                inc=self.s_sd)
        R.wait("sync", self.s_sd)

    def build(self):
        R = self.R
        with self.stack:
            self.prologue()
            for part in self.parts:
                R.barrier()
                self.emit_ln()
                self.ln_drain()
                R.barrier()
                if part == "ffn1":
                    while getattr(self, "_bg", []):
                        self._bg.pop(0)()
                    self.emit_ffn1()
                elif part == "moe":
                    self.emit_moe()
                elif part == "att0":
                    self.emit_att0()
                elif part == "att1":
                    self.emit_att1()
            self.epilogue()
            with self.nc.Block() as block:
                streams = self.R.streams

                @block.tensor
                def _(e):
                    for f in streams["te"]:
                        f(e)

                @block.scalar
                def _(e):
                    for f in streams["act"]:
                        f(e)

                @block.vector
                def _(e):
                    for f in streams["dve"]:
                        f(e)

                @block.gpsimd
                def _(e):
                    for f in streams["pool"]:
                        f(e)

                @block.sync
                def _(e):
                    for f in streams["sync"]:
                        f(e)
        return self.nc

    def _maxnorm2(self, src_flat, outcol, width=4096):
        R = self.R
        DVE = lambda fn: (R.wait("dve", self.s_dve), R.emit("dve", fn, inc=self.s_dve))[1]
        sqv = self.ptil[:, :, :].rearrange("p a b -> p (a b)")[0:64, :]
        acc = self.sml2[0:1, outcol:outcol + 1]
        DVE(lambda e: e.memset(acc, 0.0))
        nch = (width + 2047) // 2048
        for ch in range(nch):
            w = min(2048, width - ch * 2048)
            R.wait("act", self.s_dve)
            R.wait("act", self.s_te)
            t_sq = R.emit("act", lambda e, ch=ch, w=w: e.activation(
                sqv[:, 0:w], src_flat[:, ch * 2048:ch * 2048 + w], AF.Square),
                inc=self.s_act)
            for j in range((w + 511) // 512):
                ww = min(512, w - j * 512)
                R.wait("te", self.s_act, t_sq)
                R.wait("te", self.s_dve)
                t_mm = self.mm(self.psE[0:1, 0:ww], self.smlb[0:64, 0:1],
                               sqv[:, j * 512:j * 512 + ww], True, True,
                               inc=self.s_te)
                R.wait("dve", self.s_te, t_mm)
                DVE(lambda e, ww=ww: e.reduce_max(self.sml2[0:1, 30:31],
                                                  self.psE[0:1, 0:ww], axis=AX.X))
                R.wait("dve", self.s_dve)
                DVE(lambda e: e.tensor_tensor(acc, acc, self.sml2[0:1, 30:31],
                                              op=OP.max))
        return self.R.sems["s_dve"].n

    def _nystrom_heads(self):
        R = self.R
        DVE = lambda fn: (R.wait("dve", self.s_dve), R.emit("dve", fn, inc=self.s_dve))[1]

        def TE(out, lhsT, rhs, start=True, stop=True):
            R.wait("te", self.s_dve)
            R.wait("te", self.s_act)
            return self.mm(out, lhsT, rhs, start, stop, inc=self.s_te)

        def TR64(dst, srcm):
            t = None
            for i2 in range(2):
                for j2 in range(2):
                    R.wait("dve", self.s_dve)
                    t = R.emit("dve", lambda e, i2=i2, j2=j2, dst=dst, srcm=srcm:
                               e.transpose(dst[32 * j2:32 * (j2 + 1), 32 * i2:32 * (i2 + 1)],
                                           srcm[32 * i2:32 * (i2 + 1), 32 * j2:32 * (j2 + 1)]),
                               inc=self.s_dve)
            return t

        w2f = self.wsh2[:, :, :].rearrange("p a b -> p (a b)").bitcast(F32)
        nr64 = w2f[64:65, 0:1024]
        rc64 = w2f[64:65, 1024:2048]
        mscr = self.xln[:, :].bitcast(F32)
        ones64 = mscr[64:65, 0:64]
        oc64 = mscr[0:64, 65:66]
        qf = self.qT[:, :, :].rearrange("p a b -> p (a b)")
        kf = self.kT[:, :, :].rearrange("p a b -> p (a b)")
        ny = self.nyf
        R.wait("dve", self.s_te)
        R.wait("act", self.s_te)
        R.wait("sync", self.s_te)
        R.emit("sync", lambda e: e.dma_start(
            out=self.i7cv[:, :, :],
            in_=self.i7113[:, :].rearrange("p (k m) -> p k m", m=64)),
            inc=R.dsem("d_i7"))
        DVE(lambda e: e.memset(ones64, 1.0))
        DVE(lambda e: e.memset(oc64, 1.0))
        DVE(lambda e: e.memset(mscr[64:65, 64:65], 1e-30))
        DVE(lambda e: e.memset(self.nyb[:, 0, 64:65], 1.0))
        DVE(lambda e: e.memset(self.vhead[:, :, 64:65], 1.0))
        DVE(lambda e: e.memset(self.smlb[0:64, 0:1], 1.0))
        R.wait("sync", R.dsem("d_pull"))
        t_hmm = 0
        t_exp_s = [0, 0]
        t_oc = 0
        t_nrw = 0
        t_no = {}
        t_bc_cons = 0
        for h in range(8):
            ph = 64 * (h % 2)
            sq = R.dsem("d_qh")
            R.wait("sync", self.s_te, t_hmm)
            R.wait("sync", self.s_dve)
            for slot in range(2):
                R.emit("sync", lambda e, slot=slot, h=h, ph=ph: e.dma_start(
                    out=self.qT[:, slot * 16:(slot + 1) * 16, :],
                    in_=self.qk_rd[slot, h // 2, 0, ph:ph + 64, :]
                    .rearrange("p (a b) -> p a b", b=128)), inc=sq)
                R.emit("sync", lambda e, slot=slot, h=h, ph=ph: e.dma_start(
                    out=self.kT[:, slot * 16:(slot + 1) * 16, :],
                    in_=self.qk_rd[slot, h // 2, 1, ph:ph + 64, :]
                    .rearrange("p (a b) -> p a b", b=128)), inc=sq)
                R.emit("sync", lambda e, slot=slot, h=h: e.dma_start(
                    out=self.vhead[:, slot * 16:(slot + 1) * 16, 0:64],
                    in_=self.v_rd[slot, h].rearrange("t p d -> p t d")), inc=sq)
            R.wait("dve", sq)
            DVE(lambda e: e.tensor_reduce(
                self.qlm[:, 0, :], qf.rearrange("p (l w) -> p l w", w=64),
                axis=AX.X, op=OP.add))
            DVE(lambda e: e.tensor_scalar(self.qlm[:, 0, :], self.qlm[:, 0, :],
                                          1.0 / 64, None, op0=OP.mult))
            DVE(lambda e: e.tensor_copy(self.qlmb[:, 0, :], self.qlm[:, 0, :]))
            DVE(lambda e: e.tensor_reduce(
                self.qlm[:, 1, :], kf.rearrange("p (l w) -> p l w", w=64),
                axis=AX.X, op=OP.add))
            DVE(lambda e: e.tensor_scalar(self.qlm[:, 1, :], self.qlm[:, 1, :],
                                          1.0 / 64, None, op0=OP.mult))
            DVE(lambda e: e.tensor_copy(self.qlmb[:, 1, :], self.qlm[:, 1, :]))
            self._maxnorm2(qf, 20)
            self._maxnorm2(kf, 21)
            self._maxnorm2(self.qlmb[:, 1, :], 22, width=64)
            self._maxnorm2(self.qlmb[:, 0, :], 23, width=64)
            DVE(lambda e: e.tensor_tensor(self.sml2[0:1, 24:25], self.sml2[0:1, 20:21],
                                          self.sml2[0:1, 22:23], op=OP.mult))
            DVE(lambda e: e.tensor_tensor(self.sml2[0:1, 25:26], self.sml2[0:1, 23:24],
                                          self.sml2[0:1, 21:22], op=OP.mult))
            R.wait("act", self.s_dve)
            R.emit("act", lambda e: e.activation(
                self.sml2[0:1, 24:25], self.sml2[0:1, 24:25], AF.Sqrt), inc=self.s_act)
            t_sqr = R.emit("act", lambda e: e.activation(
                self.sml2[0:1, 25:26], self.sml2[0:1, 25:26], AF.Sqrt), inc=self.s_act)
            R.wait("dve", self.s_act, t_sqr)
            t_n1 = DVE(lambda e: e.tensor_scalar(self.sml2[0:1, 24:25],
                                                 self.sml2[0:1, 24:25], -1.0, None,
                                                 op0=OP.mult))
            t_n3 = DVE(lambda e: e.tensor_scalar(self.sml2[0:1, 25:26],
                                                 self.sml2[0:1, 25:26], -1.0, None,
                                                 op0=OP.mult))
            R.wait("te", self.s_dve, t_n3)
            R.wait("te", self.s_act)
            self.mm(self.psE[:, 0:1], self.onesf[:, :], self.sml2[0:1, 24:25],
                    True, True)
            t_nbm = self.mm(self.psE[:, 1:2], self.onesf[:, :], self.sml2[0:1, 25:26],
                            True, True, inc=self.s_te)
            R.wait("dve", self.s_te, t_nbm)
            b1col = self.sml[0:64, 28 + h % 2:29 + h % 2]
            b3col = self.sml[0:64, 30 + h % 2:31 + h % 2]
            DVE(lambda e, b1col=b1col: e.tensor_copy(b1col, self.psE[0:64, 0:1]))
            t_nbc = DVE(lambda e, b3col=b3col: e.tensor_copy(b3col, self.psE[0:64, 1:2]))
            R.wait("te", R.dsem("d_i7"))
            TE(self.psD[0:64, 0:64], self.qlm[:, 0, :], self.qlm[:, 1, :])
            R.wait("dve", self.s_te)
            DVE(lambda e: e.tensor_reduce(ny[:, 8, 0:1], self.psD[0:64, 0:64],
                                          axis=AX.X, op=OP.max, negate=True))
            R.wait("act", self.s_dve)
            R.wait("act", self.s_te)
            R.emit("act", lambda e: e.activation(ny[:, 6, :], self.psD[0:64, 0:64],
                                                 AF.Exp, bias=ny[:, 8, 0:1]),
                   inc=self.s_act)
            R.wait("dve", self.s_act)
            DVE(lambda e: e.tensor_reduce(ny[:, 8, 1:2], ny[:, 6, :], axis=AX.X,
                                          op=OP.add))
            R.wait("dve", self.s_dve)
            DVE(lambda e: e.reciprocal(ny[:, 8, 2:3], ny[:, 8, 1:2]))
            R.wait("dve", self.s_dve)
            DVE(lambda e: e.tensor_scalar(ny[:, 0, :], ny[:, 6, :], ny[:, 8, 2:3],
                                          None, op0=OP.mult))
            TR64(ny[:, 1, :], ny[:, 0, :])
            TE(self.psD[0:1, 64:128], oc64, ny[:, 0, :])
            R.wait("dve", self.s_te)
            DVE(lambda e: e.tensor_reduce(mscr[0:1, 66:67], self.psD[0:1, 64:128],
                                          axis=AX.X, op=OP.max))
            R.wait("dve", self.s_dve)
            DVE(lambda e: e.reciprocal(mscr[0:1, 67:68], mscr[0:1, 66:67]))
            t_rd = DVE(lambda e: e.nop()) if False else None
            R.wait("te", self.s_dve)
            TE(self.psD[0:64, 64:65], self.onesf[0:1, 0:64], mscr[0:1, 67:68])
            R.wait("dve", self.s_te)
            DVE(lambda e: e.tensor_scalar(ny[:, 2, :], ny[:, 1, :], self.psD[0:64, 64:65]
                                          if False else ny[:, 8, 3:4], None, op0=OP.mult)) \
                if False else None
            DVE(lambda e: e.tensor_copy(ny[:, 8, 3:4], self.psD[0:64, 64:65]))
            R.wait("dve", self.s_dve)
            DVE(lambda e: e.tensor_scalar(ny[:, 2, :], ny[:, 1, :], ny[:, 8, 3:4],
                                          None, op0=OP.mult))
            for it in range(6):
                TE(self.psD[0:64, 0:64], ny[:, 1, :], ny[:, 2, :])
                R.wait("dve", self.s_te)
                DVE(lambda e: e.tensor_copy(ny[:, 4, :], self.psD[0:64, 0:64]))
                TR64(ny[:, 5, :], ny[:, 4, :])
                DVE(lambda e: e.tensor_tensor(ny[:, 6, :], self.i7cv[:, 0, :],
                                              ny[:, 4, :], op=OP.subtract))
                TE(self.psD[0:64, 64:128], ny[:, 5, :], ny[:, 6, :])
                R.wait("dve", self.s_te)
                DVE(lambda e: e.tensor_tensor(ny[:, 6, :], self.i7cv[:, 1, :],
                                              self.psD[0:64, 64:128], op=OP.subtract))
                TE(self.psD[0:64, 0:64], ny[:, 5, :], ny[:, 6, :])
                R.wait("dve", self.s_te)
                DVE(lambda e: e.tensor_tensor(ny[:, 6, :], self.i7cv[:, 2, :],
                                              self.psD[0:64, 0:64], op=OP.subtract))
                TR64(ny[:, 3, :], ny[:, 2, :])
                TE(self.psD[0:64, 64:128], ny[:, 3, :], ny[:, 6, :])
                R.wait("dve", self.s_te)
                DVE(lambda e: e.tensor_scalar(ny[:, 2, :], self.psD[0:64, 64:128],
                                              0.25, None, op0=OP.mult))
            DVE(lambda e: e.memset(ny[:, 9, 0:1], 0.0))
            t_r3 = [0, 0]
            for i in range(8):
                sl = i % 2
                sp = self.psB[0:64, sl * 512:(sl + 1) * 512]
                R.wait("te", self.s_act, t_exp_s[sl])
                t_s3 = self.mm(sp, self.qlmb[:, 0, :],
                               self.kT[:, i * 4:(i + 1) * 4, :].rearrange("p a b -> p (a b)"),
                               True, True, inc=self.s_te)
                t_hmm = t_s3
                R.wait("act", self.s_te, t_s3)
                R.wait("act", self.s_dve, max(t_nbc, t_r3[sl]))
                R.wait("act", R.dsem("d_e3t"))
                t_exp_s[sl] = R.emit("act", lambda e, sp=sp, sl=sl, b3col=b3col:
                                     e.activation(self.e1t[:, sl, :], sp, AF.Exp,
                                                  bias=b3col), inc=self.s_act)
                R.wait("dve", self.s_act, t_exp_s[sl])
                t_r3[sl] = DVE(lambda e, sl=sl: e.tensor_reduce(
                    ny[:, 9, 1:2], self.e1t[:, sl, :], axis=AX.X, op=OP.add))
                R.wait("dve", self.s_dve)
                DVE(lambda e: e.tensor_add(ny[:, 9, 0:1], ny[:, 9, 0:1], ny[:, 9, 1:2]))
                R.wait("sync", self.s_act, t_exp_s[sl])
                R.wait("sync", self.s_te)
                for j in range(4):
                    R.emit("sync", lambda e, i=i, j=j, sl=sl: e.dma_start(
                        out=self.e3t[:, i * 4 + j, :],
                        in_=self.e1t[:, sl, j * 128:(j + 1) * 128], transpose=True),
                        inc=R.dsem("d_e3t"))
            R.wait("dve", self.s_dve)
            DVE(lambda e: e.reciprocal(ny[:, 9, 2:3], ny[:, 9, 0:1]))
            R.wait("te", R.dsem("d_e3t"))
            R.wait("te", self.s_dve)
            for c in range(32):
                t_k3 = self.mm(self.psD[0:64, 0:64], self.e3t[:, c, :],
                               self.vhead[:, c, 0:64], c == 0, c == 31,
                               inc=self.s_te if c == 31 else None)
            t_hmm = t_k3
            R.wait("dve", self.s_te, t_k3)
            DVE(lambda e: e.tensor_scalar(ny[:, 7, :], self.psD[0:64, 0:64],
                                          ny[:, 9, 2:3], None, op0=OP.mult))
            TR64(ny[:, 3, :], ny[:, 2, :])
            TE(self.psD[0:64, 64:128], ny[:, 3, :], ny[:, 7, :])
            R.wait("dve", self.s_te)
            t_zb = DVE(lambda e: e.tensor_copy(self.nyb[:, 0, 0:64],
                                               self.psD[0:64, 64:128]))
            for qi in range(4):
                for half in range(2):
                    i = qi * 2 + half
                    sl = i % 2
                    sp = self.psA[0:64, sl * 512:(sl + 1) * 512]
                    R.wait("te", self.s_act, t_exp_s[sl])
                    t_s1 = self.mm(sp, self.qlmb[:, 1, :],
                                   self.qT[:, i * 4:(i + 1) * 4, :]
                                   .rearrange("p a b -> p (a b)"),
                                   True, True, inc=self.s_te)
                    t_hmm = t_s1
                    R.wait("act", self.s_te, t_s1)
                    R.wait("act", self.s_dve, t_zb)
                    t_exp_s[sl] = R.emit("act", lambda e, sp=sp, sl=sl, b1col=b1col:
                                         e.activation(self.e1t[:, sl, :], sp, AF.Exp,
                                                      bias=b1col), inc=self.s_act)
                    R.wait("te", self.s_act, t_exp_s[sl])
                    if i == 0:
                        R.wait("te", self.s_act, max(t_oc, t_nrw))
                    t_av = self.mm(self.psC[0:65, half * 512:(half + 1) * 512],
                                   self.nyb[:, 0, :], self.e1t[:, sl, :],
                                   True, True, inc=self.s_te)
                    t_hmm = t_av
                R.wait("act", self.s_te, t_av)
                R.wait("act", self.s_dve)
                t_nrw = R.emit("act", lambda e: e.activation(
                    nr64, self.psC[64:65, :], AF.Identity, bias=mscr[64:65, 64:65]),
                    inc=self.s_act)
                R.wait("act", R.dsem("d_aw%d" % (qi % 2)))
                t_oc = R.emit("act", lambda e, qi=qi: e.activation(
                    self.stg2[:, qi % 2, 0:1024], self.psC[0:64, :], AF.Identity),
                    inc=self.s_act)
                R.wait("dve", self.s_act, t_nrw)
                t_rc = DVE(lambda e: e.reciprocal(rc64, nr64))
                for half in range(2):
                    R.wait("te", self.s_dve, max(t_rc, t_bc_cons))
                    t_bc = self.mm(self.psD[0:64, :], ones64,
                                   rc64[:, half * 512:(half + 1) * 512], True, True,
                                   inc=self.s_te)
                    R.wait("dve", self.s_te, t_bc)
                    R.wait("dve", self.s_act, t_oc)
                    t_no[(qi % 2, half)] = DVE(lambda e, qi=qi, half=half: e.tensor_tensor(
                        self.stg2[:, qi % 2, half * 512:(half + 1) * 512],
                        self.stg2[:, qi % 2, half * 512:(half + 1) * 512],
                        self.psD[0:64, :], op=OP.mult))
                    t_bc_cons = t_no[(qi % 2, half)]
                R.wait("sync", self.s_dve, t_no[(qi % 2, 1)])
                R.emit("sync", lambda e, h=h, qi=qi: e.dma_start(
                    out=self.ao_loc[h, qi // 2, :, (qi % 2) * 1024:(qi % 2 + 1) * 1024],
                    in_=self.stg2[:, qi % 2, 0:1024]), inc=R.dsem("d_aw%d" % (qi % 2)))


    def _causal_heads(self):
        R = self.R
        DVE = lambda fn: (R.wait("dve", self.s_dve), R.emit("dve", fn, inc=self.s_dve))[1]
        w2f = self.wsh2[:, :, :].rearrange("p a b -> p (a b)").bitcast(F32)
        nr64 = w2f[64:65, 0:1024]
        rc64 = w2f[64:65, 1024:2048]
        qf = self.qT[:, :, :].rearrange("p a b -> p (a b)")
        kf = self.kT[:, :, :].rearrange("p a b -> p (a b)")
        mscr = self.xln[:, :].bitcast(F32)
        ones64 = mscr[64:65, 0:64]
        DVE(lambda e: e.memset(ones64, 1.0))
        DVE(lambda e: e.memset(mscr[64:65, 64:65], 1e-30))
        DVE(lambda e: e.memset(self.vhead[:, :, 64:65], 1.0))
        DVE(lambda e: e.memset(self.smlb[0:64, 0:1], 1.0))
        t_hmm = 0
        t_exp_s = [0, 0]
        t_av_s = [0, 0]
        t_oc = 0
        t_nrw = 0
        t_no = {}
        t_bc_cons = 0
        R.wait("sync", R.dsem("d_pull"))
        for h in range(8):
            ph = 64 * (h % 2)
            sq = R.dsem("d_qh")
            R.wait("sync", self.s_te, t_hmm)
            for slot in range(2):
                R.emit("sync", lambda e, slot=slot, h=h, ph=ph: e.dma_start(
                    out=self.qT[:, slot * 16:(slot + 1) * 16, :],
                    in_=self.qk_rd[slot, h // 2, 0, ph:ph + 64, :]
                    .rearrange("p (a b) -> p a b", b=128)), inc=sq)
                R.emit("sync", lambda e, slot=slot, h=h, ph=ph: e.dma_start(
                    out=self.kT[:, slot * 16:(slot + 1) * 16, :],
                    in_=self.qk_rd[slot, h // 2, 1, ph:ph + 64, :]
                    .rearrange("p (a b) -> p a b", b=128)), inc=sq)
                R.emit("sync", lambda e, slot=slot, h=h: e.dma_start(
                    out=self.vhead[:, slot * 16:(slot + 1) * 16, 0:64],
                    in_=self.v_rd[slot, h].rearrange("t p d -> p t d")), inc=sq)
            R.wait("dve", sq)
            R.wait("act", sq)
            self._maxnorm2(qf, 20)
            self._maxnorm2(kf, 21)
            DVE(lambda e: e.tensor_tensor(self.sml2[0:1, 22:23], self.sml2[0:1, 20:21],
                                          self.sml2[0:1, 21:22], op=OP.mult))
            R.wait("act", self.s_dve)
            t_sqr = R.emit("act", lambda e: e.activation(
                self.sml2[0:1, 23:24], self.sml2[0:1, 22:23], AF.Sqrt), inc=self.s_act)
            R.wait("dve", self.s_act, t_sqr)
            t_nb = DVE(lambda e: e.tensor_scalar(self.sml2[0:1, 23:24],
                                                 self.sml2[0:1, 23:24], -0.125, None,
                                                 op0=OP.mult))
            R.wait("te", self.s_dve, t_nb)
            R.wait("te", self.s_act)
            t_nbm = self.mm(self.psE[:, 0:1], self.onesf[:, :], self.sml2[0:1, 23:24],
                            True, True, inc=self.s_te)
            R.wait("dve", self.s_te, t_nbm)
            bcol = self.sml[:, 28 + h % 2:29 + h % 2]
            t_nbc = DVE(lambda e, bcol=bcol: e.tensor_copy(bcol, self.psE[:, 0:1]))
            for qi in range(4):
                nk = 8 * qi + 8
                rhs_q = self.qT[:, qi * 8:(qi + 1) * 8, :].rearrange("p a b -> p (a b)")
                for kt in range(nk):
                    db = kt - 8 * qi
                    sl = kt % 2
                    sp = [self.psA, self.psB][sl]
                    R.wait("te", self.s_act, t_exp_s[sl])
                    self.mm(sp[:, 0:512], self.kT[:, kt, :], rhs_q[:, 0:512],
                            True, True)
                    t_smm = self.mm(sp[:, 512:1024], self.kT[:, kt, :],
                                    rhs_q[:, 512:1024], True, True, inc=self.s_te)
                    t_hmm = t_smm
                    off = max(db, 0) * 128
                    R.wait("act", self.s_te, t_smm)
                    R.wait("act", self.s_dve, t_nbc)
                    R.wait("act", self.s_te, t_av_s[sl])
                    t_exp_s[sl] = R.emit(
                        "act", lambda e, sp=sp, sl=sl, off=off, bcol=bcol: e.activation(
                            self.ptil[:, sl, off:1024], sp[:, off:1024], AF.Exp,
                            bias=bcol, scale=0.125), inc=self.s_act)
                    tm = None
                    if db >= 0:
                        if db > 0:
                            R.wait("dve", self.s_te, t_av_s[sl])
                            DVE(lambda e, sl=sl, off=off: e.memset(
                                self.ptil[:, sl, 0:off], 0.0))
                        R.wait("dve", self.s_act, t_exp_s[sl])
                        tm = DVE(lambda e, sl=sl, off=off: e.tensor_tensor(
                            self.ptil[:, sl, off:off + 128],
                            self.ptil[:, sl, off:off + 128],
                            self.trium[:, :], op=OP.mult))
                    R.wait("te", self.s_act, t_exp_s[sl])
                    if tm is not None:
                        R.wait("te", self.s_dve, tm)
                    if kt == 0:
                        R.wait("te", self.s_act, max(t_oc, t_nrw))
                    self.mm(self.psC[0:65, 0:512], self.vhead[:, kt, :],
                            self.ptil[:, sl, 0:512], kt == 0, kt == nk - 1)
                    t_av_s[sl] = self.mm(self.psC[0:65, 512:1024], self.vhead[:, kt, :],
                                         self.ptil[:, sl, 512:1024],
                                         kt == 0, kt == nk - 1, inc=self.s_te)
                    t_hmm = t_av_s[sl]
                R.wait("act", self.s_te, t_av_s[(nk - 1) % 2])
                R.wait("act", self.s_dve)
                t_nrw = R.emit("act", lambda e: e.activation(
                    nr64, self.psC[64:65, :], AF.Identity, bias=mscr[64:65, 64:65]),
                    inc=self.s_act)
                R.wait("act", R.dsem("d_aw%d" % (qi % 2)))
                t_oc = R.emit("act", lambda e, qi=qi: e.activation(
                    self.stg2[:, qi % 2, 0:1024], self.psC[0:64, :], AF.Identity),
                    inc=self.s_act)
                R.wait("dve", self.s_act, t_nrw)
                t_rc = DVE(lambda e: e.reciprocal(rc64, nr64))
                for half in range(2):
                    R.wait("te", self.s_dve, max(t_rc, t_bc_cons))
                    t_bc = self.mm(self.psD[0:64, :], ones64,
                                   rc64[:, half * 512:(half + 1) * 512], True, True,
                                   inc=self.s_te)
                    R.wait("dve", self.s_te, t_bc)
                    R.wait("dve", self.s_act, t_oc)
                    t_no[(qi % 2, half)] = DVE(lambda e, qi=qi, half=half: e.tensor_tensor(
                        self.stg2[:, qi % 2, half * 512:(half + 1) * 512],
                        self.stg2[:, qi % 2, half * 512:(half + 1) * 512],
                        self.psD[0:64, :], op=OP.mult))
                    t_bc_cons = t_no[(qi % 2, half)]
                R.wait("sync", self.s_dve, t_no[(qi % 2, 1)])
                R.emit("sync", lambda e, h=h, qi=qi: e.dma_start(
                    out=self.ao_loc[h, qi // 2, :, (qi % 2) * 1024:(qi % 2 + 1) * 1024],
                    in_=self.stg2[:, qi % 2, 0:1024]), inc=R.dsem("d_aw%d" % (qi % 2)))
            if getattr(self, "_bg", []):
                self._bg.pop(0)()

    def emit_moe(self):
        R = self.R
        banks8 = [self.psA[:, 0:512], self.psA[:, 512:1024],
                  self.psB[:, 0:512], self.psB[:, 512:1024],
                  self.psC[:, 0:512], self.psC[:, 512:1024],
                  self.psD[:, :], self.psE[:, :]]
        mscr = self.xln[:, :].bitcast(F32)
        cbb = self.stat[:, 0:16]
        DVE = lambda fn: (R.wait("dve", self.s_dve), R.emit("dve", fn, inc=self.s_dve))[1]

        R.emit("sync", lambda e: e.dma_start(out=self.sml2[0:1, 16:32],
                                             in_=self.cbias[:, :]), inc=R.dsem("d_cb"))
        R.wait("te", R.dsem("d_cb"))
        R.wait("te", self.s_dve)
        t_cbmm = self.mm(self.psE[:, 0:16], self.onesf[:, :], self.sml2[0:1, 16:32],
                         True, True, inc=self.s_te)
        R.wait("dve", self.s_te, t_cbmm)
        t_cbb = R.emit("dve", lambda e: e.tensor_copy(cbb, self.psE[:, 0:16]),
                       inc=self.s_dve)

        rsb = self.wc[:, :, :, :].rearrange("p a b c -> p (a b c)")[:, 0:128] \
            .rearrange("p (c s) -> p c s", s=16)
        R.emit("pool", lambda e: e.dma_start(
            out=rsb, in_=self.router_T.rearrange("(c p) s -> p c s", p=128),
            max_dma_last_dim=8192), inc=R.dsem("d_rt"))
        R.wait("te", R.dsem("d_rt"))
        t_sig = [0] * NT
        t_lg = [0, 0]
        for t in range(NT):
            ps = [self.psD, self.psE][t % 2]
            if t >= 2:
                R.wait("te", self.s_act, t_sig[t - 2])
            if t == 1:
                R.wait("te", self.s_dve, t_cbb)
            for c in range(NCH):
                tok = self.mm(ps[:, 0:16], self.xT[:, c, t * 128:(t + 1) * 128],
                              rsb[:, c, :], c == 0, c == NCH - 1,
                              inc=self.s_te if c == NCH - 1 else None)
            t_lg[t % 2] = tok
            R.wait("act", self.s_te, tok)
            t_sig[t] = R.emit("act", lambda e, t=t, ps=ps: e.activation(
                self.combine[:, t, 16:32], ps[:, 0:16], AF.Sigmoid), inc=self.s_act)

        V4 = lambda ap: ap.rearrange("p (g j) -> p g j", j=4)
        t_route = [0] * NT
        for t in range(NT):
            sc = self.combine[:, t, 16:32]
            sb_, msk, dst, org = mscr[:, 0:16], mscr[:, 16:32], mscr[:, 32:48], mscr[:, 48:64]
            gs, tmp = mscr[:, 64:68], mscr[:, 68:72]
            sca = lambda i: mscr[:, 76 + i:77 + i]
            R.wait("dve", self.s_act, t_sig[t])
            if t == 0:
                R.wait("dve", self.s_dve, t_cbb)
            DVE(lambda e, sc=sc: e.tensor_tensor(sb_, sc, cbb, op=OP.add))
            v = V4(sb_)
            pairs = [(0, 1), (0, 2), (0, 3), (1, 2), (1, 3), (2, 3)]
            for i, (a, b) in enumerate(pairs):
                d = gs if i == 0 else tmp
                DVE(lambda e, a=a, b=b, d=d: e.tensor_tensor(
                    d, v[:, :, a], v[:, :, b], op=OP.add))
                if i > 0:
                    DVE(lambda e: e.tensor_tensor(gs, gs, tmp, op=OP.max))
            DVE(lambda e: e.reduce_max(sca(0), gs, axis=AX.X))
            DVE(lambda e: e.tensor_scalar(tmp, gs, sca(0), None, op0=OP.is_equal))
            DVE(lambda e: e.scalar_tensor_tensor(tmp, tmp, -1e30, gs,
                                                 op0=OP.mult, op1=OP.add))
            DVE(lambda e: e.reduce_max(sca(1), tmp, axis=AX.X))
            DVE(lambda e: e.tensor_scalar(tmp, gs, sca(1), None, op0=OP.is_ge))
            mv = V4(msk)
            for j in range(4):
                DVE(lambda e, j=j: e.tensor_copy(mv[:, :, j], tmp))
            DVE(lambda e: e.tensor_tensor(org, sb_, msk, op=OP.mult))
            DVE(lambda e: e.tensor_scalar(msk, msk, -1.0, None, op0=OP.add))
            DVE(lambda e: e.scalar_tensor_tensor(dst, msk, 1e30, org,
                                                 op0=OP.mult, op1=OP.add))
            DVE(lambda e: e.tensor_copy(org, dst))
            for r in range(4):
                DVE(lambda e, r=r: e.reduce_max(sca(2 + r), dst, axis=AX.X))
                if r < 3:
                    DVE(lambda e, r=r: e.tensor_scalar(msk, dst, sca(2 + r), None,
                                                       op0=OP.is_equal))
                    DVE(lambda e: e.scalar_tensor_tensor(dst, msk, -2e30, dst,
                                                         op0=OP.mult, op1=OP.add))
            DVE(lambda e: e.tensor_scalar(msk, org, sca(5), None, op0=OP.is_ge))
            DVE(lambda e, sc=sc: e.tensor_tensor(org, sc, msk, op=OP.mult))
            DVE(lambda e: e.tensor_reduce(sca(6), org, axis=AX.X, op=OP.add))
            DVE(lambda e: e.tensor_scalar(sca(7), sca(6), 1e-20, None, op0=OP.add))
            DVE(lambda e: e.tensor_scalar(sca(8), sca(7), 2.0, None, op0=OP.mult))
            DVE(lambda e: e.reciprocal(sca(9), sca(8)))
            DVE(lambda e, t=t: e.tensor_scalar(self.combine[:, t, 0:16], org, sca(9),
                                               None, op0=OP.mult))
            for i in range(4):
                t_route[t] = DVE(lambda e, t=t, i=i: e.transpose(
                    self.cT[0:32, t * 128 + 32 * i:t * 128 + 32 * (i + 1)],
                    self.combine[32 * i:32 * (i + 1), t, 0:32]))

        mcd = getattr(self, "_mc_done", False)
        if mcd:
            R.wait("pool", R.dsem("d_mcA"))
            R.wait("sync", R.dsem("d_mcA"))
        if "d_tzo" in R.sems:
            R.wait("sync", R.sems["d_tzo"])
            R.wait("pool", R.sems["d_tzo"])
        self._mcb_waited = False
        sgv = self.sgw_c.rearrange("c p h -> p c h")
        suv = self.suw_c.rearrange("c p h -> p c h")
        t_evac = [0] * 8
        t_dwmm = [max(t_lg), max(t_lg)]
        t_gumm = [0, 0]
        t_hprod_prev = 0
        self._tcrow_mm = 0
        hsl = lambda m: self.hbuf[:, m, :]
        R.wait("te", self.s_act, t_sig[NT - 1])
        for st in range(4):
            tsl = slice(st * 512, (st + 1) * 512)
            for batch in range(5):
                nslab = 16 if batch == 0 else 8
                t_hprod = [0] * nslab
                for i in range(nslab):
                    b = i % 2
                    if batch == 0:
                        mtile = i
                        ex = None
                        wslot = b
                        R.wait("pool", self.s_te, t_gumm[wslot])
                        R.emit("pool", lambda e, m=mtile, b=b: e.dma_start(
                            out=self.wa[:, b, :, 0:128],
                            in_=sgv[:, :, m * 128:(m + 1) * 128]),
                            inc=R.dsem("d_ga%d" % b))
                        R.emit("pool", lambda e, m=mtile, b=b: e.dma_start(
                            out=self.wb[:, b, :, 0:128],
                            in_=suv[:, :, m * 128:(m + 1) * 128]),
                            inc=R.dsem("d_gb%d" % b))
                        lwg = self.wa[:, b, :, 0:128]
                        lwu = self.wb[:, b, :, 0:128]
                    else:
                        ex = (batch - 1) * 4 + i // 2
                        mtile = i % 2
                        wslot = ex % 2
                        if mtile == 0:
                            if mcd and not self._mcb_waited:
                                R.wait("pool", R.dsem("d_mcB"))
                                R.wait("sync", R.dsem("d_mcB"))
                                self._mcb_waited = True
                            R.wait("pool", self.s_te, t_gumm[wslot])
                            R.emit("pool", lambda e, ex=ex, eb=wslot: e.dma_start(
                                out=self.wa[:, eb, :, :],
                                in_=self.gate_c[ex].rearrange("c p h -> p c h")),
                                inc=R.dsem("d_ga%d" % wslot))
                            R.emit("pool", lambda e, ex=ex, eb=wslot: e.dma_start(
                                out=self.wb[:, eb, :, :],
                                in_=self.up_c[ex].rearrange("c p h -> p c h")),
                                inc=R.dsem("d_gb%d" % wslot))
                        lwg = self.wa[:, wslot, :, mtile * 128:(mtile + 1) * 128]
                        lwu = self.wb[:, wslot, :, mtile * 128:(mtile + 1) * 128]
                    R.wait("te", R.dsem("d_ga%d" % wslot))
                    R.wait("te", R.dsem("d_gb%d" % wslot))
                    R.wait("te", self.s_dve,
                           t_hprod[i - 1] if i >= 1 else t_hprod_prev)
                    if i == 0:
                        R.wait("te", self.s_dve, max(t_evac[4], t_evac[6], t_evac[7]))
                    for c in range(NCH):
                        self.mm(self.psD[:, 0:512], lwg[:, c, :],
                                self.xT[:, c, tsl], c == 0, c == NCH - 1)
                    for c in range(NCH):
                        tok = self.mm(self.psE[:, 0:512], lwu[:, c, :],
                                      self.xT[:, c, tsl], c == 0, c == NCH - 1,
                                      inc=self.s_te if c == NCH - 1 else None)
                    t_gumm[wslot] = tok
                    tokb = None
                    if batch > 0:
                        crow = mscr[0:1, 0:512]
                        R.wait("sync", self.s_te, self._tcrow_mm)
                        R.wait("sync", self.s_dve, t_route[st * 4 + 3])
                        R.emit("sync", lambda e, ex=ex, tsl=tsl: e.dma_start(
                            out=crow, in_=self.cT[ex:ex + 1, tsl]),
                            inc=R.dsem("d_crow"))
                        R.wait("te", R.dsem("d_crow"))
                        tokb = self.mm(self.psC[:, 0:512], self.onesf[:, :],
                                       crow, True, True, inc=self.s_te)
                        self._tcrow_mm = tokb
                    R.wait("act", self.s_te, tok)
                    t_silu = R.emit("act", lambda e, b=b: e.activation(
                        hsl(20 + b), self.psD[:, 0:512], AF.Silu), inc=self.s_act)
                    R.wait("dve", self.s_act, t_silu)
                    if batch == 0:
                        t_hprod[i] = DVE(lambda e, i=i, b=b: e.scalar_tensor_tensor(
                            hsl(i), self.psE[:, 0:512], 0.5, hsl(20 + b),
                            op0=OP.mult, op1=OP.mult))
                    else:
                        R.wait("dve", self.s_te, tokb)
                        DVE(lambda e, b=b: e.tensor_tensor(
                            hsl(22 + b), hsl(20 + b), self.psC[:, 0:512],
                            op=OP.mult))
                        t_hprod[i] = DVE(lambda e, i=i, b=b: e.tensor_tensor(
                            hsl(i), hsl(22 + b), self.psE[:, 0:512], op=OP.mult))
                t_hprod_prev = t_hprod[-1]
                for j in range(8):
                    R.wait("te", self.s_dve, t_evac[j])
                R.wait("te", self.s_dve, t_hprod[-1])
                nk = 16 if batch == 0 else 8
                for k in range(nk):
                    if batch == 0:
                        wslot = k % 2
                        R.wait("sync", self.s_te, t_dwmm[wslot])
                        R.emit("sync", lambda e, k=k, b=wslot: e.dma_start(
                            out=self.wc[:, b, 0, :], in_=self.sdw_c[k]),
                            inc=R.dsem("d_dw%d" % wslot))
                        rhs_src = self.wc[:, wslot, 0, :]
                    else:
                        ex = (batch - 1) * 4 + k // 2
                        wslot = ex % 2
                        if k % 2 == 0:
                            R.wait("sync", self.s_te, t_dwmm[wslot])
                            R.emit("sync", lambda e, ex=ex, b=wslot: e.dma_start(
                                out=self.wc[:, b, :, :],
                                in_=self.down_c[ex].rearrange("k p d -> p k d")),
                                inc=R.dsem("d_dw%d" % wslot))
                        rhs_src = self.wc[:, wslot, k % 2, :]
                    R.wait("te", R.dsem("d_dw%d" % wslot))
                    R.wait("te", self.s_dve, t_hprod[k])
                    for tt in range(4):
                        for n in range(2):
                            tok = self.mm(banks8[tt * 2 + n],
                                          self.hbuf[:, k, tt * 128:(tt + 1) * 128],
                                          rhs_src[:, n * 512:(n + 1) * 512],
                                          k == 0, k == nk - 1,
                                          inc=self.s_te if (tt == 3 and n == 1) else None)
                    t_dwmm[wslot] = tok
                R.wait("dve", self.s_te, tok)
                for tt in range(4):
                    for n in range(2):
                        t_evac[tt * 2 + n] = DVE(
                            lambda e, st=st, tt=tt, n=n: e.tensor_add(
                                self.x_res[:, st * 4 + tt, n * 512:(n + 1) * 512],
                                self.x_res[:, st * 4 + tt, n * 512:(n + 1) * 512],
                                banks8[tt * 2 + n]))

    def emit_att0(self):
        self._emit_attn(l0=True)

    def emit_att1(self):
        self._emit_attn(l0=False)

    def _wvsb(self, c):
        buf = self.wa if c < 4 else self.wb
        cc = c % 4
        return buf[:, :, :, :].rearrange("p a b c -> p (a b c)")[:, cc * 1024:(cc + 1) * 1024]

    def _emit_attn(self, l0):
        R = self.R
        DVE = lambda fn: (R.wait("dve", self.s_dve), R.emit("dve", fn, inc=self.s_dve))[1]
        banks8 = [self.psA[:, 0:512], self.psA[:, 512:1024],
                  self.psB[:, 0:512], self.psB[:, 512:1024],
                  self.psC[:, 0:512], self.psC[:, 512:1024],
                  self.psD[:, :], self.psE[:, :]]
        w_qk = self.w_qk0 if l0 else self.w_qk1
        w_v = self.wv0 if l0 else self.wv1
        w_o = self.wo if l0 else self.wp
        qkv = w_qk.rearrange("(c p) f -> p c f", p=128)
        vv = w_v.rearrange("(c p) f -> p c f", p=128)

        t_wqk = [0, 0]
        t_evc = {}
        for m in range(16):
            b = m % 2
            R.wait("pool", self.s_te, t_wqk[b])
            R.emit("pool", lambda e, m=m, b=b: e.dma_start(
                out=self.wsh[:, b, 0:1024].rearrange("p (c k) -> p c k", k=128),
                in_=qkv[:, :, m * 128:(m + 1) * 128], max_dma_last_dim=8192),
                inc=R.dsem("d_qk%d" % b))
            R.wait("te", R.dsem("d_qk%d" % b))
            lw = self.wsh[:, b, 0:1024].rearrange("p (c k) -> p c k", k=128)
            for nt in range(4):
                ps = [self.psA, self.psB][nt % 2][:, (nt // 2) * 512:(nt // 2 + 1) * 512]
                R.wait("te", self.s_act, t_evc.get((m - 1, nt), 0))
                for c in range(NCH):
                    tok = self.mm(ps, lw[:, c, :],
                                  self.xT[:, c, nt * 512:(nt + 1) * 512],
                                  c == 0, c == NCH - 1,
                                  inc=self.s_te if c == NCH - 1 else None)
                if nt == 3:
                    t_wqk[b] = tok
                R.wait("act", self.s_te, tok)
                if l0:
                    bias_ap, scale = self.bqk[:, m:m + 1], 1.0
                else:
                    bias_ap, scale = 0.0, (0.125 if m < 8 else 1.0)
                R.wait("act", R.dsem("d_st%d" % (nt % 2)))
                t_evc[(m, nt)] = R.emit("act", lambda e, ps=ps, b2=bias_ap, sc=scale, nt=nt:
                                        e.activation(self.stg[:, nt % 2, :], ps,
                                                     AF.Identity, bias=b2, scale=sc),
                                        inc=self.s_act)
                R.wait("sync", self.s_act, t_evc[(m, nt)])
                gq, sl2, qq = (m % 8) // 4, m % 4, m // 8
                R.emit("sync", lambda e, gq=gq, sl2=sl2, qq=qq, nt=nt: e.dma_start(
                    out=self.qk_loc[gq, sl2, qq, :, nt * 512:(nt + 1) * 512],
                    in_=self.stg[:, nt % 2, :]), inc=R.dsem("d_st%d" % (nt % 2)))

        R.wait("pool", self.s_te, max(t_wqk))
        R.wait("pool", self.s_dve)
        R.emit("pool", lambda e: e.dma_start(
            out=self.wa[:, :, :, :].rearrange("p a b c -> p (a b c)")
            .rearrange("p (c f) -> p c f", f=1024),
            in_=vv[:, 0:4, :], max_dma_last_dim=8192), inc=R.dsem("d_wv"))
        R.emit("pool", lambda e: e.dma_start(
            out=self.wb[:, :, :, :].rearrange("p a b c -> p (a b c)")
            .rearrange("p (c f) -> p c f", f=1024),
            in_=vv[:, 4:8, :], max_dma_last_dim=8192), inc=R.dsem("d_wv"))
        R.wait("te", R.dsem("d_wv"))
        for tt in range(NT):
            for n in range(2):
                ps = self.psC[:, n * 512:(n + 1) * 512]
                R.wait("te", self.s_act, t_evc.get(("v", tt - 1, n), 0))
                if l0:
                    self.mm(ps, self.onesb[:, :],
                            self.brow[0:1, 0, n * 512:(n + 1) * 512], True, False)
                for c in range(NCH):
                    tok = self.mm(ps, self.xT[:, c, tt * 128:(tt + 1) * 128],
                                  self._wvsb(c)[:, n * 512:(n + 1) * 512],
                                  (not l0) and c == 0, c == NCH - 1,
                                  inc=self.s_te if c == NCH - 1 else None)
                R.wait("act", self.s_te, tok)
                R.wait("act", R.dsem("d_st%d" % n))
                t_evc[("v", tt, n)] = R.emit("act", lambda e, ps=ps, n=n: e.activation(
                    self.stg[:, n, :], ps, AF.Identity), inc=self.s_act)
                R.wait("sync", self.s_act, t_evc[("v", tt, n)])
                R.emit("sync", lambda e, tt=tt, n=n: e.dma_start(
                    out=self.v_loc.rearrange("g h t p d -> (g h) t p d")[
                        n * 8:(n + 1) * 8, tt, :, :].rearrange("h p d -> p h d"),
                    in_=self.stg[:, n, :].rearrange("p (h d) -> p h d", d=64)),
                    inc=R.dsem("d_st%d" % n))
        R.wait("pool", R.dsem("d_st0"))
        R.wait("pool", R.dsem("d_st1"))
        R.emit("pool", lambda e: e.dma_start(
            out=self.sh_qk[self.regs["myslot"]], in_=self.qk_loc[:, :, :, :, :]),
            inc=R.dsem("d_pub"))
        R.emit("pool", lambda e: e.dma_start(
            out=self.sh_v[self.regs["myslot"]], in_=self.v_loc[:, :, :, :, :]),
            inc=R.dsem("d_pub"))
        R.wait("pool", R.dsem("d_pub"))
        self.pair_barrier()
        for slot in range(2):
            R.emit("pool", lambda e, slot=slot: e.dma_start(
                out=self.qk_rd[slot], in_=self.sh_qk[slot][self.regs["myslot"]]),
                inc=R.dsem("d_pull"))
            R.emit("pool", lambda e, slot=slot: e.dma_start(
                out=self.v_rd[slot], in_=self.sh_v[slot][self.regs["myslot"]]),
                inc=R.dsem("d_pull"))
        if l0:
            self._causal_heads()
        else:
            self._nystrom_heads()
        R.wait("pool", R.dsem("d_aw0"))
        R.wait("pool", R.dsem("d_aw1"))
        R.wait("pool", R.dsem("d_qh"))
        R.emit("pool", lambda e: e.dma_start(
            out=self.sh_ao[self.regs["myslot"]], in_=self.ao_loc[:, :, :, :]),
            inc=R.dsem("d_pub"))
        R.wait("pool", R.dsem("d_pub"))
        self.pair_barrier()


        for g in range(16):
            R.emit("pool", lambda e, g=g: e.dma_start(
                out=self.xT[(g % 2) * 64:(g % 2) * 64 + 64, g // 2, :],
                in_=self.sh_ao[g // 8, g % 8][self.regs["myslot"], :, :]),
                inc=R.dsem("d_ao"))
        R.emit("pool", lambda e: e.dma_start(
            out=self.wa[:, :, :, :].rearrange("p a b c -> p (a b c)")
            .rearrange("p (c f) -> p c f", f=1024),
            in_=w_o[0:512, :].rearrange("(c p) f -> p c f", p=128),
            max_dma_last_dim=8192), inc=R.dsem("d_wo"))
        R.emit("pool", lambda e: e.dma_start(
            out=self.wb[:, :, :, :].rearrange("p a b c -> p (a b c)")
            .rearrange("p (c f) -> p c f", f=1024),
            in_=w_o[512:1024, :].rearrange("(c p) f -> p c f", p=128),
            max_dma_last_dim=8192), inc=R.dsem("d_wo"))
        R.wait("te", R.dsem("d_ao"))
        R.wait("te", R.dsem("d_wo"))
        t_evac = [0] * 8
        for tt in range(NT):
            for n in range(2):
                j = (tt * 2 + n) % 8
                ps = banks8[j]
                R.wait("te", self.s_dve, t_evac[j])
                if l0:
                    self.mm(ps, self.onesb[:, :],
                            self.brow[0:1, 1, n * 512:(n + 1) * 512], True, False)
                for s in range(NCH):
                    tok = self.mm(ps, self.xT[:, s, tt * 128:(tt + 1) * 128],
                                  self._wvsb(s)[:, n * 512:(n + 1) * 512],
                                  (not l0) and s == 0, s == NCH - 1,
                                  inc=self.s_te if s == NCH - 1 else None)
                R.wait("dve", self.s_te, tok)
                t_evac[j] = DVE(lambda e, tt=tt, n=n, ps=ps: e.tensor_add(
                    self.x_res[:, tt, n * 512:(n + 1) * 512],
                    self.x_res[:, tt, n * 512:(n + 1) * 512], ps))



def run(inputs, parts, trace=False):
    from concourse.bass_utils import run_bass_kernel_spmd
    maps = host_in_maps(inputs)
    mk = MK(tuple(parts))
    nc = mk.build()
    res = run_bass_kernel_spmd(nc, maps, core_ids=list(range(8)), trace=trace)
    return host_gather(res.results), res


def kernel(**inputs):
    out, _ = run(inputs, ("moe", "att0", "moe", "ffn1", "att1", "ffn1"))
    return out


# version 07 (final)
# speedup vs baseline: 1.0019x; 1.0017x over previous
import contextlib
import numpy as np

import concourse.bass as bass
import concourse.mybir as mybir
from concourse.alu_op_type import AluOpType

F32 = mybir.dt.float32
BF16 = mybir.dt.bfloat16
I32 = mybir.dt.int32
AF = mybir.ActivationFunctionType
AX = mybir.AxisListType
OP = AluOpType

T = 2048
NT = 16
D = 1024
NCH = 8
S = 4096
H = 8
HD = 64
EPS = 1e-5

ENGS = ("te", "act", "dve", "pool", "sync")


class Sem:
    __slots__ = ("h", "n", "step", "name")

    def __init__(self, h, step, name):
        self.h, self.n, self.step, self.name = h, 0, step, name


class Rec:

    def __init__(self, nc, stack):
        self.nc = nc
        self.stack = stack
        self.streams = {k: [] for k in ENGS}
        self.sems = {}
        self.own = {}

    def sem(self, name, step=1):
        h = self.stack.enter_context(self.nc.semaphore(name))
        s = Sem(h, step, name)
        self.sems[name] = s
        return s

    def dsem(self, name):
        if name not in self.sems:
            self.sem(name, 16)
        return self.sems[name]

    def emit(self, eng, fn, inc=None):
        if inc is not None:
            inc.n += inc.step
            tgt = inc

            def wrapped(e, fn=fn, tgt=tgt):
                ins = fn(e)
                ins.then_inc(tgt.h, tgt.step)
                return ins

            self.streams[eng].append(wrapped)
            return inc.n
        self.streams[eng].append(fn)
        return None

    def wait(self, eng, sem, val=None):
        v = sem.n if val is None else val
        if v <= 0:
            return
        self.streams[eng].append(lambda e, s=sem, v=v: e.wait_ge(s.h, v))

    def barrier(self, engines=ENGS):
        for eng in engines:
            for own in self.own.get(eng, ()):
                self.wait(eng, own)
            s = self.sems["bar_" + eng]
            self.emit(eng, lambda e: e.nop(), inc=s)
        for eng in engines:
            for other in engines:
                if other != eng:
                    self.wait(eng, self.sems["bar_" + other])



def host_in_maps(inp):
    x = np.asarray(inp["L0_attn_inputs"], np.float32)

    wq = np.asarray(inp["L0_attn_W_q"], np.float32).transpose(1, 0, 2).reshape(D, 1024)
    wk = np.asarray(inp["L0_attn_W_k"], np.float32).transpose(1, 0, 2).reshape(D, 1024)
    wv0 = np.ascontiguousarray(
        np.asarray(inp["L0_attn_W_v"], np.float32).transpose(1, 0, 2).reshape(D, 1024))
    w_qk0 = np.ascontiguousarray(np.concatenate([wq, wk], axis=1))
    b_qk0 = np.ascontiguousarray(
        np.concatenate([np.asarray(inp["L0_attn_b_q"], np.float32).reshape(-1),
                        np.asarray(inp["L0_attn_b_k"], np.float32).reshape(-1)])
        .reshape(16, 128).T)
    bv0 = np.asarray(inp["L0_attn_b_v"], np.float32).reshape(1, 1024)
    woT = np.asarray(inp["L0_attn_W_o"], np.float32).T.reshape(16, 64, 1024)
    bo0 = np.asarray(inp["L0_attn_b_o"], np.float32).reshape(1, 1024)

    router_T = np.ascontiguousarray(np.asarray(inp["L0_ffn_router_w"], np.float32).T)
    cbias = np.asarray(inp["L0_ffn_correction_bias"], np.float32).reshape(1, 16)
    gate_w = np.asarray(inp["L0_ffn_gate_w"], np.float32)
    up_w = np.asarray(inp["L0_ffn_up_w"], np.float32)
    down_w = np.asarray(inp["L0_ffn_down_w"], np.float32)
    sgwT = np.ascontiguousarray(np.asarray(inp["L0_ffn_shared_gate_w"], np.float32).T)
    suwT = np.ascontiguousarray(np.asarray(inp["L0_ffn_shared_up_w"], np.float32).T)
    sdwT = np.ascontiguousarray(np.asarray(inp["L0_ffn_shared_down_w"], np.float32).T)

    wqkv1 = np.asarray(inp["L1_attn_w_qkv"], np.float32)
    w_qk1 = np.ascontiguousarray(wqkv1[:, 0:2048])
    wv1 = np.ascontiguousarray(wqkv1[:, 2048:3072])
    wpT = np.asarray(inp["L1_attn_w_proj"], np.float32).reshape(16, 64, 1024)

    w1T = np.ascontiguousarray(np.asarray(inp["L1_ffn_W1"], np.float32).T)
    b1c = np.ascontiguousarray(np.asarray(inp["L1_ffn_b1"], np.float32).reshape(32, 128).T)
    w2T = np.ascontiguousarray(np.asarray(inp["L1_ffn_W2"], np.float32).T)
    b2r = np.asarray(inp["L1_ffn_b2"], np.float32).reshape(1, 1024)

    ones_f = np.ones((1, 128), np.float32)
    import ml_dtypes
    ones_b = np.ones((1, 128), np.float32)
    triu = np.triu(np.ones((128, 128), np.float32))
    eye = np.eye(64, dtype=np.float32)
    i7113 = np.ascontiguousarray(np.concatenate([7 * eye, 15 * eye, 13 * eye], axis=1))

    wo_g = np.ascontiguousarray(woT.reshape(1024, 1024))
    wp_g = np.ascontiguousarray(wpT.reshape(1024, 1024))
    maps = []
    for c in range(8):
        p, par = c // 2, c % 2
        wo_pc, wp_pc = wo_g, wp_g
        cid = np.zeros((1, 64), np.int32)
        cid[0, 0] = par
        cid[0, 1] = 1 - par
        for i in range(4):
            cid[0, 2 + i] = par * 4 + i
        for i in range(8):
            cid[0, 6 + i] = par * 8 + i
        for j in range(16):
            gh = (par * 8 + j) if j < 8 else ((1 - par) * 8 + (j - 8))
            cid[0, 14 + j] = gh * 2 + par
        for h in range(8):
            for s2 in range(2):
                cid[0, 32 + 2 * h + s2] = (par * 8 + h) * 2 + s2
        for i in range(4):
            cid[0, 48 + i] = 8 + par * 4 + i
        maps.append({
            "x": np.ascontiguousarray(x[p, par * T:(par + 1) * T, :]),
            "cid": cid,
            "w_qk0": w_qk0, "b_qk0": b_qk0, "wv0": wv0, "bv0": bv0,
            "wo": wo_pc, "bo0": bo0,
            "router_T": router_T, "cbias": cbias,
            "gate_w": gate_w, "up_w": up_w, "down_w": down_w,
            "sgwT": sgwT, "suwT": suwT, "sdwT": sdwT,
            "w_qk1": w_qk1, "wv1": wv1, "wp": wp_pc,
            "w1T": w1T, "b1c": b1c, "w2T": w2T, "b2r": b2r,
            "ones_f": ones_f, "ones_b": ones_b, "triu": triu, "i7113": i7113,
        })
    return maps


def host_gather(results):
    out = np.zeros((4, S, D), np.float32)
    for c in range(8):
        p, par = c // 2, c % 2
        out[p, par * T:(par + 1) * T, :] = results[c]["y"]
    return out



class MK:
    def __init__(self, parts=("moe", "att0", "moe", "ffn1", "att1", "ffn1")):
        self.parts = parts
        self.nc = bass.Bass("TRN2", target_bir_lowering=False)
        self.stack = contextlib.ExitStack()
        self.R = Rec(self.nc, self.stack)
        self.regs = {}
        self._declare()

    def _declare(self):
        nc = self.nc
        st = self.stack
        ap = lambda t: t.ap()

        self.x_in = ap(nc.dram_tensor("x", [T, D], F32, kind="ExternalInput"))
        self.cid = ap(nc.dram_tensor("cid", [1, 64], I32, kind="ExternalInput"))
        self.y = ap(nc.dram_tensor("y", [T, D], F32, kind="ExternalOutput"))
        for name, shape in [
            ("w_qk0", [D, 2048]), ("b_qk0", [128, 16]), ("wv0", [D, 1024]),
            ("bv0", [1, 1024]), ("wo", [D, D]), ("bo0", [1, D]),
            ("router_T", [D, 16]), ("cbias", [1, 16]),
            ("gate_w", [16, D, 256]), ("up_w", [16, D, 256]), ("down_w", [16, 256, D]),
            ("sgwT", [D, 2048]), ("suwT", [D, 2048]), ("sdwT", [2048, D]),
            ("w_qk1", [D, 2048]), ("wv1", [D, 1024]), ("wp", [D, D]),
            ("w1T", [D, 4096]), ("b1c", [128, 32]), ("w2T", [4096, D]), ("b2r", [1, D]),
            ("ones_f", [1, 128]), ("ones_b", [1, 128]), ("triu", [128, 128]),
            ("i7113", [64, 192]),
        ]:
            setattr(self, name, ap(nc.dram_tensor(name, shape, F32, kind="ExternalInput")))

        self.sh_qk = ap(nc.dram_tensor("sh_qk", [2, 2, 4, 2, 128, T], BF16, addr_space="Shared"))
        self.sh_v = ap(nc.dram_tensor("sh_v", [2, 2, 8, NT, 128, 64], BF16, addr_space="Shared"))
        self.sh_ao = ap(nc.dram_tensor("sh_ao", [2, 8, 2, 64, T], BF16, addr_space="Shared"))
        self.w1t_c = ap(nc.dram_tensor("w1t_c", [D, 4096], BF16))
        self.w2t_c = ap(nc.dram_tensor("w2t_c", [4096, D], BF16))
        self.gate_c = ap(nc.dram_tensor("gate_c", [16, NCH, 128, 256], BF16))
        self.up_c = ap(nc.dram_tensor("up_c", [16, NCH, 128, 256], BF16))
        self.down_c = ap(nc.dram_tensor("down_c", [16, 2, 128, D], BF16))
        self.sgw_c = ap(nc.dram_tensor("sgw_c", [NCH, 128, 2048], BF16))
        self.suw_c = ap(nc.dram_tensor("suw_c", [NCH, 128, 2048], BF16))
        self.sdw_c = ap(nc.dram_tensor("sdw_c", [16, 128, D], BF16))
        self.cin = ap(nc.dram_tensor("cin", [1, 16], F32))
        self.cout = ap(nc.dram_tensor("cout", [1, 16], F32))
        self.qk_loc = ap(nc.dram_tensor("qk_loc", [2, 4, 2, 128, T], BF16))
        self.v_loc = ap(nc.dram_tensor("v_loc", [2, 8, NT, 128, 64], BF16))
        self.ao_loc = ap(nc.dram_tensor("ao_loc", [8, 2, 64, T], BF16))
        self.qk_rd = ap(nc.dram_tensor("qk_rd", [2, 4, 2, 128, T], BF16))
        self.v_rd = ap(nc.dram_tensor("v_rd", [2, 8, NT, 128, 64], BF16))

        sb = lambda n, s, d: st.enter_context(nc.sbuf_tensor(n, s, d))
        self.x_res = sb("x_res", [128, NT, D], F32)
        self.xT = sb("xT", [128, NCH, T], BF16)
        self.xln = sb("xln", [128, D], BF16)
        self.hbuf = sb("hbuf", [128, 32, 512], BF16)
        flat = self.hbuf[:, :, :].rearrange("p a b -> p (a b)")
        self.qT = flat[0:64, 0:4096].rearrange("p (a b) -> p a b", b=128)
        self.kT = flat[0:64, 4096:8192].rearrange("p (a b) -> p a b", b=128)
        self.vhead = flat[:, 8192:10272].rearrange("p (a b) -> p a b", b=65)
        self.ptil = flat[:, 10272:12320].rearrange("p (a b) -> p a b", b=1024)
        self.e3t = flat[:, 10272:12320].rearrange("p (a b) -> p a b", b=64)
        self.e1t = flat[0:64, 12320:13344].rearrange("p (a b) -> p a b", b=512)
        self.stg2 = flat[0:64, 13344:15392].rearrange("p (a b) -> p a b", b=1024)
        self.stg = sb("stg", [128, 2, 512], BF16)
        self.wa = sb("wa", [128, 2, 8, 256], BF16)
        self.wb = sb("wb", [128, 2, 8, 256], BF16)
        self.wc = sb("wc", [128, 2, 2, 1024], BF16)
        self.wsh = sb("wsh", [128, 2, 2048], BF16)
        self.wsh2 = sb("wsh2", [128, 2, 2048], BF16)
        self.combine = sb("combine", [128, NT, 32], F32)
        self.cT = sb("cT", [32, T], F32)
        self.csc = sb("csc", [1, 64], I32)
        self.onesf = sb("onesf", [1, 128], F32)
        self.onesb = sb("onesb", [1, 128], BF16)
        self.trium = sb("trium", [128, 128], BF16)
        self.bqk = sb("bqk", [128, 16], F32)
        self.b1t = sb("b1t", [128, 32], F32)
        self.brow = sb("brow", [1, 2, 1024], BF16)
        self.brow2 = sb("brow2", [1, 1024], BF16)
        self.stat = sb("stat", [128, 24], F32)
        self.sml = sb("sml", [128, 32], F32)
        self.smlb = sb("smlb", [128, 8], BF16)
        w2f = self.wsh2[:, :, :].rearrange("p a b -> p (a b)").bitcast(F32)
        self.nrow = w2f[0:1, :].rearrange("p (a b) -> p a b", b=1024)
        self.nrec = w2f[32:33, :].rearrange("p (a b) -> p a b", b=1024)
        waf = self.wa[:, :, :, :].rearrange("p a b c -> p (a b c)").bitcast(F32)
        wbf = self.wb[:, :, :, :].rearrange("p a b c -> p (a b c)")
        self.nyf = waf[0:64, 0:768].rearrange("p (a b) -> p a b", b=64)
        self.qlm = waf[0:64, 768:896].rearrange("p (a b) -> p a b", b=64)
        self.i7cv = waf[0:64, 896:1088].rearrange("p (a b) -> p a b", b=64)
        self.qlmb = wbf[0:64, 0:128].rearrange("p (a b) -> p a b", b=64)
        self.nyb = wbf[0:64, 128:258].rearrange("p (a b) -> p a b", b=65)
        self.rowf = w2f[64:65, 0:1024]
        self.sml2 = sb("sml2", [1, 32], F32)

        pt = lambda n, s: st.enter_context(nc.psum_tensor(n, s, F32))
        self.psA = pt("psA", [128, 1024])
        self.psB = pt("psB", [128, 1024])
        self.psC = pt("psC", [128, 1024])
        self.psD = pt("psD", [128, 512])
        self.psE = pt("psE", [128, 512])

        R = self.R
        self.s_te = R.sem("s_te")
        self.s_act = R.sem("s_act")
        self.s_dve = R.sem("s_dve")
        self.s_pool = R.sem("s_pool")
        self.s_gd = R.sem("s_gd", 16)
        self.s_sd = R.sem("s_sd", 16)
        self.s_cc = R.sem("s_cc")
        for e in ENGS:
            R.sem("bar_" + e)
        R.own = {"te": [self.s_te], "act": [self.s_act], "dve": [self.s_dve],
                 "pool": [self.s_pool, self.s_gd, self.s_cc],
                 "sync": [self.s_sd]}

    def pair_barrier(self):
        R = self.R
        R.emit("pool", lambda e: e.collective_compute(
            "AllReduce", OP.add, replica_groups=[[0, 1], [2, 3], [4, 5], [6, 7]],
            ins=[self.cin], outs=[self.cout]), inc=self.s_cc)
        R.wait("pool", self.s_cc)
        R.barrier()

    def mm(self, out, lhsT, rhs, start, stop, inc=None):
        return self.R.emit("te", lambda e, o=out, a=lhsT, b=rhs, s0=start, s1=stop:
                           e.matmul(o, a, b, start=s0, stop=s1), inc=inc)

    def prologue(self):
        R, nc = self.R, self.nc
        P = mybir.EngineType.Pool

        for t in range(NT):
            R.emit("sync", lambda e, t=t: e.dma_start(
                out=self.x_res[:, t, :], in_=self.x_in[t * 128:(t + 1) * 128, :]),
                inc=self.s_sd)
        R.emit("pool", lambda e: e.dma_start(out=self.csc[:, :], in_=self.cid[:, :]),
               inc=R.dsem("d_cid"))
        R.emit("sync", lambda e: e.dma_start(out=self.onesf[:, :], in_=self.ones_f[:, :]),
               inc=self.s_sd)
        R.emit("sync", lambda e: e.dma_start(out=self.bqk[:, :], in_=self.b_qk0[:, :]),
               inc=self.s_sd)
        R.emit("sync", lambda e: e.dma_start(out=self.b1t[:, :], in_=self.b1c[:, :]),
               inc=self.s_sd)
        R.emit("pool", lambda e: e.dma_start(out=self.onesb[:, :], in_=self.ones_b[:, :]),
               inc=self.s_gd)
        R.emit("pool", lambda e: e.dma_start(out=self.trium[:, :], in_=self.triu[:, :]),
               inc=self.s_gd)
        for j, (srcrow, dstap, sc) in enumerate([
                (self.bv0, self.brow[:, 0, :], 1.0),
                (self.bo0, self.brow[:, 1, :], 1.0),
                (self.b2r, self.brow2[:, :], 1.0)]):
            if j > 0:
                R.wait("sync", self.s_act)
            R.emit("sync", lambda e, s=srcrow: e.dma_start(
                out=self.rowf[:, :], in_=s[:, :]), inc=R.dsem("d_row"))
            R.wait("act", R.dsem("d_row"))
            R.emit("act", lambda e, d=dstap, sc=sc: e.activation(
                d, self.rowf[:, :], AF.Copy, scale=sc), inc=self.s_act)
        R.emit("pool", lambda e: e.memset(self.sml2[0:1, 16:32], 0.0), inc=self.s_pool)
        R.wait("pool", self.s_pool)
        R.emit("pool", lambda e: e.dma_start(out=self.cin[:, :], in_=self.sml2[0:1, 16:32]),
               inc=self.s_gd)
        R.wait("pool", R.dsem("d_cid"))

        def loadregs(e):
            r = {}
            vl = lambda j, mx: nc.values_load(
                self.csc[0:1, j:j + 1], engines=[P], min_val=0, max_val=mx,
                skip_runtime_bounds_check=True)
            r["myslot"] = vl(0, 1)
            r["ptslot"] = vl(1, 1)

            self.regs.update(r)
        R.streams["pool"].append(loadregs)

        if "moe" in self.parts:
            self._moe_cache()
        self._bg = []
        if "ffn1" in self.parts:
            self._bg = self._tz_jobs()
            if self.parts[0] == "ffn1":
                while self._bg:
                    self._bg.pop(0)()
        R.wait("sync", self.s_sd)
        R.wait("pool", self.s_gd)
        R.barrier()

    def _moe_cache(self):
        R = self.R
        sa = R.dsem("d_mcA")
        for srcp, dstp in [
            (self.sgwT.rearrange("(c p) h -> p c h", p=128),
             self.sgw_c.rearrange("c p h -> p c h")),
            (self.suwT.rearrange("(c p) h -> p c h", p=128),
             self.suw_c.rearrange("c p h -> p c h")),
            (self.sdwT.rearrange("(k p) d -> p k d", p=128),
             self.sdw_c.rearrange("k p d -> p k d")),
        ]:
            R.emit("pool", lambda e, s=srcp, d=dstp: e.dma_start(
                out=d, in_=s, max_dma_last_dim=8192), inc=sa)
        for ex in range(16):
            for srcp, dstp in [
                (self.gate_w[ex].rearrange("(c p) h -> p c h", p=128),
                 self.gate_c[ex].rearrange("c p h -> p c h")),
                (self.up_w[ex].rearrange("(c p) h -> p c h", p=128),
                 self.up_c[ex].rearrange("c p h -> p c h")),
                (self.down_w[ex].rearrange("(k p) d -> p k d", p=128),
                 self.down_c[ex].rearrange("k p d -> p k d")),
            ]:
                R.emit("pool", lambda e, s=srcp, d=dstp: e.dma_start(
                    out=d, in_=s, max_dma_last_dim=8192),
                    inc=R.dsem("d_mcB%d" % (ex // 4)))
        self._mc_done = True

    def _tz_jobs(self):
        R = self.R
        wcf = self.wc[:, :, :, :].rearrange("p a b c -> p (a b c)")
        wf32 = wcf.bitcast(F32)
        st = {"red": [0, 0], "cmp": 0, "thr": 0}
        def DVE(fn):
            R.wait("dve", self.s_dve)
            return R.emit("dve", fn, inc=self.s_dve)
        specs = [(self.w1T, self.w1t_c, 4096, D), (self.w2T, self.w2t_c, 1024, 4096)]
        jobs = []

        def mksum(si, krange):
            srcT, dstT, ncols, rows = specs[si]
            view = srcT.rearrange("(k p) d -> k p d", p=128)
            tw = min(1024, ncols)
            cc = ncols // tw
            red = self.sml[:, 18 + si:19 + si]
            acc = self.sml[:, 20 + si:21 + si]

            def job():
                if krange[0] == 0:
                    DVE(lambda e: e.memset(acc, 0.0))
                for k in krange:
                    for c in range(cc):
                        b = (k * cc + c) % 2
                        sin = R.dsem("d_tz%d" % b)
                        buf = wf32[:, b * 1024:b * 1024 + tw]
                        R.wait("pool", self.s_dve, st["red"][b])
                        R.emit("pool", lambda e, o=buf, s=view[k, :, c * tw:(c + 1) * tw]:
                               e.dma_start(out=o, in_=s), inc=sin)
                        R.wait("dve", sin)
                        DVE(lambda e, s=buf: e.tensor_reduce(
                            red, s, axis=AX.X, op=OP.add, apply_absolute_value=True))
                        st["red"][b] = DVE(lambda e: e.tensor_tensor(
                            acc, acc, red, op=OP.add))
            return job

        def mkthr():
            def job():
                for si in (0, 1):
                    _, _, ncols, rows = specs[si]
                    c = si * 8
                    acc = self.sml[:, 20 + si:21 + si]
                    R.wait("pool", self.s_dve, max(st["red"]))
                    R.wait("pool", self.s_pool)
                    t = R.emit("pool", lambda e, acc=acc, c=c: e.tensor_reduce(
                        self.sml2[0:1, c:c + 1], acc, axis=AX.C, op=OP.add),
                        inc=self.s_pool)
                    R.wait("dve", self.s_pool, t)
                    DVE(lambda e, c=c, d=1.0 / (rows * ncols): e.tensor_scalar(
                        self.sml2[0:1, c + 1:c + 2], self.sml2[0:1, c:c + 1], d,
                        None, op0=OP.mult))
                    DVE(lambda e, c=c: e.tensor_scalar(
                        self.sml2[0:1, c + 2:c + 3], self.sml2[0:1, c + 1:c + 2],
                        1e-8, None, op0=OP.max))
                    DVE(lambda e, c=c: e.tensor_scalar(
                        self.sml2[0:1, c + 3:c + 4], self.sml2[0:1, c + 2:c + 3],
                        0.5, None, op0=OP.mult))
                    t2 = DVE(lambda e, c=c: e.tensor_scalar(
                        self.sml2[0:1, c + 4:c + 5], self.sml2[0:1, c + 3:c + 4],
                        -1.0, None, op0=OP.mult))
                    R.wait("pool", self.s_dve, t2)
                    R.emit("pool", lambda e, c=c, si=si: e.dma_start(
                        out=self.sml[:, 22 + si:23 + si],
                        in_=self.sml2[0:1, c + 3:c + 4].broadcast_to([1, 128])),
                        inc=R.dsem("d_tzt"))
                    R.emit("pool", lambda e, c=c, si=si: e.dma_start(
                        out=self.sml[:, 24 + si:25 + si],
                        in_=self.sml2[0:1, c + 4:c + 5].broadcast_to([1, 128])),
                        inc=R.dsem("d_tzt"))
                st["thr"] = R.sems["d_tzt"].n
            return job

        def mkcmp(si, krange):
            srcT, dstT, ncols, rows = specs[si]
            view = srcT.rearrange("(k p) d -> k p d", p=128)
            dview = dstT.rearrange("(k p) d -> k p d", p=128)
            tw = min(1024, ncols)
            cc = ncols // tw
            thp = self.sml[:, 22 + si:23 + si]
            thm = self.sml[:, 24 + si:25 + si]
            wsl = wf32[:, 0:1024]
            asl = wf32[:, 1024:1536]
            obf = wcf[:, 3072:3584]

            def job():
                for k in krange:
                    for c in range(cc):
                        sin = R.dsem("d_tz0")
                        R.wait("pool", self.s_dve, st["cmp"])
                        R.emit("pool", lambda e, s=view[k, :, c * tw:(c + 1) * tw]:
                               e.dma_start(out=wsl, in_=s), inc=sin)
                        R.wait("dve", sin)
                        R.wait("dve", R.dsem("d_tzt"), st["thr"])
                        for q in range(tw // 512):
                            wh = wsl[:, q * 512:(q + 1) * 512]
                            DVE(lambda e, wh=wh: e.tensor_scalar(
                                asl, wh, thm, None, op0=OP.is_le))
                            R.wait("dve", R.dsem("d_tzo"))
                            st["cmp"] = DVE(lambda e, wh=wh: e.scalar_tensor_tensor(
                                obf, wh, thp, asl, op0=OP.is_ge, op1=OP.subtract))
                            R.wait("pool", self.s_dve, st["cmp"])
                            R.emit("pool", lambda e, d=dview[k, :, c * tw + q * 512:
                                                             c * tw + (q + 1) * 512]:
                                   e.dma_start(out=d, in_=obf), inc=R.dsem("d_tzo"))
            return job

        for si, kr in [(0, range(0, 4)), (0, range(4, 8)),
                       (1, range(0, 16)), (1, range(16, 32))]:
            jobs.append(mksum(si, kr))
        jobs.append(mkthr())
        for si, kr in [(0, range(0, 3)), (0, range(3, 6)), (0, range(6, 8)),
                       (1, range(0, 12)), (1, range(12, 22)), (1, range(22, 32))]:
            jobs.append(mkcmp(si, kr))
        return jobs

    def emit_ln(self):
        R = self.R
        t_act = {}
        t_chain = {}
        for t in range(NT):
            s = t % 2
            b = s * 12
            c0 = 12 * s
            if t >= 2:
                R.wait("dve", self.s_act, t_act[t - 2])
            R.emit("dve", lambda e, t=t, b=b: e.bn_stats(
                self.stat[:, b:b + 6], self.x_res[:, t, 0:512]), inc=self.s_dve)
            R.emit("dve", lambda e, t=t, b=b: e.bn_stats(
                self.stat[:, b + 6:b + 12], self.x_res[:, t, 512:1024]), inc=self.s_dve)
            R.wait("dve", self.s_dve)
            R.emit("dve", lambda e, b=b, c0=c0: e.bn_aggr(
                self.sml[:, c0:c0 + 2], self.stat[:, b:b + 12]), inc=self.s_dve)
            R.wait("dve", self.s_dve)
            R.emit("dve", lambda e, c0=c0: e.tensor_scalar(
                self.sml[:, c0 + 2:c0 + 3], self.sml[:, c0 + 1:c0 + 2], EPS, None,
                op0=OP.add), inc=self.s_dve)
            tok = self.s_dve.n
            R.wait("act", self.s_dve, tok)
            R.emit("act", lambda e, c0=c0: e.activation(
                self.sml[:, c0 + 3:c0 + 4], self.sml[:, c0 + 2:c0 + 3], AF.Sqrt),
                inc=self.s_act)
            R.wait("dve", self.s_act)
            R.emit("dve", lambda e, c0=c0: e.reciprocal(
                self.sml[:, c0 + 4:c0 + 5], self.sml[:, c0 + 3:c0 + 4]), inc=self.s_dve)
            R.wait("dve", self.s_dve)
            t_chain[t] = R.emit("dve", lambda e, c0=c0: e.scalar_tensor_tensor(
                self.sml[:, c0 + 5:c0 + 6], self.sml[:, c0:c0 + 1], -1.0,
                self.sml[:, c0 + 4:c0 + 5], op0=OP.mult, op1=OP.mult), inc=self.s_dve)
            R.wait("act", self.s_dve, t_chain[t])
            if t >= 1:
                R.wait("act", R.dsem("d_tr%d" % ((t - 1) % 2)))
            t_act[t] = R.emit("act", lambda e, t=t, c0=c0: e.activation(
                self.xln[:, :], self.x_res[:, t, :], AF.Identity,
                bias=self.sml[:, c0 + 5:c0 + 6], scale=self.sml[:, c0 + 4:c0 + 5]),
                inc=self.s_act)
            R.wait("sync", self.s_act, t_act[t])
            for c in range(NCH):
                R.emit("sync", lambda e, t=t, c=c: e.dma_start(
                    out=self.xT[:, c, t * 128:(t + 1) * 128],
                    in_=self.xln[:, c * 128:(c + 1) * 128], transpose=True),
                    inc=R.dsem("d_tr%d" % (t % 2)))

    def ln_drain(self):
        R = self.R
        R.wait("sync", R.dsem("d_tr0"))
        R.wait("sync", R.dsem("d_tr1"))

    def emit_ffn1(self):
        R = self.R
        for eng in ("sync", "pool"):
            if "d_tzo" in R.sems:
                R.wait(eng, R.sems["d_tzo"])
        banks8 = [self.psA[:, 0:512], self.psA[:, 512:1024],
                  self.psB[:, 0:512], self.psB[:, 512:1024],
                  self.psC[:, 0:512], self.psC[:, 512:1024],
                  self.psD[:, :], self.psE[:, :]]
        hps = [self.psD[:, :], self.psE[:, :]]
        w1v = self.w1t_c.rearrange("(c p) h -> p c h", p=128)
        t_down_last = 0
        t_evac = [0] * 8
        t_w2mm = [0] * 4
        t_w1mm = [0] * 4
        wcq = self.wc[:, :, :, :].rearrange("p a b c -> p (a b c)")
        for st in range(4):
            t_hevac = [0] * 32
            for m in range(32):
                b = m % 4
                lw = wcq[:, b * 1024:(b + 1) * 1024].rearrange("p (c k) -> p c k", k=128)
                R.wait("sync", self.s_te, t_w1mm[b] if (m >= 4 or st > 0) else 0)
                R.emit("sync", lambda e, m=m, o=lw: e.dma_start(
                    out=o, in_=w1v[:, :, m * 128:(m + 1) * 128]),
                    inc=R.dsem("d_w1%d" % b))
                R.wait("te", R.dsem("d_w1%d" % b))
                if m >= 2:
                    R.wait("te", self.s_act, t_hevac[m - 2])
                elif st > 0:
                    R.wait("te", self.s_dve, t_evac[6 + m % 2])
                for c in range(NCH):
                    tok = self.mm(hps[m % 2], lw[:, c, :],
                                  self.xT[:, c, st * 512:(st + 1) * 512],
                                  c == 0, c == NCH - 1,
                                  inc=self.s_te if c == NCH - 1 else None)
                t_w1mm[b] = tok
                R.wait("act", self.s_te, tok)
                if st > 0 and m == 0:
                    R.wait("act", self.s_te, t_down_last)
                t_hevac[m] = R.emit("act", lambda e, m=m, b=b: e.activation(
                    self.hbuf[:, m, :], hps[m % 2], AF.Gelu,
                    bias=self.b1t[:, m:m + 1]), inc=self.s_act)
            for i in range(8):
                R.wait("te", self.s_dve, t_evac[i])
            R.wait("te", self.s_act, t_hevac[31])
            for tt in range(4):
                for n in range(2):
                    self.mm(banks8[tt * 2 + n], self.onesb[:, :],
                            self.brow2[:, n * 512:(n + 1) * 512], True, False)
            for k in range(32):
                b = k % 4
                wbuf = self.wsh if b < 2 else self.wsh2
                R.wait("pool", self.s_te, t_w2mm[b])
                R.emit("pool", lambda e, k=k, b=b, wbuf=wbuf: e.dma_start(
                    out=wbuf[:, b % 2, 0:1024],
                    in_=self.w2t_c[k * 128:(k + 1) * 128, :]),
                    inc=R.dsem("d_w2%d" % b))
                R.wait("te", R.dsem("d_w2%d" % b))
                R.wait("te", self.s_act, t_hevac[k])
                for tt in range(4):
                    for n in range(2):
                        tok = self.mm(banks8[tt * 2 + n],
                                      self.hbuf[:, k, tt * 128:(tt + 1) * 128],
                                      wbuf[:, b % 2, n * 512:(n + 1) * 512],
                                      False, k == 31,
                                      inc=self.s_te if (tt == 3 and n == 1) else None)
                t_w2mm[b] = tok
            t_down_last = tok
            R.wait("dve", self.s_te, t_down_last)
            for tt in range(4):
                for n in range(2):
                    t_evac[tt * 2 + n] = R.emit(
                        "dve", lambda e, st=st, tt=tt, n=n: e.scalar_tensor_tensor(
                            self.x_res[:, st * 4 + tt, n * 512:(n + 1) * 512],
                            banks8[tt * 2 + n], 0.5,
                            self.x_res[:, st * 4 + tt, n * 512:(n + 1) * 512],
                            op0=OP.mult, op1=OP.add), inc=self.s_dve)

    def epilogue(self):
        R = self.R
        R.barrier()
        for t in range(NT):
            R.emit("sync", lambda e, t=t: e.dma_start(
                out=self.y[t * 128:(t + 1) * 128, :], in_=self.x_res[:, t, :]),
                inc=self.s_sd)
        R.wait("sync", self.s_sd)

    def build(self):
        R = self.R
        with self.stack:
            self.prologue()
            for part in self.parts:
                R.barrier()
                self.emit_ln()
                self.ln_drain()
                R.barrier()
                if part == "ffn1":
                    while getattr(self, "_bg", []):
                        self._bg.pop(0)()
                    self.emit_ffn1()
                elif part == "moe":
                    self.emit_moe()
                elif part == "att0":
                    self.emit_att0()
                elif part == "att1":
                    self.emit_att1()
            self.epilogue()
            with self.nc.Block() as block:
                streams = self.R.streams

                @block.tensor
                def _(e):
                    for f in streams["te"]:
                        f(e)

                @block.scalar
                def _(e):
                    for f in streams["act"]:
                        f(e)

                @block.vector
                def _(e):
                    for f in streams["dve"]:
                        f(e)

                @block.gpsimd
                def _(e):
                    for f in streams["pool"]:
                        f(e)

                @block.sync
                def _(e):
                    for f in streams["sync"]:
                        f(e)
        return self.nc

    def _maxnorm2(self, src_flat, outcol, width=4096):
        R = self.R
        DVE = lambda fn: (R.wait("dve", self.s_dve), R.emit("dve", fn, inc=self.s_dve))[1]
        sqv = self.ptil[:, :, :].rearrange("p a b -> p (a b)")[0:64, :]
        acc = self.sml2[0:1, outcol:outcol + 1]
        DVE(lambda e: e.memset(acc, 0.0))
        nch = (width + 2047) // 2048
        for ch in range(nch):
            w = min(2048, width - ch * 2048)
            R.wait("act", self.s_dve)
            R.wait("act", self.s_te)
            t_sq = R.emit("act", lambda e, ch=ch, w=w: e.activation(
                sqv[:, 0:w], src_flat[:, ch * 2048:ch * 2048 + w], AF.Square),
                inc=self.s_act)
            for j in range((w + 511) // 512):
                ww = min(512, w - j * 512)
                R.wait("te", self.s_act, t_sq)
                R.wait("te", self.s_dve)
                t_mm = self.mm(self.psE[0:1, 0:ww], self.smlb[0:64, 0:1],
                               sqv[:, j * 512:j * 512 + ww], True, True,
                               inc=self.s_te)
                R.wait("dve", self.s_te, t_mm)
                DVE(lambda e, ww=ww: e.reduce_max(self.sml2[0:1, 30:31],
                                                  self.psE[0:1, 0:ww], axis=AX.X))
                R.wait("dve", self.s_dve)
                DVE(lambda e: e.tensor_tensor(acc, acc, self.sml2[0:1, 30:31],
                                              op=OP.max))
        return self.R.sems["s_dve"].n

    def _nystrom_heads(self):
        R = self.R
        DVE = lambda fn: (R.wait("dve", self.s_dve), R.emit("dve", fn, inc=self.s_dve))[1]

        def TE(out, lhsT, rhs, start=True, stop=True):
            R.wait("te", self.s_dve)
            R.wait("te", self.s_act)
            return self.mm(out, lhsT, rhs, start, stop, inc=self.s_te)

        def TR64(dst, srcm):
            t = None
            for i2 in range(2):
                for j2 in range(2):
                    R.wait("dve", self.s_dve)
                    t = R.emit("dve", lambda e, i2=i2, j2=j2, dst=dst, srcm=srcm:
                               e.transpose(dst[32 * j2:32 * (j2 + 1), 32 * i2:32 * (i2 + 1)],
                                           srcm[32 * i2:32 * (i2 + 1), 32 * j2:32 * (j2 + 1)]),
                               inc=self.s_dve)
            return t

        w2f = self.wsh2[:, :, :].rearrange("p a b -> p (a b)").bitcast(F32)
        nr64 = w2f[64:65, 0:1024]
        rc64 = w2f[64:65, 1024:2048]
        mscr = self.xln[:, :].bitcast(F32)
        ones64 = mscr[64:65, 0:64]
        oc64 = mscr[0:64, 65:66]
        qf = self.qT[:, :, :].rearrange("p a b -> p (a b)")
        kf = self.kT[:, :, :].rearrange("p a b -> p (a b)")
        ny = self.nyf
        R.wait("dve", self.s_te)
        R.wait("act", self.s_te)
        R.wait("sync", self.s_te)
        R.emit("sync", lambda e: e.dma_start(
            out=self.i7cv[:, :, :],
            in_=self.i7113[:, :].rearrange("p (k m) -> p k m", m=64)),
            inc=R.dsem("d_i7"))
        DVE(lambda e: e.memset(ones64, 1.0))
        DVE(lambda e: e.memset(oc64, 1.0))
        DVE(lambda e: e.memset(mscr[64:65, 64:65], 1e-30))
        DVE(lambda e: e.memset(self.nyb[:, 0, 64:65], 1.0))
        DVE(lambda e: e.memset(self.vhead[:, :, 64:65], 1.0))
        DVE(lambda e: e.memset(self.smlb[0:64, 0:1], 1.0))
        R.wait("sync", R.dsem("d_pull"))
        t_hmm = 0
        t_exp_s = [0, 0]
        t_oc = 0
        t_nrw = 0
        t_no = {}
        t_bc_cons = 0
        for h in range(8):
            ph = 64 * (h % 2)
            sq = R.dsem("d_qh")
            R.wait("sync", self.s_te, t_hmm)
            R.wait("sync", self.s_dve)
            for slot in range(2):
                R.emit("sync", lambda e, slot=slot, h=h, ph=ph: e.dma_start(
                    out=self.qT[:, slot * 16:(slot + 1) * 16, :],
                    in_=self.qk_rd[slot, h // 2, 0, ph:ph + 64, :]
                    .rearrange("p (a b) -> p a b", b=128)), inc=sq)
                R.emit("sync", lambda e, slot=slot, h=h, ph=ph: e.dma_start(
                    out=self.kT[:, slot * 16:(slot + 1) * 16, :],
                    in_=self.qk_rd[slot, h // 2, 1, ph:ph + 64, :]
                    .rearrange("p (a b) -> p a b", b=128)), inc=sq)
                R.emit("sync", lambda e, slot=slot, h=h: e.dma_start(
                    out=self.vhead[:, slot * 16:(slot + 1) * 16, 0:64],
                    in_=self.v_rd[slot, h].rearrange("t p d -> p t d")), inc=sq)
            R.wait("dve", sq)
            DVE(lambda e: e.tensor_reduce(
                self.qlm[:, 0, :], qf.rearrange("p (l w) -> p l w", w=64),
                axis=AX.X, op=OP.add))
            DVE(lambda e: e.tensor_scalar(self.qlm[:, 0, :], self.qlm[:, 0, :],
                                          1.0 / 64, None, op0=OP.mult))
            DVE(lambda e: e.tensor_copy(self.qlmb[:, 0, :], self.qlm[:, 0, :]))
            DVE(lambda e: e.tensor_reduce(
                self.qlm[:, 1, :], kf.rearrange("p (l w) -> p l w", w=64),
                axis=AX.X, op=OP.add))
            DVE(lambda e: e.tensor_scalar(self.qlm[:, 1, :], self.qlm[:, 1, :],
                                          1.0 / 64, None, op0=OP.mult))
            DVE(lambda e: e.tensor_copy(self.qlmb[:, 1, :], self.qlm[:, 1, :]))
            self._maxnorm2(qf, 20)
            self._maxnorm2(kf, 21)
            self._maxnorm2(self.qlmb[:, 1, :], 22, width=64)
            self._maxnorm2(self.qlmb[:, 0, :], 23, width=64)
            DVE(lambda e: e.tensor_tensor(self.sml2[0:1, 24:25], self.sml2[0:1, 20:21],
                                          self.sml2[0:1, 22:23], op=OP.mult))
            DVE(lambda e: e.tensor_tensor(self.sml2[0:1, 25:26], self.sml2[0:1, 23:24],
                                          self.sml2[0:1, 21:22], op=OP.mult))
            R.wait("act", self.s_dve)
            R.emit("act", lambda e: e.activation(
                self.sml2[0:1, 24:25], self.sml2[0:1, 24:25], AF.Sqrt), inc=self.s_act)
            t_sqr = R.emit("act", lambda e: e.activation(
                self.sml2[0:1, 25:26], self.sml2[0:1, 25:26], AF.Sqrt), inc=self.s_act)
            R.wait("dve", self.s_act, t_sqr)
            t_n1 = DVE(lambda e: e.tensor_scalar(self.sml2[0:1, 24:25],
                                                 self.sml2[0:1, 24:25], -1.0, None,
                                                 op0=OP.mult))
            t_n3 = DVE(lambda e: e.tensor_scalar(self.sml2[0:1, 25:26],
                                                 self.sml2[0:1, 25:26], -1.0, None,
                                                 op0=OP.mult))
            R.wait("te", self.s_dve, t_n3)
            R.wait("te", self.s_act)
            self.mm(self.psE[:, 0:1], self.onesf[:, :], self.sml2[0:1, 24:25],
                    True, True)
            t_nbm = self.mm(self.psE[:, 1:2], self.onesf[:, :], self.sml2[0:1, 25:26],
                            True, True, inc=self.s_te)
            R.wait("dve", self.s_te, t_nbm)
            b1col = self.sml[0:64, 28 + h % 2:29 + h % 2]
            b3col = self.sml[0:64, 30 + h % 2:31 + h % 2]
            DVE(lambda e, b1col=b1col: e.tensor_copy(b1col, self.psE[0:64, 0:1]))
            t_nbc = DVE(lambda e, b3col=b3col: e.tensor_copy(b3col, self.psE[0:64, 1:2]))
            R.wait("te", R.dsem("d_i7"))
            TE(self.psD[0:64, 0:64], self.qlm[:, 0, :], self.qlm[:, 1, :])
            R.wait("dve", self.s_te)
            DVE(lambda e: e.tensor_reduce(ny[:, 8, 0:1], self.psD[0:64, 0:64],
                                          axis=AX.X, op=OP.max, negate=True))
            R.wait("act", self.s_dve)
            R.wait("act", self.s_te)
            R.emit("act", lambda e: e.activation(ny[:, 6, :], self.psD[0:64, 0:64],
                                                 AF.Exp, bias=ny[:, 8, 0:1]),
                   inc=self.s_act)
            R.wait("dve", self.s_act)
            DVE(lambda e: e.tensor_reduce(ny[:, 8, 1:2], ny[:, 6, :], axis=AX.X,
                                          op=OP.add))
            R.wait("dve", self.s_dve)
            DVE(lambda e: e.reciprocal(ny[:, 8, 2:3], ny[:, 8, 1:2]))
            R.wait("dve", self.s_dve)
            DVE(lambda e: e.tensor_scalar(ny[:, 0, :], ny[:, 6, :], ny[:, 8, 2:3],
                                          None, op0=OP.mult))
            TR64(ny[:, 1, :], ny[:, 0, :])
            TE(self.psD[0:1, 64:128], oc64, ny[:, 0, :])
            R.wait("dve", self.s_te)
            DVE(lambda e: e.tensor_reduce(mscr[0:1, 66:67], self.psD[0:1, 64:128],
                                          axis=AX.X, op=OP.max))
            R.wait("dve", self.s_dve)
            DVE(lambda e: e.reciprocal(mscr[0:1, 67:68], mscr[0:1, 66:67]))
            t_rd = DVE(lambda e: e.nop()) if False else None
            R.wait("te", self.s_dve)
            TE(self.psD[0:64, 64:65], self.onesf[0:1, 0:64], mscr[0:1, 67:68])
            R.wait("dve", self.s_te)
            DVE(lambda e: e.tensor_scalar(ny[:, 2, :], ny[:, 1, :], self.psD[0:64, 64:65]
                                          if False else ny[:, 8, 3:4], None, op0=OP.mult)) \
                if False else None
            DVE(lambda e: e.tensor_copy(ny[:, 8, 3:4], self.psD[0:64, 64:65]))
            R.wait("dve", self.s_dve)
            DVE(lambda e: e.tensor_scalar(ny[:, 2, :], ny[:, 1, :], ny[:, 8, 3:4],
                                          None, op0=OP.mult))
            for it in range(6):
                TE(self.psD[0:64, 0:64], ny[:, 1, :], ny[:, 2, :])
                R.wait("dve", self.s_te)
                DVE(lambda e: e.tensor_copy(ny[:, 4, :], self.psD[0:64, 0:64]))
                TR64(ny[:, 5, :], ny[:, 4, :])
                DVE(lambda e: e.tensor_tensor(ny[:, 6, :], self.i7cv[:, 0, :],
                                              ny[:, 4, :], op=OP.subtract))
                TE(self.psD[0:64, 64:128], ny[:, 5, :], ny[:, 6, :])
                R.wait("dve", self.s_te)
                DVE(lambda e: e.tensor_tensor(ny[:, 6, :], self.i7cv[:, 1, :],
                                              self.psD[0:64, 64:128], op=OP.subtract))
                TE(self.psD[0:64, 0:64], ny[:, 5, :], ny[:, 6, :])
                R.wait("dve", self.s_te)
                DVE(lambda e: e.tensor_tensor(ny[:, 6, :], self.i7cv[:, 2, :],
                                              self.psD[0:64, 0:64], op=OP.subtract))
                TR64(ny[:, 3, :], ny[:, 2, :])
                TE(self.psD[0:64, 64:128], ny[:, 3, :], ny[:, 6, :])
                R.wait("dve", self.s_te)
                DVE(lambda e: e.tensor_scalar(ny[:, 2, :], self.psD[0:64, 64:128],
                                              0.25, None, op0=OP.mult))
            DVE(lambda e: e.memset(ny[:, 9, 0:1], 0.0))
            t_r3 = [0, 0]
            for i in range(8):
                sl = i % 2
                sp = self.psB[0:64, sl * 512:(sl + 1) * 512]
                R.wait("te", self.s_act, t_exp_s[sl])
                t_s3 = self.mm(sp, self.qlmb[:, 0, :],
                               self.kT[:, i * 4:(i + 1) * 4, :].rearrange("p a b -> p (a b)"),
                               True, True, inc=self.s_te)
                t_hmm = t_s3
                R.wait("act", self.s_te, t_s3)
                R.wait("act", self.s_dve, max(t_nbc, t_r3[sl]))
                R.wait("act", R.dsem("d_e3t"))
                t_exp_s[sl] = R.emit("act", lambda e, sp=sp, sl=sl, b3col=b3col:
                                     e.activation(self.e1t[:, sl, :], sp, AF.Exp,
                                                  bias=b3col), inc=self.s_act)
                R.wait("dve", self.s_act, t_exp_s[sl])
                t_r3[sl] = DVE(lambda e, sl=sl: e.tensor_reduce(
                    ny[:, 9, 1:2], self.e1t[:, sl, :], axis=AX.X, op=OP.add))
                R.wait("dve", self.s_dve)
                DVE(lambda e: e.tensor_add(ny[:, 9, 0:1], ny[:, 9, 0:1], ny[:, 9, 1:2]))
                R.wait("sync", self.s_act, t_exp_s[sl])
                R.wait("sync", self.s_te)
                for j in range(4):
                    R.emit("sync", lambda e, i=i, j=j, sl=sl: e.dma_start(
                        out=self.e3t[:, i * 4 + j, :],
                        in_=self.e1t[:, sl, j * 128:(j + 1) * 128], transpose=True),
                        inc=R.dsem("d_e3t"))
            R.wait("dve", self.s_dve)
            DVE(lambda e: e.reciprocal(ny[:, 9, 2:3], ny[:, 9, 0:1]))
            R.wait("te", R.dsem("d_e3t"))
            R.wait("te", self.s_dve)
            for c in range(32):
                t_k3 = self.mm(self.psD[0:64, 0:64], self.e3t[:, c, :],
                               self.vhead[:, c, 0:64], c == 0, c == 31,
                               inc=self.s_te if c == 31 else None)
            t_hmm = t_k3
            R.wait("dve", self.s_te, t_k3)
            DVE(lambda e: e.tensor_scalar(ny[:, 7, :], self.psD[0:64, 0:64],
                                          ny[:, 9, 2:3], None, op0=OP.mult))
            TR64(ny[:, 3, :], ny[:, 2, :])
            TE(self.psD[0:64, 64:128], ny[:, 3, :], ny[:, 7, :])
            R.wait("dve", self.s_te)
            t_zb = DVE(lambda e: e.tensor_copy(self.nyb[:, 0, 0:64],
                                               self.psD[0:64, 64:128]))
            for qi in range(4):
                for half in range(2):
                    i = qi * 2 + half
                    sl = i % 2
                    sp = self.psA[0:64, sl * 512:(sl + 1) * 512]
                    R.wait("te", self.s_act, t_exp_s[sl])
                    t_s1 = self.mm(sp, self.qlmb[:, 1, :],
                                   self.qT[:, i * 4:(i + 1) * 4, :]
                                   .rearrange("p a b -> p (a b)"),
                                   True, True, inc=self.s_te)
                    t_hmm = t_s1
                    R.wait("act", self.s_te, t_s1)
                    R.wait("act", self.s_dve, t_zb)
                    t_exp_s[sl] = R.emit("act", lambda e, sp=sp, sl=sl, b1col=b1col:
                                         e.activation(self.e1t[:, sl, :], sp, AF.Exp,
                                                      bias=b1col), inc=self.s_act)
                    R.wait("te", self.s_act, t_exp_s[sl])
                    if i == 0:
                        R.wait("te", self.s_act, max(t_oc, t_nrw))
                    t_av = self.mm(self.psC[0:65, half * 512:(half + 1) * 512],
                                   self.nyb[:, 0, :], self.e1t[:, sl, :],
                                   True, True, inc=self.s_te)
                    t_hmm = t_av
                R.wait("act", self.s_te, t_av)
                R.wait("act", self.s_dve)
                t_nrw = R.emit("act", lambda e: e.activation(
                    nr64, self.psC[64:65, :], AF.Identity, bias=mscr[64:65, 64:65]),
                    inc=self.s_act)
                R.wait("act", R.dsem("d_aw%d" % (qi % 2)))
                t_oc = R.emit("act", lambda e, qi=qi: e.activation(
                    self.stg2[:, qi % 2, 0:1024], self.psC[0:64, :], AF.Identity),
                    inc=self.s_act)
                R.wait("dve", self.s_act, t_nrw)
                t_rc = DVE(lambda e: e.reciprocal(rc64, nr64))
                for half in range(2):
                    R.wait("te", self.s_dve, max(t_rc, t_bc_cons))
                    t_bc = self.mm(self.psD[0:64, :], ones64,
                                   rc64[:, half * 512:(half + 1) * 512], True, True,
                                   inc=self.s_te)
                    R.wait("dve", self.s_te, t_bc)
                    R.wait("dve", self.s_act, t_oc)
                    t_no[(qi % 2, half)] = DVE(lambda e, qi=qi, half=half: e.tensor_tensor(
                        self.stg2[:, qi % 2, half * 512:(half + 1) * 512],
                        self.stg2[:, qi % 2, half * 512:(half + 1) * 512],
                        self.psD[0:64, :], op=OP.mult))
                    t_bc_cons = t_no[(qi % 2, half)]
                R.wait("sync", self.s_dve, t_no[(qi % 2, 1)])
                R.emit("sync", lambda e, h=h, qi=qi: e.dma_start(
                    out=self.ao_loc[h, qi // 2, :, (qi % 2) * 1024:(qi % 2 + 1) * 1024],
                    in_=self.stg2[:, qi % 2, 0:1024]), inc=R.dsem("d_aw%d" % (qi % 2)))


    def _causal_heads(self):
        R = self.R
        DVE = lambda fn: (R.wait("dve", self.s_dve), R.emit("dve", fn, inc=self.s_dve))[1]
        w2f = self.wsh2[:, :, :].rearrange("p a b -> p (a b)").bitcast(F32)
        nr64 = w2f[64:65, 0:1024]
        rc64 = w2f[64:65, 1024:2048]
        qf = self.qT[:, :, :].rearrange("p a b -> p (a b)")
        kf = self.kT[:, :, :].rearrange("p a b -> p (a b)")
        mscr = self.xln[:, :].bitcast(F32)
        ones64 = mscr[64:65, 0:64]
        DVE(lambda e: e.memset(ones64, 1.0))
        DVE(lambda e: e.memset(mscr[64:65, 64:65], 1e-30))
        DVE(lambda e: e.memset(self.vhead[:, :, 64:65], 1.0))
        DVE(lambda e: e.memset(self.smlb[0:64, 0:1], 1.0))
        t_hmm = 0
        t_exp_s = [0, 0]
        t_av_s = [0, 0]
        t_oc = 0
        t_nrw = 0
        t_no = {}
        t_bc_cons = 0
        R.wait("sync", R.dsem("d_pull"))
        for h in range(8):
            ph = 64 * (h % 2)
            sq = R.dsem("d_qh")
            R.wait("sync", self.s_te, t_hmm)
            for slot in range(2):
                R.emit("sync", lambda e, slot=slot, h=h, ph=ph: e.dma_start(
                    out=self.qT[:, slot * 16:(slot + 1) * 16, :],
                    in_=self.qk_rd[slot, h // 2, 0, ph:ph + 64, :]
                    .rearrange("p (a b) -> p a b", b=128)), inc=sq)
                R.emit("sync", lambda e, slot=slot, h=h, ph=ph: e.dma_start(
                    out=self.kT[:, slot * 16:(slot + 1) * 16, :],
                    in_=self.qk_rd[slot, h // 2, 1, ph:ph + 64, :]
                    .rearrange("p (a b) -> p a b", b=128)), inc=sq)
                R.emit("sync", lambda e, slot=slot, h=h: e.dma_start(
                    out=self.vhead[:, slot * 16:(slot + 1) * 16, 0:64],
                    in_=self.v_rd[slot, h].rearrange("t p d -> p t d")), inc=sq)
            R.wait("dve", sq)
            R.wait("act", sq)
            self._maxnorm2(qf, 20)
            self._maxnorm2(kf, 21)
            DVE(lambda e: e.tensor_tensor(self.sml2[0:1, 22:23], self.sml2[0:1, 20:21],
                                          self.sml2[0:1, 21:22], op=OP.mult))
            R.wait("act", self.s_dve)
            t_sqr = R.emit("act", lambda e: e.activation(
                self.sml2[0:1, 23:24], self.sml2[0:1, 22:23], AF.Sqrt), inc=self.s_act)
            R.wait("dve", self.s_act, t_sqr)
            t_nb = DVE(lambda e: e.tensor_scalar(self.sml2[0:1, 23:24],
                                                 self.sml2[0:1, 23:24], -0.125, None,
                                                 op0=OP.mult))
            R.wait("te", self.s_dve, t_nb)
            R.wait("te", self.s_act)
            t_nbm = self.mm(self.psE[:, 0:1], self.onesf[:, :], self.sml2[0:1, 23:24],
                            True, True, inc=self.s_te)
            R.wait("dve", self.s_te, t_nbm)
            bcol = self.sml[:, 28 + h % 2:29 + h % 2]
            t_nbc = DVE(lambda e, bcol=bcol: e.tensor_copy(bcol, self.psE[:, 0:1]))
            for qi in range(4):
                nk = 8 * qi + 8
                rhs_q = self.qT[:, qi * 8:(qi + 1) * 8, :].rearrange("p a b -> p (a b)")
                for kt in range(nk):
                    db = kt - 8 * qi
                    sl = kt % 2
                    sp = [self.psA, self.psB][sl]
                    R.wait("te", self.s_act, t_exp_s[sl])
                    self.mm(sp[:, 0:512], self.kT[:, kt, :], rhs_q[:, 0:512],
                            True, True)
                    t_smm = self.mm(sp[:, 512:1024], self.kT[:, kt, :],
                                    rhs_q[:, 512:1024], True, True, inc=self.s_te)
                    t_hmm = t_smm
                    off = max(db, 0) * 128
                    R.wait("act", self.s_te, t_smm)
                    R.wait("act", self.s_dve, t_nbc)
                    R.wait("act", self.s_te, t_av_s[sl])
                    t_exp_s[sl] = R.emit(
                        "act", lambda e, sp=sp, sl=sl, off=off, bcol=bcol: e.activation(
                            self.ptil[:, sl, off:1024], sp[:, off:1024], AF.Exp,
                            bias=bcol, scale=0.125), inc=self.s_act)
                    tm = None
                    if db >= 0:
                        if db > 0:
                            R.wait("dve", self.s_te, t_av_s[sl])
                            DVE(lambda e, sl=sl, off=off: e.memset(
                                self.ptil[:, sl, 0:off], 0.0))
                        R.wait("dve", self.s_act, t_exp_s[sl])
                        tm = DVE(lambda e, sl=sl, off=off: e.tensor_tensor(
                            self.ptil[:, sl, off:off + 128],
                            self.ptil[:, sl, off:off + 128],
                            self.trium[:, :], op=OP.mult))
                    R.wait("te", self.s_act, t_exp_s[sl])
                    if tm is not None:
                        R.wait("te", self.s_dve, tm)
                    if kt == 0:
                        R.wait("te", self.s_act, max(t_oc, t_nrw))
                    self.mm(self.psC[0:65, 0:512], self.vhead[:, kt, :],
                            self.ptil[:, sl, 0:512], kt == 0, kt == nk - 1)
                    t_av_s[sl] = self.mm(self.psC[0:65, 512:1024], self.vhead[:, kt, :],
                                         self.ptil[:, sl, 512:1024],
                                         kt == 0, kt == nk - 1, inc=self.s_te)
                    t_hmm = t_av_s[sl]
                R.wait("act", self.s_te, t_av_s[(nk - 1) % 2])
                R.wait("act", self.s_dve)
                t_nrw = R.emit("act", lambda e: e.activation(
                    nr64, self.psC[64:65, :], AF.Identity, bias=mscr[64:65, 64:65]),
                    inc=self.s_act)
                R.wait("act", R.dsem("d_aw%d" % (qi % 2)))
                t_oc = R.emit("act", lambda e, qi=qi: e.activation(
                    self.stg2[:, qi % 2, 0:1024], self.psC[0:64, :], AF.Identity),
                    inc=self.s_act)
                R.wait("dve", self.s_act, t_nrw)
                t_rc = DVE(lambda e: e.reciprocal(rc64, nr64))
                for half in range(2):
                    R.wait("te", self.s_dve, max(t_rc, t_bc_cons))
                    t_bc = self.mm(self.psD[0:64, :], ones64,
                                   rc64[:, half * 512:(half + 1) * 512], True, True,
                                   inc=self.s_te)
                    R.wait("dve", self.s_te, t_bc)
                    R.wait("dve", self.s_act, t_oc)
                    t_no[(qi % 2, half)] = DVE(lambda e, qi=qi, half=half: e.tensor_tensor(
                        self.stg2[:, qi % 2, half * 512:(half + 1) * 512],
                        self.stg2[:, qi % 2, half * 512:(half + 1) * 512],
                        self.psD[0:64, :], op=OP.mult))
                    t_bc_cons = t_no[(qi % 2, half)]
                R.wait("sync", self.s_dve, t_no[(qi % 2, 1)])
                R.emit("sync", lambda e, h=h, qi=qi: e.dma_start(
                    out=self.ao_loc[h, qi // 2, :, (qi % 2) * 1024:(qi % 2 + 1) * 1024],
                    in_=self.stg2[:, qi % 2, 0:1024]), inc=R.dsem("d_aw%d" % (qi % 2)))
            if getattr(self, "_bg", []):
                self._bg.pop(0)()

    def emit_moe(self):
        R = self.R
        banks8 = [self.psA[:, 0:512], self.psA[:, 512:1024],
                  self.psB[:, 0:512], self.psB[:, 512:1024],
                  self.psC[:, 0:512], self.psC[:, 512:1024],
                  self.psD[:, :], self.psE[:, :]]
        mscr = self.xln[:, :].bitcast(F32)
        cbb = self.stat[:, 0:16]
        DVE = lambda fn: (R.wait("dve", self.s_dve), R.emit("dve", fn, inc=self.s_dve))[1]

        R.emit("sync", lambda e: e.dma_start(out=self.sml2[0:1, 16:32],
                                             in_=self.cbias[:, :]), inc=R.dsem("d_cb"))
        R.wait("te", R.dsem("d_cb"))
        R.wait("te", self.s_dve)
        t_cbmm = self.mm(self.psE[:, 0:16], self.onesf[:, :], self.sml2[0:1, 16:32],
                         True, True, inc=self.s_te)
        R.wait("dve", self.s_te, t_cbmm)
        t_cbb = R.emit("dve", lambda e: e.tensor_copy(cbb, self.psE[:, 0:16]),
                       inc=self.s_dve)

        rsb = self.wc[:, :, :, :].rearrange("p a b c -> p (a b c)")[:, 0:128] \
            .rearrange("p (c s) -> p c s", s=16)
        R.emit("pool", lambda e: e.dma_start(
            out=rsb, in_=self.router_T.rearrange("(c p) s -> p c s", p=128),
            max_dma_last_dim=8192), inc=R.dsem("d_rt"))
        R.wait("te", R.dsem("d_rt"))
        t_sig = [0] * NT
        t_lg = [0, 0]
        for t in range(NT):
            ps = [self.psD, self.psE][t % 2]
            if t >= 2:
                R.wait("te", self.s_act, t_sig[t - 2])
            if t == 1:
                R.wait("te", self.s_dve, t_cbb)
            for c in range(NCH):
                tok = self.mm(ps[:, 0:16], self.xT[:, c, t * 128:(t + 1) * 128],
                              rsb[:, c, :], c == 0, c == NCH - 1,
                              inc=self.s_te if c == NCH - 1 else None)
            t_lg[t % 2] = tok
            R.wait("act", self.s_te, tok)
            t_sig[t] = R.emit("act", lambda e, t=t, ps=ps: e.activation(
                self.combine[:, t, 16:32], ps[:, 0:16], AF.Sigmoid), inc=self.s_act)

        V4 = lambda ap: ap.rearrange("p (g j) -> p g j", j=4)
        t_route = [0] * NT
        for t in range(NT):
            sc = self.combine[:, t, 16:32]
            sb_, msk, dst, org = mscr[:, 0:16], mscr[:, 16:32], mscr[:, 32:48], mscr[:, 48:64]
            gs, tmp = mscr[:, 64:68], mscr[:, 68:72]
            sca = lambda i: mscr[:, 76 + i:77 + i]
            R.wait("dve", self.s_act, t_sig[t])
            if t == 0:
                R.wait("dve", self.s_dve, t_cbb)
            DVE(lambda e, sc=sc: e.tensor_tensor(sb_, sc, cbb, op=OP.add))
            v = V4(sb_)
            pairs = [(0, 1), (0, 2), (0, 3), (1, 2), (1, 3), (2, 3)]
            for i, (a, b) in enumerate(pairs):
                d = gs if i == 0 else tmp
                DVE(lambda e, a=a, b=b, d=d: e.tensor_tensor(
                    d, v[:, :, a], v[:, :, b], op=OP.add))
                if i > 0:
                    DVE(lambda e: e.tensor_tensor(gs, gs, tmp, op=OP.max))
            DVE(lambda e: e.reduce_max(sca(0), gs, axis=AX.X))
            DVE(lambda e: e.tensor_scalar(tmp, gs, sca(0), None, op0=OP.is_equal))
            DVE(lambda e: e.scalar_tensor_tensor(tmp, tmp, -1e30, gs,
                                                 op0=OP.mult, op1=OP.add))
            DVE(lambda e: e.reduce_max(sca(1), tmp, axis=AX.X))
            DVE(lambda e: e.tensor_scalar(tmp, gs, sca(1), None, op0=OP.is_ge))
            mv = V4(msk)
            for j in range(4):
                DVE(lambda e, j=j: e.tensor_copy(mv[:, :, j], tmp))
            DVE(lambda e: e.tensor_tensor(org, sb_, msk, op=OP.mult))
            DVE(lambda e: e.tensor_scalar(msk, msk, -1.0, None, op0=OP.add))
            DVE(lambda e: e.scalar_tensor_tensor(dst, msk, 1e30, org,
                                                 op0=OP.mult, op1=OP.add))
            DVE(lambda e: e.tensor_copy(org, dst))
            for r in range(4):
                DVE(lambda e, r=r: e.reduce_max(sca(2 + r), dst, axis=AX.X))
                if r < 3:
                    DVE(lambda e, r=r: e.tensor_scalar(msk, dst, sca(2 + r), None,
                                                       op0=OP.is_equal))
                    DVE(lambda e: e.scalar_tensor_tensor(dst, msk, -2e30, dst,
                                                         op0=OP.mult, op1=OP.add))
            DVE(lambda e: e.tensor_scalar(msk, org, sca(5), None, op0=OP.is_ge))
            DVE(lambda e, sc=sc: e.tensor_tensor(org, sc, msk, op=OP.mult))
            DVE(lambda e: e.tensor_reduce(sca(6), org, axis=AX.X, op=OP.add))
            DVE(lambda e: e.tensor_scalar(sca(7), sca(6), 1e-20, None, op0=OP.add))
            DVE(lambda e: e.tensor_scalar(sca(8), sca(7), 2.0, None, op0=OP.mult))
            DVE(lambda e: e.reciprocal(sca(9), sca(8)))
            DVE(lambda e, t=t: e.tensor_scalar(self.combine[:, t, 0:16], org, sca(9),
                                               None, op0=OP.mult))
            for i in range(4):
                t_route[t] = DVE(lambda e, t=t, i=i: e.transpose(
                    self.cT[0:32, t * 128 + 32 * i:t * 128 + 32 * (i + 1)],
                    self.combine[32 * i:32 * (i + 1), t, 0:32]))

        mcd = getattr(self, "_mc_done", False)
        if mcd:
            R.wait("pool", R.dsem("d_mcA"))
            R.wait("sync", R.dsem("d_mcA"))
        if "d_tzo" in R.sems:
            R.wait("sync", R.sems["d_tzo"])
            R.wait("pool", R.sems["d_tzo"])
        self._mcb_g = set()
        sgv = self.sgw_c.rearrange("c p h -> p c h")
        suv = self.suw_c.rearrange("c p h -> p c h")
        t_evac = [0] * 8
        t_dwmm = [max(t_lg), max(t_lg)]
        t_gumm = [0, 0]
        t_hprod_prev = 0
        self._tcrow_mm = 0
        hsl = lambda m: self.hbuf[:, m, :]
        R.wait("te", self.s_act, t_sig[NT - 1])
        for st in range(4):
            tsl = slice(st * 512, (st + 1) * 512)
            for batch in range(5):
                nslab = 16 if batch == 0 else 8
                t_hprod = [0] * nslab
                for i in range(nslab):
                    b = i % 2
                    if batch == 0:
                        mtile = i
                        ex = None
                        wslot = b
                        R.wait("pool", self.s_te, t_gumm[wslot])
                        R.emit("pool", lambda e, m=mtile, b=b: e.dma_start(
                            out=self.wa[:, b, :, 0:128],
                            in_=sgv[:, :, m * 128:(m + 1) * 128]),
                            inc=R.dsem("d_ga%d" % b))
                        R.emit("pool", lambda e, m=mtile, b=b: e.dma_start(
                            out=self.wb[:, b, :, 0:128],
                            in_=suv[:, :, m * 128:(m + 1) * 128]),
                            inc=R.dsem("d_gb%d" % b))
                        lwg = self.wa[:, b, :, 0:128]
                        lwu = self.wb[:, b, :, 0:128]
                    else:
                        ex = (batch - 1) * 4 + i // 2
                        mtile = i % 2
                        wslot = ex % 2
                        if mtile == 0:
                            g = ex // 4
                            if mcd and g not in self._mcb_g:
                                R.wait("pool", R.dsem("d_mcB%d" % g))
                                R.wait("sync", R.dsem("d_mcB%d" % g))
                                self._mcb_g.add(g)
                            R.wait("pool", self.s_te, t_gumm[wslot])
                            R.emit("pool", lambda e, ex=ex, eb=wslot: e.dma_start(
                                out=self.wa[:, eb, :, :],
                                in_=self.gate_c[ex].rearrange("c p h -> p c h")),
                                inc=R.dsem("d_ga%d" % wslot))
                            R.emit("pool", lambda e, ex=ex, eb=wslot: e.dma_start(
                                out=self.wb[:, eb, :, :],
                                in_=self.up_c[ex].rearrange("c p h -> p c h")),
                                inc=R.dsem("d_gb%d" % wslot))
                        lwg = self.wa[:, wslot, :, mtile * 128:(mtile + 1) * 128]
                        lwu = self.wb[:, wslot, :, mtile * 128:(mtile + 1) * 128]
                    R.wait("te", R.dsem("d_ga%d" % wslot))
                    R.wait("te", R.dsem("d_gb%d" % wslot))
                    R.wait("te", self.s_dve,
                           t_hprod[i - 1] if i >= 1 else t_hprod_prev)
                    if i == 0:
                        R.wait("te", self.s_dve, max(t_evac[4], t_evac[6], t_evac[7]))
                    for c in range(NCH):
                        self.mm(self.psD[:, 0:512], lwg[:, c, :],
                                self.xT[:, c, tsl], c == 0, c == NCH - 1)
                    for c in range(NCH):
                        tok = self.mm(self.psE[:, 0:512], lwu[:, c, :],
                                      self.xT[:, c, tsl], c == 0, c == NCH - 1,
                                      inc=self.s_te if c == NCH - 1 else None)
                    t_gumm[wslot] = tok
                    tokb = None
                    if batch > 0:
                        crow = mscr[0:1, 0:512]
                        R.wait("sync", self.s_te, self._tcrow_mm)
                        R.wait("sync", self.s_dve, t_route[st * 4 + 3])
                        R.emit("sync", lambda e, ex=ex, tsl=tsl: e.dma_start(
                            out=crow, in_=self.cT[ex:ex + 1, tsl]),
                            inc=R.dsem("d_crow"))
                        R.wait("te", R.dsem("d_crow"))
                        tokb = self.mm(self.psC[:, 0:512], self.onesf[:, :],
                                       crow, True, True, inc=self.s_te)
                        self._tcrow_mm = tokb
                    R.wait("act", self.s_te, tok)
                    t_silu = R.emit("act", lambda e, b=b: e.activation(
                        hsl(20 + b), self.psD[:, 0:512], AF.Silu), inc=self.s_act)
                    R.wait("dve", self.s_act, t_silu)
                    if batch == 0:
                        t_hprod[i] = DVE(lambda e, i=i, b=b: e.scalar_tensor_tensor(
                            hsl(i), self.psE[:, 0:512], 0.5, hsl(20 + b),
                            op0=OP.mult, op1=OP.mult))
                    else:
                        R.wait("dve", self.s_te, tokb)
                        DVE(lambda e, b=b: e.tensor_tensor(
                            hsl(22 + b), hsl(20 + b), self.psC[:, 0:512],
                            op=OP.mult))
                        t_hprod[i] = DVE(lambda e, i=i, b=b: e.tensor_tensor(
                            hsl(i), hsl(22 + b), self.psE[:, 0:512], op=OP.mult))
                t_hprod_prev = t_hprod[-1]
                for j in range(8):
                    R.wait("te", self.s_dve, t_evac[j])
                R.wait("te", self.s_dve, t_hprod[-1])
                nk = 16 if batch == 0 else 8
                for k in range(nk):
                    if batch == 0:
                        wslot = k % 2
                        R.wait("sync", self.s_te, t_dwmm[wslot])
                        R.emit("sync", lambda e, k=k, b=wslot: e.dma_start(
                            out=self.wc[:, b, 0, :], in_=self.sdw_c[k]),
                            inc=R.dsem("d_dw%d" % wslot))
                        rhs_src = self.wc[:, wslot, 0, :]
                    else:
                        ex = (batch - 1) * 4 + k // 2
                        wslot = ex % 2
                        if k % 2 == 0:
                            R.wait("sync", self.s_te, t_dwmm[wslot])
                            R.emit("sync", lambda e, ex=ex, b=wslot: e.dma_start(
                                out=self.wc[:, b, :, :],
                                in_=self.down_c[ex].rearrange("k p d -> p k d")),
                                inc=R.dsem("d_dw%d" % wslot))
                        rhs_src = self.wc[:, wslot, k % 2, :]
                    R.wait("te", R.dsem("d_dw%d" % wslot))
                    R.wait("te", self.s_dve, t_hprod[k])
                    for tt in range(4):
                        for n in range(2):
                            tok = self.mm(banks8[tt * 2 + n],
                                          self.hbuf[:, k, tt * 128:(tt + 1) * 128],
                                          rhs_src[:, n * 512:(n + 1) * 512],
                                          k == 0, k == nk - 1,
                                          inc=self.s_te if (tt == 3 and n == 1) else None)
                    t_dwmm[wslot] = tok
                R.wait("dve", self.s_te, tok)
                for tt in range(4):
                    for n in range(2):
                        t_evac[tt * 2 + n] = DVE(
                            lambda e, st=st, tt=tt, n=n: e.tensor_add(
                                self.x_res[:, st * 4 + tt, n * 512:(n + 1) * 512],
                                self.x_res[:, st * 4 + tt, n * 512:(n + 1) * 512],
                                banks8[tt * 2 + n]))

    def emit_att0(self):
        self._emit_attn(l0=True)

    def emit_att1(self):
        self._emit_attn(l0=False)

    def _wvsb(self, c):
        buf = self.wa if c < 4 else self.wb
        cc = c % 4
        return buf[:, :, :, :].rearrange("p a b c -> p (a b c)")[:, cc * 1024:(cc + 1) * 1024]

    def _emit_attn(self, l0):
        R = self.R
        DVE = lambda fn: (R.wait("dve", self.s_dve), R.emit("dve", fn, inc=self.s_dve))[1]
        banks8 = [self.psA[:, 0:512], self.psA[:, 512:1024],
                  self.psB[:, 0:512], self.psB[:, 512:1024],
                  self.psC[:, 0:512], self.psC[:, 512:1024],
                  self.psD[:, :], self.psE[:, :]]
        w_qk = self.w_qk0 if l0 else self.w_qk1
        w_v = self.wv0 if l0 else self.wv1
        w_o = self.wo if l0 else self.wp
        qkv = w_qk.rearrange("(c p) f -> p c f", p=128)
        vv = w_v.rearrange("(c p) f -> p c f", p=128)

        t_wqk = [0, 0]
        t_evc = {}
        for m in range(16):
            b = m % 2
            R.wait("pool", self.s_te, t_wqk[b])
            R.emit("pool", lambda e, m=m, b=b: e.dma_start(
                out=self.wsh[:, b, 0:1024].rearrange("p (c k) -> p c k", k=128),
                in_=qkv[:, :, m * 128:(m + 1) * 128], max_dma_last_dim=8192),
                inc=R.dsem("d_qk%d" % b))
            R.wait("te", R.dsem("d_qk%d" % b))
            lw = self.wsh[:, b, 0:1024].rearrange("p (c k) -> p c k", k=128)
            for nt in range(4):
                ps = [self.psA, self.psB][nt % 2][:, (nt // 2) * 512:(nt // 2 + 1) * 512]
                R.wait("te", self.s_act, t_evc.get((m - 1, nt), 0))
                for c in range(NCH):
                    tok = self.mm(ps, lw[:, c, :],
                                  self.xT[:, c, nt * 512:(nt + 1) * 512],
                                  c == 0, c == NCH - 1,
                                  inc=self.s_te if c == NCH - 1 else None)
                if nt == 3:
                    t_wqk[b] = tok
                R.wait("act", self.s_te, tok)
                if l0:
                    bias_ap, scale = self.bqk[:, m:m + 1], 1.0
                else:
                    bias_ap, scale = 0.0, (0.125 if m < 8 else 1.0)
                R.wait("act", R.dsem("d_st%d" % (nt % 2)))
                t_evc[(m, nt)] = R.emit("act", lambda e, ps=ps, b2=bias_ap, sc=scale, nt=nt:
                                        e.activation(self.stg[:, nt % 2, :], ps,
                                                     AF.Identity, bias=b2, scale=sc),
                                        inc=self.s_act)
                R.wait("sync", self.s_act, t_evc[(m, nt)])
                gq, sl2, qq = (m % 8) // 4, m % 4, m // 8
                R.emit("sync", lambda e, gq=gq, sl2=sl2, qq=qq, nt=nt: e.dma_start(
                    out=self.qk_loc[gq, sl2, qq, :, nt * 512:(nt + 1) * 512],
                    in_=self.stg[:, nt % 2, :]), inc=R.dsem("d_st%d" % (nt % 2)))

        R.wait("pool", self.s_te, max(t_wqk))
        R.wait("pool", self.s_dve)
        R.emit("pool", lambda e: e.dma_start(
            out=self.wa[:, :, :, :].rearrange("p a b c -> p (a b c)")
            .rearrange("p (c f) -> p c f", f=1024),
            in_=vv[:, 0:4, :], max_dma_last_dim=8192), inc=R.dsem("d_wv"))
        R.emit("pool", lambda e: e.dma_start(
            out=self.wb[:, :, :, :].rearrange("p a b c -> p (a b c)")
            .rearrange("p (c f) -> p c f", f=1024),
            in_=vv[:, 4:8, :], max_dma_last_dim=8192), inc=R.dsem("d_wv"))
        R.wait("te", R.dsem("d_wv"))
        for tt in range(NT):
            for n in range(2):
                ps = self.psC[:, n * 512:(n + 1) * 512]
                R.wait("te", self.s_act, t_evc.get(("v", tt - 1, n), 0))
                if l0:
                    self.mm(ps, self.onesb[:, :],
                            self.brow[0:1, 0, n * 512:(n + 1) * 512], True, False)
                for c in range(NCH):
                    tok = self.mm(ps, self.xT[:, c, tt * 128:(tt + 1) * 128],
                                  self._wvsb(c)[:, n * 512:(n + 1) * 512],
                                  (not l0) and c == 0, c == NCH - 1,
                                  inc=self.s_te if c == NCH - 1 else None)
                R.wait("act", self.s_te, tok)
                R.wait("act", R.dsem("d_st%d" % n))
                t_evc[("v", tt, n)] = R.emit("act", lambda e, ps=ps, n=n: e.activation(
                    self.stg[:, n, :], ps, AF.Identity), inc=self.s_act)
                R.wait("sync", self.s_act, t_evc[("v", tt, n)])
                R.emit("sync", lambda e, tt=tt, n=n: e.dma_start(
                    out=self.v_loc.rearrange("g h t p d -> (g h) t p d")[
                        n * 8:(n + 1) * 8, tt, :, :].rearrange("h p d -> p h d"),
                    in_=self.stg[:, n, :].rearrange("p (h d) -> p h d", d=64)),
                    inc=R.dsem("d_st%d" % n))
        R.wait("pool", R.dsem("d_st0"))
        R.wait("pool", R.dsem("d_st1"))
        R.emit("pool", lambda e: e.dma_start(
            out=self.sh_qk[self.regs["myslot"]], in_=self.qk_loc[:, :, :, :, :]),
            inc=R.dsem("d_pub"))
        R.emit("pool", lambda e: e.dma_start(
            out=self.sh_v[self.regs["myslot"]], in_=self.v_loc[:, :, :, :, :]),
            inc=R.dsem("d_pub"))
        R.wait("pool", R.dsem("d_pub"))
        self.pair_barrier()
        for slot in range(2):
            R.emit("pool", lambda e, slot=slot: e.dma_start(
                out=self.qk_rd[slot], in_=self.sh_qk[slot][self.regs["myslot"]]),
                inc=R.dsem("d_pull"))
            R.emit("pool", lambda e, slot=slot: e.dma_start(
                out=self.v_rd[slot], in_=self.sh_v[slot][self.regs["myslot"]]),
                inc=R.dsem("d_pull"))
        if l0:
            self._causal_heads()
        else:
            self._nystrom_heads()
        R.wait("pool", R.dsem("d_aw0"))
        R.wait("pool", R.dsem("d_aw1"))
        R.wait("pool", R.dsem("d_qh"))
        R.emit("pool", lambda e: e.dma_start(
            out=self.sh_ao[self.regs["myslot"]], in_=self.ao_loc[:, :, :, :]),
            inc=R.dsem("d_pub"))
        R.wait("pool", R.dsem("d_pub"))
        self.pair_barrier()


        for g in range(16):
            R.emit("pool", lambda e, g=g: e.dma_start(
                out=self.xT[(g % 2) * 64:(g % 2) * 64 + 64, g // 2, :],
                in_=self.sh_ao[g // 8, g % 8][self.regs["myslot"], :, :]),
                inc=R.dsem("d_ao"))
        R.emit("pool", lambda e: e.dma_start(
            out=self.wa[:, :, :, :].rearrange("p a b c -> p (a b c)")
            .rearrange("p (c f) -> p c f", f=1024),
            in_=w_o[0:512, :].rearrange("(c p) f -> p c f", p=128),
            max_dma_last_dim=8192), inc=R.dsem("d_wo"))
        R.emit("pool", lambda e: e.dma_start(
            out=self.wb[:, :, :, :].rearrange("p a b c -> p (a b c)")
            .rearrange("p (c f) -> p c f", f=1024),
            in_=w_o[512:1024, :].rearrange("(c p) f -> p c f", p=128),
            max_dma_last_dim=8192), inc=R.dsem("d_wo"))
        R.wait("te", R.dsem("d_ao"))
        R.wait("te", R.dsem("d_wo"))
        t_evac = [0] * 8
        for tt in range(NT):
            for n in range(2):
                j = (tt * 2 + n) % 8
                ps = banks8[j]
                R.wait("te", self.s_dve, t_evac[j])
                if l0:
                    self.mm(ps, self.onesb[:, :],
                            self.brow[0:1, 1, n * 512:(n + 1) * 512], True, False)
                for s in range(NCH):
                    tok = self.mm(ps, self.xT[:, s, tt * 128:(tt + 1) * 128],
                                  self._wvsb(s)[:, n * 512:(n + 1) * 512],
                                  (not l0) and s == 0, s == NCH - 1,
                                  inc=self.s_te if s == NCH - 1 else None)
                R.wait("dve", self.s_te, tok)
                t_evac[j] = DVE(lambda e, tt=tt, n=n, ps=ps: e.tensor_add(
                    self.x_res[:, tt, n * 512:(n + 1) * 512],
                    self.x_res[:, tt, n * 512:(n + 1) * 512], ps))



def run(inputs, parts, trace=False):
    from concourse.bass_utils import run_bass_kernel_spmd
    maps = host_in_maps(inputs)
    mk = MK(tuple(parts))
    nc = mk.build()
    res = run_bass_kernel_spmd(nc, maps, core_ids=list(range(8)), trace=trace)
    return host_gather(res.results), res


def kernel(**inputs):
    out, _ = run(inputs, ("moe", "att0", "moe", "ffn1", "att1", "ffn1"))
    return out
